# version 09
# baseline (speedup 1.0000x reference)
import math
import numpy as np

import concourse.bass as bass
import concourse.mybir as mybir
from concourse.bass_utils import run_bass_kernel_spmd

F32 = mybir.dt.float32
BF16 = mybir.dt.bfloat16
AF = mybir.ActivationFunctionType
ALU = mybir.AluOpType
AX = mybir.AxisListType.X

D = 4096
C = D // 128
NCORE = 8
H = 32
DH = 128
NEXP = 8
R = 16
EPS = 1e-5
QSCALE = 1.0 / math.sqrt(DH)
T_WIN = 70.0

SLOPES = [2.0 ** (-(h + 1) / 4.0) for h in range(H)]
SLOT_W = [T_WIN / SLOPES[8 * j + 7] for j in range(4)]


def kt_lo(slot, qt):
    return max(0, int(math.floor((512.0 * qt - SLOT_W[slot]) / 128.0)))


class Sem:
    def __init__(self, nc, name):
        self._cm = nc.semaphore(name)
        self.h = self._cm.__enter__()
        self.v = 0


def build_program(S=8192):
    TC = S // NCORE
    NTH = TC // 512
    NQT = S // 512
    NT128 = TC // 128
    assert TC % 512 == 0

    nc = bass.Bass("TRN2", target_bir_lowering=False, debug=False)

    x_f = nc.dram_tensor("x_f", [D, TC], F32, kind="ExternalInput").ap()
    basewT = nc.dram_tensor("basewT", [D, D], F32, kind="ExternalInput").ap()
    wkv_in = nc.dram_tensor("wkv", [D, 256], F32, kind="ExternalInput").ap()
    wq_in = nc.dram_tensor("wq", [D, 512], F32, kind="ExternalInput").ap()
    wc_p = nc.dram_tensor("wc_p", [D, D], F32, kind="ExternalInput").ap()
    gate_wT = nc.dram_tensor("gate_wT", [D, NEXP], F32, kind="ExternalInput").ap()
    lora_AT = nc.dram_tensor("lora_AT", [D, NEXP * R], F32, kind="ExternalInput").ap()
    lora_BT = nc.dram_tensor("lora_BT", [NEXP * R, D], F32, kind="ExternalInput").ap()
    bias_fm = nc.dram_tensor("bias_fm", [128, C], F32, kind="ExternalInput").ap()
    slope4_in = nc.dram_tensor("slope4", [128, 4], F32, kind="ExternalInput").ap()
    e2_in = nc.dram_tensor("e2_in", [128, 128], F32, kind="ExternalInput").ap()
    NCONST = 128 + 128 + 512 + 72 + 72
    consts_in = nc.dram_tensor("consts", [128, NCONST], F32, kind="ExternalInput").ap()
    dm_in = nc.dram_tensor("dm_in", [128, 4 * 512], F32, kind="ExternalInput").ap()
    dm2_in = nc.dram_tensor("dm2_in", [128, 4 * 512], F32, kind="ExternalInput").ap()
    O_ID, O_ONE, O_D0, O_DB, O_CB = 0, 128, 256, 768, 840

    out_f = nc.dram_tensor("out_f", [D, TC], F32, kind="ExternalOutput").ap()

    xs1 = nc.dram_tensor("xs1", [D, TC], F32).ap()
    xs2 = nc.dram_tensor("xs2", [D, TC], F32).ap()
    qdram = nc.dram_tensor("qdram", [512, S], BF16).ap()
    ag_x_in = nc.dram_tensor("ag_x_in", [D, TC], BF16).ap()
    ag_xo = [nc.dram_tensor(f"ag_xo{q}", [NCORE * 1024, TC], BF16, addr_space="Shared").ap()
             for q in range(4)]
    ag_k_in = nc.dram_tensor("ag_k_in", [128, TC], BF16).ap()
    ag_k_out = nc.dram_tensor("ag_k_out", [NCORE * 128, TC], BF16, addr_space="Shared").ap()
    agv_in = nc.dram_tensor("agv_in", [TC, 128], BF16).ap()
    agv_out = nc.dram_tensor("agv_out", [S, 128], BF16, addr_space="Shared").ap()
    ag_o_in = nc.dram_tensor("ag_o_in", [512, S], BF16).ap()
    ag_oo = [nc.dram_tensor(f"ag_oo{h}", [NCORE * 128, S], BF16, addr_space="Shared").ap()
             for h in range(4)]

    _stacks = {"sbuf": [], "psum": []}
    _cm_of = {}

    def AL(cm):
        h = cm.__enter__()
        space = "psum" if "PSum" in type(h).__name__ else "sbuf"
        _stacks[space].append(h)
        _cm_of[id(h)] = cm
        return h

    def FR(*hs):
        for h in hs:
            space = "psum" if "PSum" in type(h).__name__ else "sbuf"
            top = _stacks[space].pop()
            assert top is h, f"non-LIFO free: {getattr(h, 'name', h)} vs {getattr(top, 'name', top)}"
            _cm_of.pop(id(h)).__exit__(None, None, None)

    consts_sb = AL(nc.sbuf_tensor("consts_sb", [128, NCONST], F32))
    e2_sb = AL(nc.sbuf_tensor("e2_sb", [128, 128], F32))
    bias_sb = AL(nc.sbuf_tensor("bias_sb", [128, C], F32))
    bias05 = AL(nc.sbuf_tensor("bias05", [128, C], F32))
    slope4 = AL(nc.sbuf_tensor("slope4_sb", [128, 4], F32))
    onesb = AL(nc.sbuf_tensor("onesb", [128, 1], BF16))
    ones128b = AL(nc.sbuf_tensor("ones128b", [128, 128], BF16))
    c5_sb = AL(nc.sbuf_tensor("c5_sb", [128, 1], F32))
    gate_sb = AL(nc.sbuf_tensor("gate_sb", [128, C * NEXP], BF16))
    lat_sb = AL(nc.sbuf_tensor("lat_sb", [128, C * 128], BF16))
    lbt_sb = AL(nc.sbuf_tensor("lbt_sb", [128, D], BF16))
    xcb = [AL(nc.sbuf_tensor(f"xcb{i}", [128, TC], F32)) for i in range(4)]
    bwb = [AL(nc.sbuf_tensor(f"bwb{i}", [128, C * 128], BF16)) for i in range(2)]
    t1b = [AL(nc.sbuf_tensor(f"t1b{i}", [128, 512], F32)) for i in range(4)]
    x1b = [AL(nc.sbuf_tensor(f"x1b{i}", [128, 512], F32)) for i in range(4)]
    xbs = [AL(nc.sbuf_tensor(f"xbs{i}", [128, 512], BF16)) for i in range(4)]
    xqs = [AL(nc.sbuf_tensor(f"xqs{i}", [128, 512], BF16)) for i in range(4)]
    mean_r = AL(nc.sbuf_tensor("mean_r", [1, TC], F32))
    m2_r = AL(nc.sbuf_tensor("m2_r", [1, TC], F32))
    var_r = AL(nc.sbuf_tensor("var_r", [1, TC], F32))
    rstd_r = AL(nc.sbuf_tensor("rstd_r", [1, TC], F32))
    lg_sb = [AL(nc.sbuf_tensor(f"lg{i}", [128, NEXP], F32)) for i in range(2)]
    m1_sb = AL(nc.sbuf_tensor("m1_sb", [128, 1], F32))
    m2s_sb = AL(nc.sbuf_tensor("m2s_sb", [128, 1], F32))
    eq1_sb = AL(nc.sbuf_tensor("eq1_sb", [128, NEXP], F32))
    eq2_sb = AL(nc.sbuf_tensor("eq2_sb", [128, NEXP], F32))
    msk_sb = AL(nc.sbuf_tensor("msk_sb", [128, NEXP], F32))
    n1_sb = AL(nc.sbuf_tensor("n1_sb", [128, 1], F32))
    n2_sb = AL(nc.sbuf_tensor("n2_sb", [128, 1], F32))
    d21_sb = AL(nc.sbuf_tensor("d21_sb", [128, 1], F32))
    w2_sb = AL(nc.sbuf_tensor("w2_sb", [128, 1], F32))
    w1_sb = AL(nc.sbuf_tensor("w1_sb", [128, 1], F32))
    tie_sb = AL(nc.sbuf_tensor("tie_sb", [128, 1], F32))
    r1_sb = AL(nc.sbuf_tensor("r1_sb", [128, 1], F32))
    r2_sb = AL(nc.sbuf_tensor("r2_sb", [128, 1], F32))
    wfa_sb = AL(nc.sbuf_tensor("wfa_sb", [128, NEXP], F32))
    wf_all = AL(nc.sbuf_tensor("wf_all", [128, NT128 * NEXP], F32))
    wfT_sb = AL(nc.sbuf_tensor("wfT_sb", [8, TC], F32))
    wrep_sb = AL(nc.sbuf_tensor("wrep_sb", [128, TC], F32))
    mid_sb = AL(nc.sbuf_tensor("mid_sb", [128, TC], BF16))

    sm = {n: Sem(nc, n) for n in [
        "dmisc", "dmiscg", "dx0", "dx1", "dx2", "dx3", "dw0", "dw1",
        "dst0", "dst1", "dst2", "dst3", "dag", "dao0", "dao1",
        "dxg", "dwq", "dko", "dqt0", "dqt1", "dq0", "dq1", "cc", "pPE", "pPEv", "cV", "cA", "cPL"]}

    blk_cm = nc.Block()
    block = blk_cm.__enter__()

    bar = Sem(nc, "bar")

    def BAR():
        bar.v += 5
        target = bar.v
        for sec in (block.sync, block.gpsimd, block.tensor, block.vector, block.scalar):
            def f(eng, target=target):
                eng.drain().then_inc(bar.h, 1)
                eng.wait_ge(bar.h, target)
            sec(f)

    def ph0_sync(s):
        s.dma_start(out=consts_sb[:], in_=consts_in).then_inc(sm["dmisc"].h, 16)
        s.dma_start(out=e2_sb[:], in_=e2_in).then_inc(sm["dmisc"].h, 16)
        s.dma_start(out=bias_sb[:], in_=bias_fm).then_inc(sm["dmisc"].h, 16)
        s.dma_start(out=slope4[:], in_=slope4_in).then_inc(sm["dmisc"].h, 16)
    sm["dmisc"].v += 64

    def ph0_gps(g):
        g.dma_start(out=gate_sb[:].rearrange("p (c e) -> p c e", c=C),
                    in_=gate_wT.rearrange("(c p) e -> p c e", p=128)).then_inc(sm["dmiscg"].h, 16)
        g.dma_start(out=lat_sb[:].rearrange("p (c r) -> p c r", c=C),
                    in_=lora_AT.rearrange("(c p) r -> p c r", p=128)).then_inc(sm["dmiscg"].h, 16)
        g.dma_start(out=lbt_sb[:], in_=lora_BT).then_inc(sm["dmiscg"].h, 16)
        g.memset(onesb[:], 1.0)
        g.memset(ones128b[:], 1.0)
        g.memset(c5_sb[:], -5.0)
    sm["dmiscg"].v += 48

    def ph0_vec(v):
        v.wait_ge(sm["dmisc"].h, 64)
        v.tensor_scalar_mul(bias05[:], bias_sb[:], 0.5)

    block.sync(ph0_sync)
    block.gpsimd(ph0_gps)
    block.vector(ph0_vec)
    BAR()

    def ln_stream_stats(src_ap, ps_s1, ps_s2, dst_wait=None):
        dxb = [sm[f"dx{j}"].v for j in range(4)]
        cPLb, cAb, pPEb = sm["cPL"].v, sm["cA"].v, sm["pPE"].v

        def bi_of(c, th):
            return (c % 2) * 2 + th if NTH > 1 else (c % 2)

        def f_sync(s):
            if dst_wait is not None:
                for j in range(4):
                    s.wait_ge(sm[f"dst{j}"].h, dst_wait[j])
            for c in range(C):
                sl = c % 4
                if c >= 4:
                    s.wait_ge(sm["cPL"].h, cPLb + (c - 4) + 1)
                    s.wait_ge(sm["cA"].h, cAb + (c - 4) + 1)
                s.dma_start(out=xcb[sl][:], in_=src_ap[c * 128:(c + 1) * 128, :]).then_inc(sm[f"dx{sl}"].h, 16)

        def f_pool(g):
            for c in range(C):
                sl = c % 4
                g.wait_ge(sm[f"dx{sl}"].h, dxb[sl] + (c // 4 + 1) * 16)
                if c >= 2:
                    g.wait_ge(sm["pPE"].h, pPEb + (c - 2) + 1)
                for th in range(NTH):
                    ins = g.tensor_copy(xbs[bi_of(c, th)][:], xcb[sl][:, th * 512:(th + 1) * 512])
                    if th == NTH - 1:
                        ins.then_inc(sm["cPL"].h, 1)

        def f_scal(sc):
            for c in range(C):
                sl = c % 4
                sc.wait_ge(sm[f"dx{sl}"].h, dxb[sl] + (c // 4 + 1) * 16)
                if c >= 2:
                    sc.wait_ge(sm["pPE"].h, pPEb + (c - 2) + 1)
                for th in range(NTH):
                    ins = sc.activation(xqs[bi_of(c, th)][:], xcb[sl][:, th * 512:(th + 1) * 512], AF.Square)
                    if th == NTH - 1:
                        ins.then_inc(sm["cA"].h, 1)

        def f_pe(t):
            for c in range(C):
                t.wait_ge(sm["cPL"].h, cPLb + c + 1)
                t.wait_ge(sm["cA"].h, cAb + c + 1)
                for th in range(NTH):
                    bi = bi_of(c, th)
                    t.matmul(ps_s1[0:1, th * 512:(th + 1) * 512], onesb[:], xbs[bi][:],
                             start=(c == 0), stop=(c == C - 1))
                    ins = t.matmul(ps_s2[0:1, th * 512:(th + 1) * 512], onesb[:], xqs[bi][:],
                                   start=(c == 0), stop=(c == C - 1))
                ins.then_inc(sm["pPE"].h, 1)

        block.sync(f_sync)
        block.gpsimd(f_pool)
        block.scalar(f_scal)
        block.tensor(f_pe)
        for j in range(4):
            sm[f"dx{j}"].v += 16 * ((C - j + 3) // 4)
        sm["cPL"].v += C
        sm["cA"].v += C
        sm["pPE"].v += C

    def ln_rowmath_bcast(ps_s1, ps_s2, ps_rb, ps_mb):
        pPEb, cVb, cAb = sm["pPE"].v, sm["cV"].v, sm["cA"].v

        def f_vec(v):
            v.wait_ge(sm["pPE"].h, pPEb)
            v.tensor_scalar_mul(mean_r[:], ps_s1[:], 1.0 / D).then_inc(sm["cV"].h, 1)
            v.wait_ge(sm["cA"].h, cAb + 1)
            v.scalar_tensor_tensor(var_r[:], ps_s2[:], 1.0 / D, m2_r[:], ALU.mult, ALU.subtract).then_inc(sm["cV"].h, 1)
            v.wait_ge(sm["cV"].h, cVb + 2)
            v.tensor_scalar_add(var_r[:], var_r[:], EPS).then_inc(sm["cV"].h, 1)
            v.wait_ge(sm["cA"].h, cAb + 3)
            v.tensor_mul(mean_r[:], mean_r[:], rstd_r[:]).then_inc(sm["cV"].h, 1)

        def f_scal(sc):
            sc.wait_ge(sm["pPE"].h, pPEb)
            sc.activation(m2_r[:], ps_s1[:], AF.Square, scale=1.0 / D).then_inc(sm["cA"].h, 1)
            sc.wait_ge(sm["cV"].h, cVb + 3)
            sc.activation(m2_r[:], var_r[:], AF.Ln).then_inc(sm["cA"].h, 1)
            sc.wait_ge(sm["cA"].h, cAb + 2)
            sc.activation(rstd_r[:], m2_r[:], AF.Exp, scale=-0.5).then_inc(sm["cA"].h, 1)

        def f_pe(t):
            t.wait_ge(sm["cV"].h, cVb + 4)
            for th in range(NTH):
                t.matmul(ps_rb[:, th * 512:(th + 1) * 512], consts_sb[0:1, O_ONE:O_ONE + 128],
                         rstd_r[0:1, th * 512:(th + 1) * 512], start=True, stop=True)
                ins = t.matmul(ps_mb[:, th * 512:(th + 1) * 512], consts_sb[0:1, O_ONE:O_ONE + 128],
                               mean_r[0:1, th * 512:(th + 1) * 512], start=True, stop=True)
            ins.then_inc(sm["pPE"].h, 1)

        block.vector(f_vec)
        block.scalar(f_scal)
        block.tensor(f_pe)
        sm["cV"].v += 4
        sm["cA"].v += 3
        sm["pPE"].v += 1

    def ln_apply(src_ap, xnT, ps_rb, ps_mb, also_ag=None, dst_wait=None):
        dxb = [sm[f"dx{j}"].v for j in range(4)]
        cVb, pPEb = sm["cV"].v, sm["pPE"].v

        def f_sync(s):
            if dst_wait is not None:
                for j in range(4):
                    s.wait_ge(sm[f"dst{j}"].h, dst_wait[j])
            for c in range(C):
                sl = c % 4
                if c >= 4:
                    s.wait_ge(sm["cV"].h, cVb + 2 * (c - 4) + 2)
                s.dma_start(out=xcb[sl][:], in_=src_ap[c * 128:(c + 1) * 128, :]).then_inc(sm[f"dx{sl}"].h, 16)

        def f_vec(v):
            v.wait_ge(sm["pPE"].h, pPEb)
            for c in range(C):
                sl = c % 4
                v.wait_ge(sm[f"dx{sl}"].h, dxb[sl] + (c // 4 + 1) * 16)
                v.tensor_mul(xcb[sl][:], xcb[sl][:], ps_rb[:]).then_inc(sm["cV"].h, 1)
                v.wait_ge(sm["cV"].h, cVb + 2 * c + 1)
                v.tensor_sub(xnT[:, c * TC:(c + 1) * TC], xcb[sl][:], ps_mb[:]).then_inc(sm["cV"].h, 1)

        block.sync(f_sync)
        block.vector(f_vec)
        for j in range(4):
            sm[f"dx{j}"].v += 16 * ((C - j + 3) // 4)
        sm["cV"].v += 2 * C

        if also_ag is not None:
            dagb = sm["dag"].v

            def f_gps(g):
                for c in range(C):
                    g.wait_ge(sm["cV"].h, cVb + 2 * c + 2)
                    g.dma_start(out=also_ag[c * 128:(c + 1) * 128, :],
                                in_=xnT[:, c * TC:(c + 1) * TC]).then_inc(sm["dag"].h, 16)
                    if c % 8 == 7:
                        q = c // 8
                        g.wait_ge(sm["dag"].h, dagb + 16 * (c + 1))
                        g.collective_compute(
                            "AllGather", ALU.bypass, replica_groups=[list(range(NCORE))],
                            ins=[also_ag[q * 1024:(q + 1) * 1024, :]],
                            outs=[ag_xo[q]]).then_inc(sm["cc"].h, 1)
            block.gpsimd(f_gps)
            sm["dag"].v += 16 * C
            sm["cc"].v += 4

    _ffn_n = [0]

    def ffn(xnT, resid_ap, out_ap, ps_s1, ps_s2, dst_wait=None):
        tag = _ffn_n[0]
        _ffn_n[0] += 1
        do_stats = ps_s1 is not None
        ps_g = AL(nc.psum_tensor(f"ps_g{tag}", [128, NEXP], F32))
        ps_wfT = AL(nc.psum_tensor(f"ps_wfT{tag}", [8, 128], F32))
        ps_wrep = AL(nc.psum_tensor(f"ps_wrep{tag}", [128, 512], F32))
        ps_low = AL(nc.psum_tensor(f"ps_low{tag}", [128, 512], F32))
        pPEb, cVb, cAb = sm["pPE"].v, sm["cV"].v, sm["cA"].v
        NSTEP = 19

        def f_pe2(t):
            t.wait_ge(sm["dmiscg"].h, 48)
            t.wait_ge(sm["dmisc"].h, 64)
            for tt in range(NT128):
                if tt > 0:
                    t.wait_ge(sm["cV"].h, cVb + NSTEP * (tt - 1) + 1)
                for c in range(C):
                    ins = t.matmul(ps_g[:], xnT[:, c * TC + tt * 128: c * TC + (tt + 1) * 128],
                                   gate_sb[:, c * NEXP:(c + 1) * NEXP],
                                   start=(c == 0), stop=(c == C - 1))
                ins.then_inc(sm["pPE"].h, 1)
            for tt in range(NT128):
                t.wait_ge(sm["cV"].h, cVb + NSTEP * tt + NSTEP)
                if tt > 0:
                    t.wait_ge(sm["cV"].h, cVb + NSTEP * NT128 + tt)
                t.transpose(ps_wfT[:], wf_all[:, tt * NEXP:(tt + 1) * NEXP],
                            consts_sb[:, O_ID:O_ID + 128]).then_inc(sm["pPE"].h, 1)
            for th in range(NTH):
                t.wait_ge(sm["cV"].h, cVb + NSTEP * NT128 + NT128)
                if th > 0:
                    t.wait_ge(sm["cV"].h, cVb + NSTEP * NT128 + NT128 + th)
                t.matmul(ps_wrep[:], e2_sb[0:8, :], wfT_sb[0:8, th * 512:(th + 1) * 512],
                         start=True, stop=True).then_inc(sm["pPE"].h, 1)
            for th in range(NTH):
                t.wait_ge(sm["cV"].h, cVb + NSTEP * NT128 + NT128 + NTH)
                if th > 0:
                    t.wait_ge(sm["cV"].h, cVb + NSTEP * NT128 + NT128 + NTH + th)
                for c in range(C):
                    ins = t.matmul(ps_low[:], lat_sb[:, c * 128:(c + 1) * 128],
                                   xnT[:, c * TC + th * 512: c * TC + th * 512 + 512],
                                   start=(c == 0), stop=(c == C - 1))
                ins.then_inc(sm["pPE"].h, 1)

        def f_vec2(v):
            n = cVb
            for tt in range(NT128):
                sl = tt % 2
                v.wait_ge(sm["pPE"].h, pPEb + tt + 1)

                def step(ins):
                    nonlocal n
                    n += 1
                    ins.then_inc(sm["cV"].h, 1)
                    v.wait_ge(sm["cV"].h, n)
                step(v.tensor_copy(lg_sb[sl][:], ps_g[:]))
                step(v.reduce_max(m1_sb[:], lg_sb[sl][:], axis=AX))
                step(v.tensor_scalar(eq1_sb[:], lg_sb[sl][:], m1_sb[:, 0:1], None, ALU.is_ge))
                step(v.scalar_tensor_tensor(msk_sb[:], eq1_sb[:], -1e30, lg_sb[sl][:], ALU.mult, ALU.add))
                step(v.reduce_sum(n1_sb[:], eq1_sb[:], axis=AX))
                step(v.reduce_max(m2s_sb[:], msk_sb[:], axis=AX))
                step(v.tensor_scalar(eq2_sb[:], lg_sb[sl][:], m2s_sb[:, 0:1], None, ALU.is_equal))
                step(v.tensor_sub(d21_sb[:], m2s_sb[:], m1_sb[:]))
                step(v.reduce_sum(n2_sb[:], eq2_sb[:], axis=AX))
                step(v.tensor_scalar(tie_sb[:], n1_sb[:], 1.5, None, ALU.is_ge))
                step(v.reciprocal(r1_sb[:], n1_sb[:]))
                step(v.reciprocal(r2_sb[:], n2_sb[:]))
                v.wait_ge(sm["cA"].h, cAb + tt + 1)
                step(v.tensor_scalar(w1_sb[:], tie_sb[:], -1.0, 1.0, ALU.mult, ALU.add))
                step(v.tensor_mul(w2_sb[:], w2_sb[:], w1_sb[:]))
                step(v.tensor_scalar(w1_sb[:], w2_sb[:], -1.0, 1.0, ALU.mult, ALU.add))
                step(v.tensor_mul(w1_sb[:], w1_sb[:], r1_sb[:]))
                step(v.tensor_mul(w2_sb[:], w2_sb[:], r2_sb[:]))
                step(v.tensor_scalar(wfa_sb[:], eq1_sb[:], w1_sb[:, 0:1], None, ALU.mult))
                step(v.scalar_tensor_tensor(wf_all[:, tt * NEXP:(tt + 1) * NEXP], eq2_sb[:],
                                            w2_sb[:, 0:1], wfa_sb[:], ALU.mult, ALU.add))
            for tt in range(NT128):
                v.wait_ge(sm["pPE"].h, pPEb + NT128 + tt + 1)
                n += 1
                v.tensor_copy(wfT_sb[0:8, tt * 128:(tt + 1) * 128], ps_wfT[:]).then_inc(sm["cV"].h, 1)
                v.wait_ge(sm["cV"].h, n)
            for th in range(NTH):
                v.wait_ge(sm["pPE"].h, pPEb + 2 * NT128 + th + 1)
                n += 1
                v.tensor_copy(wrep_sb[:, th * 512:(th + 1) * 512], ps_wrep[:]).then_inc(sm["cV"].h, 1)
                v.wait_ge(sm["cV"].h, n)
            for th in range(NTH):
                v.wait_ge(sm["pPE"].h, pPEb + 2 * NT128 + NTH + th + 1)
                n += 1
                v.tensor_mul(mid_sb[:, th * 512:(th + 1) * 512], ps_low[:],
                             wrep_sb[:, th * 512:(th + 1) * 512]).then_inc(sm["cV"].h, 1)
                v.wait_ge(sm["cV"].h, n)

        def f_scal2(sc):
            for tt in range(NT128):
                sc.wait_ge(sm["cV"].h, cVb + tt * NSTEP + 8)
                sc.activation(w2_sb[:], d21_sb[:], AF.Sigmoid).then_inc(sm["cA"].h, 1)

        block.tensor(f_pe2)
        block.vector(f_vec2)
        block.scalar(f_scal2)
        sm["pPE"].v += 2 * NT128 + 2 * NTH
        sm["cV"].v += NSTEP * NT128 + NT128 + 2 * NTH
        sm["cA"].v += NT128
        BAR()
        FR(ps_low, ps_wrep, ps_wfT, ps_g)

        ps_o = [AL(nc.psum_tensor(f"ps_fo{tag}_{i}", [128, 512], F32)) for i in range(4)]
        pPEb, cVb, cAb, cPLb = sm["pPE"].v, sm["cV"].v, sm["cA"].v, sm["cPL"].v
        dwb = [sm["dw0"].v, sm["dw1"].v]
        dxb = [sm["dx0"].v, sm["dx1"].v]
        dstb = [sm[f"dst{j}"].v for j in range(4)]
        NIT = C * NTH
        CAK = 2 if do_stats else 1

        def it(oc, th):
            return oc * NTH + th

        def pool_copies(g, po):
            for th in range(NTH):
                k = it(po, th)
                b4 = k % 4
                g.wait_ge(sm["cV"].h, cVb + k + 1)
                if k >= 4:
                    g.wait_ge(sm["pPE"].h, pPEb + k + 1)
                g.tensor_copy(xbs[b4][:], x1b[b4][:]).then_inc(sm["cPL"].h, 1)

        def f_gps_w(g):
            for oc in range(C):
                sl = oc % 2
                if oc >= 2:
                    g.wait_ge(sm["pPE"].h, pPEb + it(oc - 2, NTH - 1) + 1)
                g.dma_start(
                    out=bwb[sl][:].rearrange("p (c o) -> p c o", c=C),
                    in_=basewT[:, oc * 128:(oc + 1) * 128].rearrange("(c p) o -> p c o", p=128),
                ).then_inc(sm[f"dw{sl}"].h, 16)
                if do_stats and oc >= 1:
                    pool_copies(g, oc - 1)
            if do_stats:
                pool_copies(g, C - 1)

        def f_sync(s):
            if dst_wait is not None:
                for j in range(4):
                    s.wait_ge(sm[f"dst{j}"].h, dst_wait[j])
            for oc in range(C):
                sl = oc % 2
                if oc >= 2:
                    s.wait_ge(sm["cV"].h, cVb + it(oc - 2, NTH - 1) + 1)
                s.dma_start(out=xcb[sl][:], in_=resid_ap[oc * 128:(oc + 1) * 128, :]).then_inc(sm[f"dx{sl}"].h, 16)
                for th in range(NTH):
                    b4 = it(oc, th) % 4
                    s.wait_ge(sm["cV"].h, cVb + it(oc, th) + 1)
                    s.dma_start(out=out_ap[oc * 128:(oc + 1) * 128, th * 512:(th + 1) * 512],
                                in_=x1b[b4][:]).then_inc(sm[f"dst{b4}"].h, 16)

        def f_pe(t):
            for oc in range(C):
                sl = oc % 2
                t.wait_ge(sm[f"dw{sl}"].h, dwb[sl] + (oc // 2 + 1) * 16)
                for th in range(NTH):
                    k = it(oc, th)
                    if k >= 4:
                        t.wait_ge(sm["cA"].h, cAb + CAK * (k - 4) + 1)
                for c in range(C):
                    for th in range(NTH):
                        b4 = it(oc, th) % 4
                        t.matmul(ps_o[b4][:], bwb[sl][:, c * 128:(c + 1) * 128],
                                 xnT[:, c * TC + th * 512: c * TC + th * 512 + 512],
                                 start=(c == 0), stop=False)
                for th in range(NTH):
                    b4 = it(oc, th) % 4
                    t.matmul(ps_o[b4][:], lbt_sb[:, oc * 128:(oc + 1) * 128],
                             mid_sb[:, th * 512:(th + 1) * 512],
                             start=False, stop=True).then_inc(sm["pPE"].h, 1)
                if do_stats and oc >= 1:
                    po = oc - 1
                    for th in range(NTH):
                        b4 = it(po, th) % 4
                        t.wait_ge(sm["cPL"].h, cPLb + it(po, th) + 1)
                        t.wait_ge(sm["cA"].h, cAb + CAK * it(po, th) + 2)
                        t.matmul(ps_s1[0:1, th * 512:(th + 1) * 512], onesb[:], xbs[b4][:],
                                 start=(po == 0), stop=(po == C - 1))
                        t.matmul(ps_s2[0:1, th * 512:(th + 1) * 512], onesb[:], xqs[b4][:],
                                 start=(po == 0), stop=(po == C - 1))
            if do_stats:
                po = C - 1
                for th in range(NTH):
                    b4 = it(po, th) % 4
                    t.wait_ge(sm["cPL"].h, cPLb + it(po, th) + 1)
                    t.wait_ge(sm["cA"].h, cAb + CAK * it(po, th) + 2)
                    t.matmul(ps_s1[0:1, th * 512:(th + 1) * 512], onesb[:], xbs[b4][:],
                             start=(po == 0), stop=(po == C - 1))
                    ins = t.matmul(ps_s2[0:1, th * 512:(th + 1) * 512], onesb[:], xqs[b4][:],
                                   start=(po == 0), stop=(po == C - 1))
                ins.then_inc(sm["pPE"].h, 1)

        def f_scal(sc):
            for oc in range(C):
                for th in range(NTH):
                    k = it(oc, th)
                    b4 = k % 4
                    sc.wait_ge(sm["pPE"].h, pPEb + k + 1)
                    if k >= 4:
                        sc.wait_ge(sm["cV"].h, cVb + (k - 4) + 1)
                    sc.activation(t1b[b4][:], ps_o[b4][:], AF.Identity,
                                  bias=bias05[:, oc:oc + 1], scale=0.5).then_inc(sm["cA"].h, 1)
                    if do_stats:
                        sc.wait_ge(sm["cV"].h, cVb + k + 1)
                        sc.activation(xqs[b4][:], x1b[b4][:], AF.Square).then_inc(sm["cA"].h, 1)

        def f_vec(v):
            for oc in range(C):
                sl = oc % 2
                v.wait_ge(sm[f"dx{sl}"].h, dxb[sl] + (oc // 2 + 1) * 16)
                for th in range(NTH):
                    k = it(oc, th)
                    b4 = k % 4
                    v.wait_ge(sm["cA"].h, cAb + CAK * k + 1)
                    if k >= 4:
                        v.wait_ge(sm[f"dst{b4}"].h, dstb[b4] + ((k - 4) // 4 + 1) * 16)
                        if do_stats:
                            v.wait_ge(sm["cA"].h, cAb + CAK * (k - 4) + 2)
                            v.wait_ge(sm["cPL"].h, cPLb + (k - 4) + 1)
                    v.tensor_add(x1b[b4][:], t1b[b4][:],
                                 xcb[sl][:, th * 512:(th + 1) * 512]).then_inc(sm["cV"].h, 1)

        block.gpsimd(f_gps_w)
        block.sync(f_sync)
        block.tensor(f_pe)
        block.scalar(f_scal)
        block.vector(f_vec)

        sm["dw0"].v += 16 * ((C + 1) // 2)
        sm["dw1"].v += 16 * (C // 2)
        sm["dx0"].v += 16 * ((C + 1) // 2)
        sm["dx1"].v += 16 * (C // 2)
        for j in range(4):
            sm[f"dst{j}"].v += 16 * ((NIT - j + 3) // 4)
        sm["pPE"].v += NIT + (1 if do_stats else 0)
        sm["cA"].v += NIT * CAK
        sm["cV"].v += NIT
        if do_stats:
            sm["cPL"].v += NIT
        BAR()
        FR(*ps_o[::-1])

    ps_rb1 = AL(nc.psum_tensor("ps_rb1", [128, TC], F32))
    ps_mb1 = AL(nc.psum_tensor("ps_mb1", [128, TC], F32))
    ps_s1a = AL(nc.psum_tensor("ps_s1a", [1, TC], F32))
    ps_s2a = AL(nc.psum_tensor("ps_s2a", [1, TC], F32))
    ln_stream_stats(x_f, ps_s1a, ps_s2a)
    ln_rowmath_bcast(ps_s1a, ps_s2a, ps_rb1, ps_mb1)
    BAR()
    FR(ps_s2a, ps_s1a)
    xn1T = AL(nc.sbuf_tensor("xn1T", [128, C * TC], BF16))
    ln_apply(x_f, xn1T, ps_rb1, ps_mb1)
    BAR()

    FR(ps_mb1, ps_rb1)
    ps_s1b = AL(nc.psum_tensor("ps_s1b", [1, TC], F32))
    ps_s2b = AL(nc.psum_tensor("ps_s2b", [1, TC], F32))
    ffn(xn1T, x_f, xs1, ps_s1b, ps_s2b)
    dst_after_ffn1 = {j: sm[f"dst{j}"].v for j in range(4)}
    ps_rb2 = AL(nc.psum_tensor("ps_rb2", [128, TC], F32))
    ps_mb2 = AL(nc.psum_tensor("ps_mb2", [128, TC], F32))
    ln_rowmath_bcast(ps_s1b, ps_s2b, ps_rb2, ps_mb2)
    BAR()

    FR(xn1T)
    wq_sb = AL(nc.sbuf_tensor("wq_sb", [128, C * 512], BF16))
    def ph4_wq(g):
        g.dma_start(out=wq_sb[:].rearrange("p (c e) -> p c e", c=C),
                    in_=wq_in.rearrange("(c p) e -> p c e", p=128)).then_inc(sm["dwq"].h, 16)
    block.gpsimd(ph4_wq)
    sm["dwq"].v += 16
    xn2T = AL(nc.sbuf_tensor("xn2T", [128, C * TC], BF16))
    ln_apply(xs1, xn2T, ps_rb2, ps_mb2, also_ag=ag_x_in, dst_wait=dst_after_ffn1)
    dag_after_x = sm["dag"].v
    BAR()
    FR(ps_mb2, ps_rb2, ps_s2b, ps_s1b)

    kc_sb = AL(nc.sbuf_tensor("kc_sb", [128, TC], BF16))
    vc_sb = AL(nc.sbuf_tensor("vc_sb", [128, TC], BF16))
    ps_k = AL(nc.psum_tensor("ps_k", [128, 512], F32))
    ps_v = AL(nc.psum_tensor("ps_v", [128, 128], F32))
    pPEb, cVb = sm["pPE"].v, sm["cV"].v
    dmgb = sm["dmiscg"].v

    def ph4_gps(g):
        g.dma_start(out=bwb[0][:].rearrange("p (c e) -> p c e", c=C),
                    in_=wkv_in[:, 0:128].rearrange("(c p) e -> p c e", p=128)).then_inc(sm["dmiscg"].h, 16)
        g.dma_start(out=bwb[1][:].rearrange("p (c e) -> p c e", c=C),
                    in_=wkv_in[:, 128:256].rearrange("(c p) e -> p c e", p=128)).then_inc(sm["dmiscg"].h, 16)

    def ph4_pe(t):
        t.wait_ge(sm["dmiscg"].h, dmgb + 32)
        for th in range(NTH):
            if th > 0:
                t.wait_ge(sm["cV"].h, cVb + th)
            for c in range(C):
                ins = t.matmul(ps_k[:], bwb[0][:, c * 128:(c + 1) * 128],
                               xn2T[:, c * TC + th * 512: c * TC + th * 512 + 512],
                               start=(c == 0), stop=(c == C - 1))
            ins.then_inc(sm["pPE"].h, 1)
        for tt in range(NT128):
            if tt > 0:
                t.wait_ge(sm["cV"].h, cVb + NTH + tt)
            for c in range(C):
                ins = t.matmul(ps_v[:], xn2T[:, c * TC + tt * 128: c * TC + (tt + 1) * 128],
                               bwb[1][:, c * 128:(c + 1) * 128],
                               start=(c == 0), stop=(c == C - 1))
            ins.then_inc(sm["pPE"].h, 1)

    def ph4_vec(v):
        for th in range(NTH):
            v.wait_ge(sm["pPE"].h, pPEb + th + 1)
            v.tensor_copy(kc_sb[:, th * 512:(th + 1) * 512], ps_k[:]).then_inc(sm["cV"].h, 1)
        for tt in range(NT128):
            v.wait_ge(sm["pPE"].h, pPEb + NTH + tt + 1)
            v.tensor_copy(vc_sb[:, tt * 128:(tt + 1) * 128], ps_v[:]).then_inc(sm["cV"].h, 1)

    def ph4_gps2(g):
        g.wait_ge(sm["cV"].h, cVb + NTH)
        g.dma_start(out=ag_k_in, in_=kc_sb[:]).then_inc(sm["dag"].h, 16)
        g.wait_ge(sm["cV"].h, cVb + NTH + NT128)
        g.dma_start(out=agv_in.rearrange("(g p) d -> p g d", p=128),
                    in_=vc_sb[:].rearrange("p (g d) -> p g d", d=128)).then_inc(sm["dag"].h, 16)
        g.wait_ge(sm["dag"].h, dag_after_x + 32)
        g.collective_compute("AllGather", ALU.bypass, replica_groups=[list(range(NCORE))],
                             ins=[ag_k_in], outs=[ag_k_out]).then_inc(sm["cc"].h, 1)
        g.collective_compute("AllGather", ALU.bypass, replica_groups=[list(range(NCORE))],
                             ins=[agv_in], outs=[agv_out]).then_inc(sm["cc"].h, 1)

    block.gpsimd(ph4_gps)
    sm["dmiscg"].v += 32
    block.tensor(ph4_pe)
    block.vector(ph4_vec)
    block.gpsimd(ph4_gps2)
    sm["pPE"].v += NTH + NT128
    sm["cV"].v += NTH + NT128
    sm["dag"].v += 32
    sm["cc"].v += 2
    dag_kv_done = sm["dag"].v
    cc_x0, cc_k, cc_v = sm["cc"].v - 6, sm["cc"].v - 1, sm["cc"].v
    BAR()
    FR(ps_v, ps_k)
    FR(vc_sb, kc_sb)

    FR(xn2T)
    xgc = [AL(nc.sbuf_tensor(f"xgc{i}", [128, TC], BF16)) for i in range(4)]
    qt_sb = [AL(nc.sbuf_tensor(f"qt{i}", [128, 512], BF16)) for i in range(2)]
    ps_q = [AL(nc.psum_tensor(f"ps_q{i}", [128, 512], F32)) for i in range(4 * NTH)]
    pPEb, cAb = sm["pPE"].v, sm["cA"].v
    dxb = [sm[f"dx{j}"].v for j in range(4)]
    dqb = [sm["dq0"].v, sm["dq1"].v]
    dmgb = sm["dmiscg"].v
    NPQ = 4 * NTH

    def ph5_gps(g):
        for r in range(NCORE):
            for k in range(NPQ):
                hh, th = k // NTH, k % NTH
                kk = r * NPQ + k
                g.wait_ge(sm["cA"].h, cAb + kk + 1)
                g.dma_start(out=qdram[hh * 128:(hh + 1) * 128,
                                      r * TC + th * 512: r * TC + th * 512 + 512],
                            in_=qt_sb[kk % 2][:]).then_inc(sm[f"dq{kk % 2}"].h, 16)

    def ph5_sync(s):
        s.wait_ge(sm["dag"].h, dag_kv_done)
        for r in range(NCORE):
            for c in range(C):
                idx = r * C + c
                sl = idx % 4
                if r == 0:
                    s.wait_ge(sm["cc"].h, cc_x0 + c // 8 + 1)
                if idx >= 4:
                    s.wait_ge(sm["pPE"].h, pPEb + (idx - 4) + 1)
                q, cq = c // 8, c % 8
                s.dma_start(out=xgc[sl][:],
                            in_=ag_xo[q][r * 1024 + cq * 128: r * 1024 + (cq + 1) * 128, :]).then_inc(sm[f"dx{sl}"].h, 16)

    def ph5_pe(t):
        t.wait_ge(sm["dwq"].h, 16)
        for r in range(NCORE):
            for c in range(C):
                idx = r * C + c
                sl = idx % 2
                sl4 = idx % 4
                t.wait_ge(sm[f"dx{sl4}"].h, dxb[sl4] + (idx // 4 + 1) * 16)
                for k in range(NPQ):
                    if c == 0 and r > 0:
                        t.wait_ge(sm["cA"].h, cAb + (r - 1) * NPQ + k + 1)
                    hh, th = k // NTH, k % NTH
                    ins = t.matmul(ps_q[k][:], wq_sb[:, c * 512 + hh * 128: c * 512 + (hh + 1) * 128],
                                   xgc[sl4][:, th * 512:(th + 1) * 512],
                                   start=(c == 0), stop=(c == C - 1))
                ins.then_inc(sm["pPE"].h, 1)

    def ph5_scal(sc):
        for r in range(NCORE):
            sc.wait_ge(sm["pPE"].h, pPEb + r * C + C)
            for k in range(NPQ):
                kk = r * NPQ + k
                if kk >= 2:
                    sc.wait_ge(sm[f"dq{kk % 2}"].h, dqb[kk % 2] + ((kk - 2) // 2 + 1) * 16)
                sc.activation(qt_sb[kk % 2][:], ps_q[k][:], AF.Copy,
                              scale=QSCALE).then_inc(sm["cA"].h, 1)

    block.gpsimd(ph5_gps)
    block.sync(ph5_sync)
    block.tensor(ph5_pe)
    block.scalar(ph5_scal)
    sm["dq0"].v += 16 * ((NCORE * NPQ + 1) // 2)
    sm["dq1"].v += 16 * ((NCORE * NPQ) // 2)
    for j in range(4):
        sm[f"dx{j}"].v += 16 * ((NCORE * C - j + 3) // 4)
    sm["pPE"].v += NCORE * C
    sm["cA"].v += NCORE * NPQ
    dq_total = (sm["dq0"].v, sm["dq1"].v)
    BAR()
    FR(*ps_q[::-1])
    FR(qt_sb[1], qt_sb[0], xgc[3], xgc[2], xgc[1], xgc[0], wq_sb)

    kf_sb = AL(nc.sbuf_tensor("kf_sb", [128, S], BF16))
    vt_sb = AL(nc.sbuf_tensor("vt_sb", [128, S], BF16))
    dm_sb = AL(nc.sbuf_tensor("dm_sb", [128, 2048], F32))
    dm2_sb = AL(nc.sbuf_tensor("dm2_sb", [128, 2048], F32))
    sD0 = AL(nc.sbuf_tensor("sD0", [128, 512], F32))
    sDm = AL(nc.sbuf_tensor("sDm", [128, 2048], F32))
    sDm2 = AL(nc.sbuf_tensor("sDm2", [128, 2048], F32))
    dT_sb = AL(nc.sbuf_tensor("dT_sb", [128, 72], F32))
    bvt_sb = AL(nc.sbuf_tensor("bvt_sb", [128, 72], F32))
    prob_sb = [AL(nc.sbuf_tensor(f"prob{i}", [128, 512], BF16)) for i in range(4)]
    qt6 = [AL(nc.sbuf_tensor(f"qt6_{i}", [128, 512], BF16)) for i in range(2)]
    zb_sb = [AL(nc.sbuf_tensor(f"zb{i}", [128, 512], F32)) for i in range(2)]
    lnz_sb = [AL(nc.sbuf_tensor(f"lnz{i}", [128, 512], F32)) for i in range(2)]
    ao_sb = [AL(nc.sbuf_tensor(f"ao{i}", [128, 512], BF16)) for i in range(2)]
    ps_sc = [AL(nc.psum_tensor(f"ps_sc{i}", [128, 512], F32)) for i in range(4)]
    ps_ao = [AL(nc.psum_tensor(f"ps_ao{i}", [128, 512], F32)) for i in range(2)]
    ps_z = [AL(nc.psum_tensor(f"ps_z{i}", [128, 512], F32)) for i in range(2)]

    sched = []
    for hh in (3, 2, 1, 0):
        for qt in range(NQT):
            sched.append((hh, qt, list(range(kt_lo(hh, qt), 4 * qt + 4))))
    NG = len(sched)
    NSC = [len(k) for (_, _, k) in sched]
    CSC = [0]
    for nsc in NSC:
        CSC.append(CSC[-1] + nsc)

    pPEb, pPEvb = sm["pPE"].v, sm["pPEv"].v
    cVb, cAb = sm["cV"].v, sm["cA"].v
    daob = [sm["dao0"].v, sm["dao1"].v]
    dkob = sm["dko"].v
    dqtb = [sm["dqt0"].v, sm["dqt1"].v]

    def is_phi(hh, m):
        return hh >= 2 and m >= 1

    cv_at, ca_at, pev_at, cv_after_group, hh_prep_at, ca_z = {}, {}, {}, {}, {}, {}
    cv_run, ca_run, pev_run = cVb, cAb, pPEvb
    for g, (hh, qt, kts) in enumerate(sched):
        if qt == 0:
            cv_run += 5
            hh_prep_at[hh] = cv_run
        for j, kt in enumerate(kts):
            m = 4 * qt - kt
            if not is_phi(hh, m):
                cv_run += 1
                cv_at[(g, j)] = cv_run
            ca_run += 1
            ca_at[(g, j)] = ca_run
        pev_run += len(kts)
        pev_at[g] = pev_run
        ca_run += 2
        ca_z[g] = ca_run
        cv_run += 1
        cv_after_group[g] = cv_run

    def ph6_sync(s):
        s.wait_ge(sm["dq0"].h, dq_total[0])
        s.wait_ge(sm["dq1"].h, dq_total[1])
        s.dma_start(out=dm_sb[:], in_=dm_in).then_inc(sm["dmisc"].h, 16)
        s.dma_start(out=dm2_sb[:], in_=dm2_in).then_inc(sm["dmisc"].h, 16)
        s.wait_ge(sm["cc"].h, cc_k)
        for r in range(NCORE):
            s.dma_start(out=kf_sb[:, r * TC:(r + 1) * TC],
                        in_=ag_k_out[r * 128:(r + 1) * 128, :]).then_inc(sm["dko"].h, 16)
        s.wait_ge(sm["cc"].h, cc_v)
        s.dma_start(out=vt_sb[:].rearrange("p (g d) -> p g d", d=128),
                    in_=agv_out.rearrange("(g p) d -> p g d", p=128)).then_inc(sm["dko"].h, 16)
        for g, (hh, qt, kts) in enumerate(sched):
            if g >= 2:
                s.wait_ge(sm["pPE"].h, pPEb + CSC[g - 1])
            s.dma_start(out=qt6[g % 2][:],
                        in_=qdram[hh * 128:(hh + 1) * 128, qt * 512:(qt + 1) * 512]).then_inc(sm[f"dqt{g % 2}"].h, 16)

    def ph6_vec(v):
        v.wait_ge(sm["dmisc"].h, 96)
        v.wait_ge(sm["dq0"].h, dq_total[0])
        v.wait_ge(sm["dq1"].h, dq_total[1])
        for g, (hh, qt, kts) in enumerate(sched):
            if qt == 0:
                if g > 0:
                    pg = g - 1
                    v.wait_ge(sm["cA"].h, ca_at[(pg, len(sched[pg][2]) - 1)])
                v.tensor_scalar(sD0[:], consts_sb[:, O_D0:O_D0 + 512], slope4[:, hh:hh + 1], None, ALU.mult).then_inc(sm["cV"].h, 1)
                v.tensor_scalar(sDm[:], dm_sb[:], slope4[:, hh:hh + 1], None, ALU.mult).then_inc(sm["cV"].h, 1)
                v.tensor_scalar(sDm2[:], dm2_sb[:], slope4[:, hh:hh + 1], None, ALU.mult).then_inc(sm["cV"].h, 1)
                v.tensor_scalar(dT_sb[:], consts_sb[:, O_DB:O_DB + 72], slope4[:, hh:hh + 1], None, ALU.mult).then_inc(sm["cV"].h, 1)
                v.tensor_scalar(bvt_sb[:], consts_sb[:, O_CB:O_CB + 72], slope4[:, hh:hh + 1], -5.0, ALU.mult, ALU.add).then_inc(sm["cV"].h, 1)
                v.wait_ge(sm["cV"].h, hh_prep_at[hh])
            for j, kt in enumerate(kts):
                sl = j % 4
                m = 4 * qt - kt
                if is_phi(hh, m):
                    continue
                v.wait_ge(sm["pPE"].h, pPEb + CSC[g] + j + 1)
                if m <= 0:
                    dmt = sDm2 if hh >= 2 else sDm
                    ins = v.tensor_add(ps_sc[sl][:], ps_sc[sl][:], dmt[:, (-m) * 512:(-m + 1) * 512])
                else:
                    ins = v.scalar_tensor_tensor(ps_sc[sl][:], ps_sc[sl][:], dT_sb[:, m:m + 1],
                                                 sD0[:], ALU.add, ALU.add)
                ins.then_inc(sm["cV"].h, 1)
            sl2 = g % 2
            v.wait_ge(sm["pPEv"].h, pev_at[g])
            v.wait_ge(sm["cA"].h, ca_z[g])
            if g >= 2:
                v.wait_ge(sm[f"dao{sl2}"].h, daob[sl2] + ((g - 2) // 2 + 1) * 16)
            v.tensor_mul(ao_sb[sl2][:], ps_ao[sl2][:], zb_sb[sl2][:]).then_inc(sm["cV"].h, 1)

    def ph6_scal(sc):
        for g, (hh, qt, kts) in enumerate(sched):
            for j, kt in enumerate(kts):
                sl = j % 4
                m = 4 * qt - kt
                if is_phi(hh, m):
                    sc.wait_ge(sm["pPE"].h, pPEb + CSC[g] + j + 1)
                    sc.wait_ge(sm["cV"].h, hh_prep_at[hh])
                    bias_ap = bvt_sb[:, m:m + 1]
                else:
                    sc.wait_ge(sm["cV"].h, cv_at[(g, j)])
                    bias_ap = c5_sb[:, 0:1]
                if j >= 4:
                    sc.wait_ge(sm["pPEv"].h, pPEvb + CSC[g] + (j - 4) + 1)
                elif g >= 1:
                    sc.wait_ge(sm["pPEv"].h, pev_at[g - 1])
                sc.activation(prob_sb[sl][:], ps_sc[sl][:], AF.Exp,
                              bias=bias_ap).then_inc(sm["cA"].h, 1)
            sl2 = g % 2
            sc.wait_ge(sm["pPEv"].h, pev_at[g])
            if g >= 2:
                sc.wait_ge(sm["cV"].h, cv_after_group[g - 2])
            sc.activation(lnz_sb[sl2][:], ps_z[sl2][:], AF.Ln).then_inc(sm["cA"].h, 1)
            sc.wait_ge(sm["cA"].h, ca_z[g] - 1)
            sc.activation(zb_sb[sl2][:], lnz_sb[sl2][:], AF.Exp, scale=-1.0).then_inc(sm["cA"].h, 1)

    def ph6_pe(t):
        t.wait_ge(sm["dko"].h, dkob + 16 * (NCORE + 1))
        for g, (hh, qt, kts) in enumerate(sched):
            sl2 = g % 2
            nk = len(kts)
            t.wait_ge(sm[f"dqt{g % 2}"].h, dqtb[g % 2] + (g // 2 + 1) * 16)
            LV = 2
            for j in range(nk + LV):
                if j < nk:
                    kt = kts[j]
                    sl = j % 4
                    if j >= 4:
                        t.wait_ge(sm["cA"].h, ca_at[(g, j - 4)])
                    elif g >= 1:
                        pg = g - 1
                        t.wait_ge(sm["cA"].h, ca_at[(pg, len(sched[pg][2]) - 1)])
                    t.matmul(ps_sc[sl][:], kf_sb[:, kt * 128:(kt + 1) * 128],
                             qt6[g % 2][:], start=True, stop=True).then_inc(sm["pPE"].h, 1)
                jj = j - LV
                if jj >= 0:
                    kt = kts[jj]
                    sl = jj % 4
                    t.wait_ge(sm["cA"].h, ca_at[(g, jj)])
                    if jj == 0 and g >= 2:
                        t.wait_ge(sm["cV"].h, cv_after_group[g - 2])
                        t.wait_ge(sm["cA"].h, ca_z[g - 2])
                    t.matmul(ps_ao[sl2][:], vt_sb[:, kt * 128:(kt + 1) * 128], prob_sb[sl][:],
                             start=(jj == 0), stop=(jj == nk - 1))
                    t.matmul(ps_z[sl2][:], ones128b[:], prob_sb[sl][:],
                             start=(jj == 0), stop=(jj == nk - 1)).then_inc(sm["pPEv"].h, 1)

    def ph6_gps(g_):
        for g, (hh, qt, kts) in enumerate(sched):
            sl2 = g % 2
            g_.wait_ge(sm["cV"].h, cv_after_group[g])
            g_.dma_start(out=ag_o_in[hh * 128:(hh + 1) * 128, qt * 512:(qt + 1) * 512],
                         in_=ao_sb[sl2][:]).then_inc(sm[f"dao{sl2}"].h, 16)
            if qt == NQT - 1:
                ng = g + 1
                g_.wait_ge(sm["dao0"].h, daob[0] + 16 * ((ng + 1) // 2))
                g_.wait_ge(sm["dao1"].h, daob[1] + 16 * (ng // 2))
                g_.collective_compute(
                    "AllGather", ALU.bypass, replica_groups=[list(range(NCORE))],
                    ins=[ag_o_in[hh * 128:(hh + 1) * 128, :]],
                    outs=[ag_oo[hh]]).then_inc(sm["cc"].h, 1)

    block.sync(ph6_sync)
    block.tensor(ph6_pe)
    block.vector(ph6_vec)
    block.scalar(ph6_scal)
    block.gpsimd(ph6_gps)
    sm["dko"].v += 16 * (NCORE + 1)
    sm["dmisc"].v += 32
    sm["dqt0"].v += 16 * ((NG + 1) // 2)
    sm["dqt1"].v += 16 * (NG // 2)
    sm["pPE"].v += CSC[-1]
    sm["pPEv"].v = pev_run
    sm["cV"].v = cv_run
    sm["cA"].v = ca_run
    sm["dao0"].v += 16 * ((NG + 1) // 2)
    sm["dao1"].v += 16 * (NG // 2)
    sm["cc"].v += 4
    cc_o = sm["cc"].v
    BAR()
    FR(*ps_z[::-1]); FR(*ps_ao[::-1]); FR(*ps_sc[::-1])
    FR(ao_sb[1], ao_sb[0], lnz_sb[1], lnz_sb[0], zb_sb[1], zb_sb[0],
       qt6[1], qt6[0], prob_sb[3], prob_sb[2], prob_sb[1], prob_sb[0],
       bvt_sb, dT_sb, sDm2, sDm, sD0, dm2_sb, dm_sb, vt_sb, kf_sb)

    attn_sb = AL(nc.sbuf_tensor("attn_sb", [128, C * TC], BF16))
    dxgb7 = sm["dxg"].v

    def ph7_sync_pre(s):
        pid = s.partition_id()
        for i, hh in enumerate((3, 2, 1, 0)):
            s.wait_ge(sm["cc"].h, cc_o - 3 + i)
            out_v = attn_sb[:].rearrange("p (r h t) -> p r h t", h=4, t=TC)[:, :, hh, :]
            in_v = ag_oo[hh][:, bass.ts(pid, TC)].rearrange("(r p) t -> p r t", p=128)
            s.dma_start(out=out_v, in_=in_v).then_inc(sm["dxg"].h, 16)
    block.sync(ph7_sync_pre)
    sm["dxg"].v += 64

    ps_s1c = AL(nc.psum_tensor("ps_s1c", [1, TC], F32))
    ps_s2c = AL(nc.psum_tensor("ps_s2c", [1, TC], F32))
    ps_o7 = [AL(nc.psum_tensor(f"ps_wo{i}", [128, 512], F32)) for i in range(4)]
    pPEb, cVb, cAb, cPLb = sm["pPE"].v, sm["cV"].v, sm["cA"].v, sm["cPL"].v
    dwb = [sm["dw0"].v, sm["dw1"].v]
    dxb = [sm["dx0"].v, sm["dx1"].v]
    dstb = [sm[f"dst{j}"].v for j in range(4)]
    NIT = C * NTH

    def it7(oc, th):
        return oc * NTH + th

    def ph7_pool_copies(g, po):
        for th in range(NTH):
            k = it7(po, th)
            b4 = k % 4
            g.wait_ge(sm["cV"].h, cVb + k + 1)
            if k >= 4:
                g.wait_ge(sm["pPE"].h, pPEb + k + 1)
            g.tensor_copy(xbs[b4][:], x1b[b4][:]).then_inc(sm["cPL"].h, 1)

    def ph7_gps_w(g):
        for oc in range(C):
            sl = oc % 2
            if oc >= 2:
                g.wait_ge(sm["pPE"].h, pPEb + it7(oc - 2, NTH - 1) + 1)
            g.dma_start(
                out=bwb[sl][:].rearrange("p (c o) -> p c o", c=C),
                in_=wc_p[:, oc * 128:(oc + 1) * 128].rearrange("(c p) o -> p c o", p=128),
            ).then_inc(sm[f"dw{sl}"].h, 16)
            if oc >= 1:
                ph7_pool_copies(g, oc - 1)
        ph7_pool_copies(g, C - 1)

    def ph7_sync(s):
        for j in range(4):
            s.wait_ge(sm[f"dst{j}"].h, dst_after_ffn1[j])
        for oc in range(C):
            sl = oc % 2
            if oc >= 2:
                s.wait_ge(sm["cV"].h, cVb + it7(oc - 2, NTH - 1) + 1)
            s.dma_start(out=xcb[sl][:], in_=xs1[oc * 128:(oc + 1) * 128, :]).then_inc(sm[f"dx{sl}"].h, 16)
            for th in range(NTH):
                b4 = it7(oc, th) % 4
                s.wait_ge(sm["cV"].h, cVb + it7(oc, th) + 1)
                s.dma_start(out=xs2[oc * 128:(oc + 1) * 128, th * 512:(th + 1) * 512],
                            in_=x1b[b4][:]).then_inc(sm[f"dst{b4}"].h, 16)

    def ph7_pe(t):
        t.wait_ge(sm["dxg"].h, dxgb7 + 64)
        for oc in range(C):
            sl = oc % 2
            t.wait_ge(sm[f"dw{sl}"].h, dwb[sl] + (oc // 2 + 1) * 16)
            for th in range(NTH):
                k = it7(oc, th)
                if k >= 4:
                    t.wait_ge(sm["cV"].h, cVb + (k - 4) + 1)
            for c in range(C):
                for th in range(NTH):
                    b4 = it7(oc, th) % 4
                    ins = t.matmul(ps_o7[b4][:], bwb[sl][:, c * 128:(c + 1) * 128],
                                   attn_sb[:, c * TC + th * 512: c * TC + th * 512 + 512],
                                   start=(c == 0), stop=(c == C - 1))
                    if c == C - 1:
                        ins.then_inc(sm["pPE"].h, 1)
            if oc >= 1:
                po = oc - 1
                for th in range(NTH):
                    b4 = it7(po, th) % 4
                    t.wait_ge(sm["cPL"].h, cPLb + it7(po, th) + 1)
                    t.wait_ge(sm["cA"].h, cAb + it7(po, th) + 1)
                    t.matmul(ps_s1c[0:1, th * 512:(th + 1) * 512], onesb[:], xbs[b4][:],
                             start=(po == 0), stop=(po == C - 1))
                    t.matmul(ps_s2c[0:1, th * 512:(th + 1) * 512], onesb[:], xqs[b4][:],
                             start=(po == 0), stop=(po == C - 1))
        po = C - 1
        for th in range(NTH):
            b4 = it7(po, th) % 4
            t.wait_ge(sm["cPL"].h, cPLb + it7(po, th) + 1)
            t.wait_ge(sm["cA"].h, cAb + it7(po, th) + 1)
            t.matmul(ps_s1c[0:1, th * 512:(th + 1) * 512], onesb[:], xbs[b4][:],
                     start=(po == 0), stop=(po == C - 1))
            ins = t.matmul(ps_s2c[0:1, th * 512:(th + 1) * 512], onesb[:], xqs[b4][:],
                           start=(po == 0), stop=(po == C - 1))
        ins.then_inc(sm["pPE"].h, 1)

    def ph7_vec(v):
        for oc in range(C):
            sl = oc % 2
            v.wait_ge(sm[f"dx{sl}"].h, dxb[sl] + (oc // 2 + 1) * 16)
            for th in range(NTH):
                k = it7(oc, th)
                b4 = k % 4
                v.wait_ge(sm["pPE"].h, pPEb + k + 1)
                if k >= 4:
                    v.wait_ge(sm[f"dst{b4}"].h, dstb[b4] + ((k - 4) // 4 + 1) * 16)
                    v.wait_ge(sm["cPL"].h, cPLb + (k - 4) + 1)
                    v.wait_ge(sm["cA"].h, cAb + (k - 4) + 1)
                v.scalar_tensor_tensor(x1b[b4][:], ps_o7[b4][:], 1.0,
                                       xcb[sl][:, th * 512:(th + 1) * 512],
                                       ALU.mult, ALU.add).then_inc(sm["cV"].h, 1)

    def ph7_scal(sc):
        for oc in range(C):
            for th in range(NTH):
                k = it7(oc, th)
                b4 = k % 4
                sc.wait_ge(sm["cV"].h, cVb + k + 1)
                sc.activation(xqs[b4][:], x1b[b4][:], AF.Square).then_inc(sm["cA"].h, 1)

    block.gpsimd(ph7_gps_w)
    block.sync(ph7_sync)
    block.tensor(ph7_pe)
    block.vector(ph7_vec)
    block.scalar(ph7_scal)
    sm["dw0"].v += 16 * ((C + 1) // 2)
    sm["dw1"].v += 16 * (C // 2)
    sm["dx0"].v += 16 * ((C + 1) // 2)
    sm["dx1"].v += 16 * (C // 2)
    for j in range(4):
        sm[f"dst{j}"].v += 16 * ((NIT - j + 3) // 4)
    sm["pPE"].v += NIT + 1
    sm["cV"].v += NIT
    sm["cA"].v += NIT
    sm["cPL"].v += NIT
    dst_after_wc = {j: sm[f"dst{j}"].v for j in range(4)}
    BAR()
    FR(*ps_o7[::-1])
    ps_rb3 = AL(nc.psum_tensor("ps_rb3", [128, TC], F32))
    ps_mb3 = AL(nc.psum_tensor("ps_mb3", [128, TC], F32))
    ln_rowmath_bcast(ps_s1c, ps_s2c, ps_rb3, ps_mb3)
    BAR()
    ln_apply(xs2, attn_sb, ps_rb3, ps_mb3, dst_wait=dst_after_wc)
    BAR()
    FR(ps_mb3, ps_rb3, ps_s2c, ps_s1c)

    ffn(attn_sb, xs2, out_f, None, None, dst_wait=dst_after_wc)

    dst_total = {j: sm[f"dst{j}"].v for j in range(4)}

    def fin_sync(s):
        for j in range(4):
            s.wait_ge(sm[f"dst{j}"].h, dst_total[j])
    block.sync(fin_sync)

    blk_cm.__exit__(None, None, None)
    for space in ("psum", "sbuf"):
        while _stacks[space]:
            h = _stacks[space].pop()
            _cm_of.pop(id(h)).__exit__(None, None, None)
    return nc


def _consts_np():
    ident = np.eye(128, dtype=np.float32)
    allones = np.ones((128, 128), dtype=np.float32)
    kk = np.arange(128, dtype=np.float32)[:, None]
    qq = np.arange(512, dtype=np.float32)[None, :]
    D0 = (kk - qq).astype(np.float32)
    deltaB = np.zeros((128, 72), dtype=np.float32)
    for mm in range(72):
        deltaB[:, mm] = -128.0 * mm
    CB = np.zeros((128, 72), dtype=np.float32)
    for mm in range(72):
        CB[:, mm] = np.arange(128) - 128.0 * mm - 511.0
    return np.ascontiguousarray(
        np.concatenate([ident, allones, D0, deltaB, CB], axis=1))


def _dm_np():
    kk = np.arange(128, dtype=np.float32)[:, None]
    qq = np.arange(512, dtype=np.float32)[None, :]
    Dms = []
    for jm in range(4):
        d = kk - qq + 128.0 * jm
        Dms.append(np.where(d <= 0.0, d, -1e13).astype(np.float32))
    return np.ascontiguousarray(np.concatenate(Dms, axis=1))


def _dm2_np():
    kk = np.arange(128, dtype=np.float32)[:, None]
    qq = np.arange(512, dtype=np.float32)[None, :]
    Dms = []
    for jm in range(4):
        valid = (kk - qq + 128.0 * jm) <= 0.0
        val = np.broadcast_to(kk + 128.0 * jm - 511.0, (128, 512))
        Dms.append(np.where(valid, val, -1e13).astype(np.float32))
    return np.ascontiguousarray(np.concatenate(Dms, axis=1))


def _e2_np():
    e2 = np.zeros((128, 128), dtype=np.float32)
    for e in range(NEXP):
        e2[e, e * R:(e + 1) * R] = 2.0
    return e2


def shard_inputs(S, L0_attn_hidden_states, L0_attn_w_qkv, L0_attn_w_c,
                 L0_ffn_gate_w, L0_ffn_base_w, L0_ffn_base_b, L0_ffn_lora_A, L0_ffn_lora_B):
    TC = S // NCORE
    x = np.ascontiguousarray(np.asarray(L0_attn_hidden_states, dtype=np.float32).reshape(S, D))
    wqkv = np.asarray(L0_attn_w_qkv, dtype=np.float32)
    wc = np.asarray(L0_attn_w_c, dtype=np.float32)
    gate_w = np.asarray(L0_ffn_gate_w, dtype=np.float32)
    base_w = np.asarray(L0_ffn_base_w, dtype=np.float32)
    base_b = np.asarray(L0_ffn_base_b, dtype=np.float32)
    lora_A = np.asarray(L0_ffn_lora_A, dtype=np.float32)
    lora_B = np.asarray(L0_ffn_lora_B, dtype=np.float32)

    basewT_np = np.ascontiguousarray(base_w.T)
    wkv_np = np.ascontiguousarray(wqkv[:, D:D + 256])
    gate_np = np.ascontiguousarray(gate_w.T)
    lat_np = np.ascontiguousarray(lora_A.reshape(NEXP * R, D).T)
    lbt_np = np.ascontiguousarray(lora_B.transpose(0, 2, 1).reshape(NEXP * R, D))
    bias_np = np.ascontiguousarray(base_b.reshape(C, 128).T)
    consts_np = _consts_np()
    e2_np = _e2_np()
    dm_np = _dm_np()
    dm2_np = _dm2_np()

    perm = np.zeros(D, dtype=np.int64)
    for r in range(NCORE):
        for hh in range(4):
            h = 8 * hh + r
            perm[r * 512 + hh * 128: r * 512 + (hh + 1) * 128] = np.arange(h * DH, (h + 1) * DH)
    wc_p_np = np.ascontiguousarray(wc[perm, :])

    in_maps = []
    for i in range(NCORE):
        heads = [8 * j + i for j in range(4)]
        wq_i = np.ascontiguousarray(
            np.concatenate([wqkv[:, h * DH:(h + 1) * DH] for h in heads], axis=1))
        slope4_np = np.zeros((128, 4), dtype=np.float32)
        for j, h in enumerate(heads):
            slope4_np[:, j] = SLOPES[h]
        x_i = np.ascontiguousarray(x[i * TC:(i + 1) * TC, :].T)
        in_maps.append(dict(
            x_f=x_i, basewT=basewT_np, wkv=wkv_np, wq=wq_i, wc_p=wc_p_np,
            gate_wT=gate_np, lora_AT=lat_np, lora_BT=lbt_np,
            bias_fm=bias_np, slope4=slope4_np, consts=consts_np, e2_in=e2_np, dm_in=dm_np, dm2_in=dm2_np,
        ))
    return in_maps


def assemble_output(S, results):
    TC = S // NCORE
    out = np.empty((S, D), dtype=np.float32)
    for i in range(NCORE):
        out[i * TC:(i + 1) * TC, :] = np.asarray(results[i]["out_f"]).T
    return out.reshape(1, S, D)


def kernel(**inputs):
    S = int(np.asarray(inputs["L0_attn_hidden_states"]).shape[1])
    in_maps = shard_inputs(S, **inputs)
    nc = build_program(S=S)
    res = run_bass_kernel_spmd(nc, in_maps, core_ids=list(range(NCORE)))
    return assemble_output(S, res.results)


# version 10
# speedup vs baseline: 1.0621x; 1.0621x over previous
import math
import numpy as np

import concourse.bass as bass
import concourse.mybir as mybir
from concourse.bass_utils import run_bass_kernel_spmd

F32 = mybir.dt.float32
BF16 = mybir.dt.bfloat16
AF = mybir.ActivationFunctionType
ALU = mybir.AluOpType
AX = mybir.AxisListType.X

D = 4096
C = D // 128
NCORE = 8
H = 32
DH = 128
NEXP = 8
R = 16
EPS = 1e-5
QSCALE = 1.0 / math.sqrt(DH)
T_WIN = 70.0

SLOPES = [2.0 ** (-(h + 1) / 4.0) for h in range(H)]
SLOT_W = [T_WIN / SLOPES[8 * j + 7] for j in range(4)]


def kt_lo(slot, qt):
    return max(0, int(math.floor((512.0 * qt - SLOT_W[slot]) / 128.0)))


class Sem:
    def __init__(self, nc, name):
        self._cm = nc.semaphore(name)
        self.h = self._cm.__enter__()
        self.v = 0


def build_program(S=8192):
    TC = S // NCORE
    NTH = TC // 512
    NQT = S // 512
    NT128 = TC // 128
    assert TC % 512 == 0

    nc = bass.Bass("TRN2", target_bir_lowering=False, debug=False)

    x_f = nc.dram_tensor("x_f", [D, TC], F32, kind="ExternalInput").ap()
    basewT = nc.dram_tensor("basewT", [D, D], F32, kind="ExternalInput").ap()
    wkv_in = nc.dram_tensor("wkv", [D, 256], F32, kind="ExternalInput").ap()
    wq_in = nc.dram_tensor("wq", [D, 512], F32, kind="ExternalInput").ap()
    wc_p = nc.dram_tensor("wc_p", [D, D], F32, kind="ExternalInput").ap()
    gate_wT = nc.dram_tensor("gate_wT", [D, NEXP], F32, kind="ExternalInput").ap()
    lora_AT = nc.dram_tensor("lora_AT", [D, NEXP * R], F32, kind="ExternalInput").ap()
    lora_BT = nc.dram_tensor("lora_BT", [NEXP * R, D], F32, kind="ExternalInput").ap()
    bias_fm = nc.dram_tensor("bias_fm", [128, C], F32, kind="ExternalInput").ap()
    slope4_in = nc.dram_tensor("slope4", [128, 4], F32, kind="ExternalInput").ap()
    e2_in = nc.dram_tensor("e2_in", [128, 128], F32, kind="ExternalInput").ap()
    NCONST = 128 + 128 + 512 + 72 + 72
    consts_in = nc.dram_tensor("consts", [128, NCONST], F32, kind="ExternalInput").ap()
    dm_in = nc.dram_tensor("dm_in", [128, 4 * 512], F32, kind="ExternalInput").ap()
    dm2_in = nc.dram_tensor("dm2_in", [128, 4 * 512], F32, kind="ExternalInput").ap()
    O_ID, O_ONE, O_D0, O_DB, O_CB = 0, 128, 256, 768, 840

    out_f = nc.dram_tensor("out_f", [D, TC], F32, kind="ExternalOutput").ap()

    xs1 = nc.dram_tensor("xs1", [D, TC], F32).ap()
    xs2 = nc.dram_tensor("xs2", [D, TC], F32).ap()
    qdram = nc.dram_tensor("qdram", [512, S], BF16).ap()
    ag_x_in = nc.dram_tensor("ag_x_in", [D, TC], BF16).ap()
    ag_xo = [nc.dram_tensor(f"ag_xo{q}", [NCORE * 1024, TC], BF16, addr_space="Shared").ap()
             for q in range(4)]
    ag_k_in = nc.dram_tensor("ag_k_in", [128, TC], BF16).ap()
    ag_k_out = nc.dram_tensor("ag_k_out", [NCORE * 128, TC], BF16, addr_space="Shared").ap()
    agv_in = nc.dram_tensor("agv_in", [TC, 128], BF16).ap()
    agv_out = nc.dram_tensor("agv_out", [S, 128], BF16, addr_space="Shared").ap()
    ag_o_in = nc.dram_tensor("ag_o_in", [512, S], BF16).ap()
    ag_oo = [nc.dram_tensor(f"ag_oo{h}", [NCORE * 128, S], BF16, addr_space="Shared").ap()
             for h in range(4)]

    _stacks = {"sbuf": [], "psum": []}
    _cm_of = {}

    def AL(cm):
        h = cm.__enter__()
        space = "psum" if "PSum" in type(h).__name__ else "sbuf"
        _stacks[space].append(h)
        _cm_of[id(h)] = cm
        return h

    def FR(*hs):
        for h in hs:
            space = "psum" if "PSum" in type(h).__name__ else "sbuf"
            top = _stacks[space].pop()
            assert top is h, f"non-LIFO free: {getattr(h, 'name', h)} vs {getattr(top, 'name', top)}"
            _cm_of.pop(id(h)).__exit__(None, None, None)

    consts_sb = AL(nc.sbuf_tensor("consts_sb", [128, NCONST], F32))
    e2_sb = AL(nc.sbuf_tensor("e2_sb", [128, 128], F32))
    bias_sb = AL(nc.sbuf_tensor("bias_sb", [128, C], F32))
    bias05 = AL(nc.sbuf_tensor("bias05", [128, C], F32))
    slope4 = AL(nc.sbuf_tensor("slope4_sb", [128, 4], F32))
    onesb = AL(nc.sbuf_tensor("onesb", [128, 1], BF16))
    ones128b = AL(nc.sbuf_tensor("ones128b", [128, 128], BF16))
    c5_sb = AL(nc.sbuf_tensor("c5_sb", [128, 1], F32))
    gate_sb = AL(nc.sbuf_tensor("gate_sb", [128, C * NEXP], BF16))
    lat_sb = AL(nc.sbuf_tensor("lat_sb", [128, C * 128], BF16))
    lbt_sb = AL(nc.sbuf_tensor("lbt_sb", [128, D], BF16))
    xcb = [AL(nc.sbuf_tensor(f"xcb{i}", [128, TC], F32)) for i in range(4)]
    bwb = [AL(nc.sbuf_tensor(f"bwb{i}", [128, C * 128], BF16)) for i in range(2)]
    t1b = [AL(nc.sbuf_tensor(f"t1b{i}", [128, 512], F32)) for i in range(4)]
    x1b = [AL(nc.sbuf_tensor(f"x1b{i}", [128, 512], F32)) for i in range(4)]
    xbs = [AL(nc.sbuf_tensor(f"xbs{i}", [128, 512], BF16)) for i in range(4)]
    xqs = [AL(nc.sbuf_tensor(f"xqs{i}", [128, 512], BF16)) for i in range(4)]
    mean_r = AL(nc.sbuf_tensor("mean_r", [1, TC], F32))
    m2_r = AL(nc.sbuf_tensor("m2_r", [1, TC], F32))
    var_r = AL(nc.sbuf_tensor("var_r", [1, TC], F32))
    rstd_r = AL(nc.sbuf_tensor("rstd_r", [1, TC], F32))
    lg_sb = [AL(nc.sbuf_tensor(f"lg{i}", [128, NEXP], F32)) for i in range(2)]
    m1_sb = AL(nc.sbuf_tensor("m1_sb", [128, 1], F32))
    m2s_sb = AL(nc.sbuf_tensor("m2s_sb", [128, 1], F32))
    eq1_sb = AL(nc.sbuf_tensor("eq1_sb", [128, NEXP], F32))
    eq2_sb = AL(nc.sbuf_tensor("eq2_sb", [128, NEXP], F32))
    msk_sb = AL(nc.sbuf_tensor("msk_sb", [128, NEXP], F32))
    n1_sb = AL(nc.sbuf_tensor("n1_sb", [128, 1], F32))
    n2_sb = AL(nc.sbuf_tensor("n2_sb", [128, 1], F32))
    d21_sb = AL(nc.sbuf_tensor("d21_sb", [128, 1], F32))
    w2_sb = AL(nc.sbuf_tensor("w2_sb", [128, 1], F32))
    w1_sb = AL(nc.sbuf_tensor("w1_sb", [128, 1], F32))
    tie_sb = AL(nc.sbuf_tensor("tie_sb", [128, 1], F32))
    r1_sb = AL(nc.sbuf_tensor("r1_sb", [128, 1], F32))
    r2_sb = AL(nc.sbuf_tensor("r2_sb", [128, 1], F32))
    wfa_sb = AL(nc.sbuf_tensor("wfa_sb", [128, NEXP], F32))
    wf_all = AL(nc.sbuf_tensor("wf_all", [128, NT128 * NEXP], F32))
    wfT_sb = AL(nc.sbuf_tensor("wfT_sb", [8, TC], F32))
    wrep_sb = AL(nc.sbuf_tensor("wrep_sb", [128, TC], F32))
    mid_sb = AL(nc.sbuf_tensor("mid_sb", [128, TC], BF16))
    kc_sb = AL(nc.sbuf_tensor("kc_sb", [128, TC], BF16))
    vc_sb = AL(nc.sbuf_tensor("vc_sb", [128, TC], BF16))

    sm = {n: Sem(nc, n) for n in [
        "dmisc", "dmiscg", "dx0", "dx1", "dx2", "dx3", "dw0", "dw1",
        "dst0", "dst1", "dst2", "dst3", "dag", "dao0", "dao1",
        "dxg", "dwq", "dko", "dqt0", "dqt1", "dq0", "dq1", "cc", "pPE", "pPEv", "cV", "cA", "cPL"]}

    blk_cm = nc.Block()
    block = blk_cm.__enter__()

    bar = Sem(nc, "bar")

    def BAR():
        bar.v += 5
        target = bar.v
        for sec in (block.sync, block.gpsimd, block.tensor, block.vector, block.scalar):
            def f(eng, target=target):
                eng.drain().then_inc(bar.h, 1)
                eng.wait_ge(bar.h, target)
            sec(f)

    def ph0_sync(s):
        s.dma_start(out=consts_sb[:], in_=consts_in).then_inc(sm["dmisc"].h, 16)
        s.dma_start(out=e2_sb[:], in_=e2_in).then_inc(sm["dmisc"].h, 16)
        s.dma_start(out=bias_sb[:], in_=bias_fm).then_inc(sm["dmisc"].h, 16)
        s.dma_start(out=slope4[:], in_=slope4_in).then_inc(sm["dmisc"].h, 16)
    sm["dmisc"].v += 64

    def ph0_gps(g):
        g.dma_start(out=gate_sb[:].rearrange("p (c e) -> p c e", c=C),
                    in_=gate_wT.rearrange("(c p) e -> p c e", p=128)).then_inc(sm["dmiscg"].h, 16)
        g.dma_start(out=lat_sb[:].rearrange("p (c r) -> p c r", c=C),
                    in_=lora_AT.rearrange("(c p) r -> p c r", p=128)).then_inc(sm["dmiscg"].h, 16)
        g.dma_start(out=lbt_sb[:], in_=lora_BT).then_inc(sm["dmiscg"].h, 16)
        g.memset(onesb[:], 1.0)
        g.memset(ones128b[:], 1.0)
        g.memset(c5_sb[:], -5.0)
    sm["dmiscg"].v += 48

    def ph0_vec(v):
        v.wait_ge(sm["dmisc"].h, 64)
        v.tensor_scalar_mul(bias05[:], bias_sb[:], 0.5)

    block.sync(ph0_sync)
    block.gpsimd(ph0_gps)
    block.vector(ph0_vec)
    BAR()

    def ln_stream_stats(src_ap, ps_s1, ps_s2, dst_wait=None):
        dxb = [sm[f"dx{j}"].v for j in range(4)]
        cPLb, cAb, pPEb = sm["cPL"].v, sm["cA"].v, sm["pPE"].v

        def bi_of(c, th):
            return (c % 2) * 2 + th if NTH > 1 else (c % 2)

        def f_sync(s):
            if dst_wait is not None:
                for j in range(4):
                    s.wait_ge(sm[f"dst{j}"].h, dst_wait[j])
            for c in range(C):
                sl = c % 4
                if c >= 4:
                    s.wait_ge(sm["cPL"].h, cPLb + (c - 4) + 1)
                    s.wait_ge(sm["cA"].h, cAb + (c - 4) + 1)
                s.dma_start(out=xcb[sl][:], in_=src_ap[c * 128:(c + 1) * 128, :]).then_inc(sm[f"dx{sl}"].h, 16)

        def f_pool(g):
            for c in range(C):
                sl = c % 4
                g.wait_ge(sm[f"dx{sl}"].h, dxb[sl] + (c // 4 + 1) * 16)
                if c >= 2:
                    g.wait_ge(sm["pPE"].h, pPEb + (c - 2) + 1)
                for th in range(NTH):
                    ins = g.tensor_copy(xbs[bi_of(c, th)][:], xcb[sl][:, th * 512:(th + 1) * 512])
                    if th == NTH - 1:
                        ins.then_inc(sm["cPL"].h, 1)

        def f_scal(sc):
            for c in range(C):
                sl = c % 4
                sc.wait_ge(sm[f"dx{sl}"].h, dxb[sl] + (c // 4 + 1) * 16)
                if c >= 2:
                    sc.wait_ge(sm["pPE"].h, pPEb + (c - 2) + 1)
                for th in range(NTH):
                    ins = sc.activation(xqs[bi_of(c, th)][:], xcb[sl][:, th * 512:(th + 1) * 512], AF.Square)
                    if th == NTH - 1:
                        ins.then_inc(sm["cA"].h, 1)

        def f_pe(t):
            for c in range(C):
                t.wait_ge(sm["cPL"].h, cPLb + c + 1)
                t.wait_ge(sm["cA"].h, cAb + c + 1)
                for th in range(NTH):
                    bi = bi_of(c, th)
                    t.matmul(ps_s1[0:1, th * 512:(th + 1) * 512], onesb[:], xbs[bi][:],
                             start=(c == 0), stop=(c == C - 1))
                    ins = t.matmul(ps_s2[0:1, th * 512:(th + 1) * 512], onesb[:], xqs[bi][:],
                                   start=(c == 0), stop=(c == C - 1))
                ins.then_inc(sm["pPE"].h, 1)

        block.sync(f_sync)
        block.gpsimd(f_pool)
        block.scalar(f_scal)
        block.tensor(f_pe)
        for j in range(4):
            sm[f"dx{j}"].v += 16 * ((C - j + 3) // 4)
        sm["cPL"].v += C
        sm["cA"].v += C
        sm["pPE"].v += C

    def ln_rowmath_bcast(ps_s1, ps_s2, ps_rb, ps_mb):
        pPEb, cVb, cAb = sm["pPE"].v, sm["cV"].v, sm["cA"].v

        def f_vec(v):
            v.wait_ge(sm["pPE"].h, pPEb)
            v.tensor_scalar_mul(mean_r[:], ps_s1[:], 1.0 / D).then_inc(sm["cV"].h, 1)
            v.wait_ge(sm["cA"].h, cAb + 1)
            v.scalar_tensor_tensor(var_r[:], ps_s2[:], 1.0 / D, m2_r[:], ALU.mult, ALU.subtract).then_inc(sm["cV"].h, 1)
            v.wait_ge(sm["cV"].h, cVb + 2)
            v.tensor_scalar_add(var_r[:], var_r[:], EPS).then_inc(sm["cV"].h, 1)
            v.wait_ge(sm["cA"].h, cAb + 3)
            v.tensor_mul(mean_r[:], mean_r[:], rstd_r[:]).then_inc(sm["cV"].h, 1)

        def f_scal(sc):
            sc.wait_ge(sm["pPE"].h, pPEb)
            sc.activation(m2_r[:], ps_s1[:], AF.Square, scale=1.0 / D).then_inc(sm["cA"].h, 1)
            sc.wait_ge(sm["cV"].h, cVb + 3)
            sc.activation(m2_r[:], var_r[:], AF.Ln).then_inc(sm["cA"].h, 1)
            sc.wait_ge(sm["cA"].h, cAb + 2)
            sc.activation(rstd_r[:], m2_r[:], AF.Exp, scale=-0.5).then_inc(sm["cA"].h, 1)

        def f_pe(t):
            t.wait_ge(sm["cV"].h, cVb + 4)
            for th in range(NTH):
                t.matmul(ps_rb[:, th * 512:(th + 1) * 512], consts_sb[0:1, O_ONE:O_ONE + 128],
                         rstd_r[0:1, th * 512:(th + 1) * 512], start=True, stop=True)
                ins = t.matmul(ps_mb[:, th * 512:(th + 1) * 512], consts_sb[0:1, O_ONE:O_ONE + 128],
                               mean_r[0:1, th * 512:(th + 1) * 512], start=True, stop=True)
            ins.then_inc(sm["pPE"].h, 1)

        block.vector(f_vec)
        block.scalar(f_scal)
        block.tensor(f_pe)
        sm["cV"].v += 4
        sm["cA"].v += 3
        sm["pPE"].v += 1

    def ln_apply(src_ap, xnT, ps_rb, ps_mb, also_ag=None, dst_wait=None):
        dxb = [sm[f"dx{j}"].v for j in range(4)]
        cVb, pPEb = sm["cV"].v, sm["pPE"].v

        def f_sync(s):
            if dst_wait is not None:
                for j in range(4):
                    s.wait_ge(sm[f"dst{j}"].h, dst_wait[j])
            for c in range(C):
                sl = c % 4
                if c >= 4:
                    s.wait_ge(sm["cV"].h, cVb + 2 * (c - 4) + 2)
                s.dma_start(out=xcb[sl][:], in_=src_ap[c * 128:(c + 1) * 128, :]).then_inc(sm[f"dx{sl}"].h, 16)

        def f_vec(v):
            v.wait_ge(sm["pPE"].h, pPEb)
            for c in range(C):
                sl = c % 4
                v.wait_ge(sm[f"dx{sl}"].h, dxb[sl] + (c // 4 + 1) * 16)
                v.tensor_mul(xcb[sl][:], xcb[sl][:], ps_rb[:]).then_inc(sm["cV"].h, 1)
                v.wait_ge(sm["cV"].h, cVb + 2 * c + 1)
                v.tensor_sub(xnT[:, c * TC:(c + 1) * TC], xcb[sl][:], ps_mb[:]).then_inc(sm["cV"].h, 1)

        block.sync(f_sync)
        block.vector(f_vec)
        for j in range(4):
            sm[f"dx{j}"].v += 16 * ((C - j + 3) // 4)
        sm["cV"].v += 2 * C

        if also_ag is not None:
            dagb = sm["dag"].v

            def f_gps(g):
                for c in range(C):
                    g.wait_ge(sm["cV"].h, cVb + 2 * c + 2)
                    g.dma_start(out=also_ag[c * 128:(c + 1) * 128, :],
                                in_=xnT[:, c * TC:(c + 1) * TC]).then_inc(sm["dag"].h, 16)
                    if c % 8 == 7:
                        q = c // 8
                        g.wait_ge(sm["dag"].h, dagb + 16 * (c + 1))
                        g.collective_compute(
                            "AllGather", ALU.bypass, replica_groups=[list(range(NCORE))],
                            ins=[also_ag[q * 1024:(q + 1) * 1024, :]],
                            outs=[ag_xo[q]]).then_inc(sm["cc"].h, 1)
            block.gpsimd(f_gps)
            sm["dag"].v += 16 * C
            sm["cc"].v += 4

    _ffn_n = [0]

    def ffn(xnT, resid_ap, out_ap, ps_s1, ps_s2, dst_wait=None):
        tag = _ffn_n[0]
        _ffn_n[0] += 1
        do_stats = ps_s1 is not None
        ps_g = AL(nc.psum_tensor(f"ps_g{tag}", [128, NEXP], F32))
        ps_wfT = AL(nc.psum_tensor(f"ps_wfT{tag}", [8, 128], F32))
        ps_wrep = AL(nc.psum_tensor(f"ps_wrep{tag}", [128, 512], F32))
        ps_low = AL(nc.psum_tensor(f"ps_low{tag}", [128, 512], F32))
        pPEb, cVb, cAb = sm["pPE"].v, sm["cV"].v, sm["cA"].v
        NSTEP = 19

        def f_pe2(t):
            t.wait_ge(sm["dmiscg"].h, 48)
            t.wait_ge(sm["dmisc"].h, 64)
            for tt in range(NT128):
                if tt > 0:
                    t.wait_ge(sm["cV"].h, cVb + NSTEP * (tt - 1) + 1)
                for c in range(C):
                    ins = t.matmul(ps_g[:], xnT[:, c * TC + tt * 128: c * TC + (tt + 1) * 128],
                                   gate_sb[:, c * NEXP:(c + 1) * NEXP],
                                   start=(c == 0), stop=(c == C - 1))
                ins.then_inc(sm["pPE"].h, 1)
            for tt in range(NT128):
                t.wait_ge(sm["cV"].h, cVb + NSTEP * tt + NSTEP)
                if tt > 0:
                    t.wait_ge(sm["cV"].h, cVb + NSTEP * NT128 + tt)
                t.transpose(ps_wfT[:], wf_all[:, tt * NEXP:(tt + 1) * NEXP],
                            consts_sb[:, O_ID:O_ID + 128]).then_inc(sm["pPE"].h, 1)
            for th in range(NTH):
                t.wait_ge(sm["cV"].h, cVb + NSTEP * NT128 + NT128)
                if th > 0:
                    t.wait_ge(sm["cV"].h, cVb + NSTEP * NT128 + NT128 + th)
                t.matmul(ps_wrep[:], e2_sb[0:8, :], wfT_sb[0:8, th * 512:(th + 1) * 512],
                         start=True, stop=True).then_inc(sm["pPE"].h, 1)
            for th in range(NTH):
                t.wait_ge(sm["cV"].h, cVb + NSTEP * NT128 + NT128 + NTH)
                if th > 0:
                    t.wait_ge(sm["cV"].h, cVb + NSTEP * NT128 + NT128 + NTH + th)
                for c in range(C):
                    ins = t.matmul(ps_low[:], lat_sb[:, c * 128:(c + 1) * 128],
                                   xnT[:, c * TC + th * 512: c * TC + th * 512 + 512],
                                   start=(c == 0), stop=(c == C - 1))
                ins.then_inc(sm["pPE"].h, 1)

        def f_vec2(v):
            n = cVb
            for tt in range(NT128):
                sl = tt % 2
                v.wait_ge(sm["pPE"].h, pPEb + tt + 1)

                def step(ins):
                    nonlocal n
                    n += 1
                    ins.then_inc(sm["cV"].h, 1)
                    v.wait_ge(sm["cV"].h, n)
                step(v.tensor_copy(lg_sb[sl][:], ps_g[:]))
                step(v.reduce_max(m1_sb[:], lg_sb[sl][:], axis=AX))
                step(v.tensor_scalar(eq1_sb[:], lg_sb[sl][:], m1_sb[:, 0:1], None, ALU.is_ge))
                step(v.scalar_tensor_tensor(msk_sb[:], eq1_sb[:], -1e30, lg_sb[sl][:], ALU.mult, ALU.add))
                step(v.reduce_sum(n1_sb[:], eq1_sb[:], axis=AX))
                step(v.reduce_max(m2s_sb[:], msk_sb[:], axis=AX))
                step(v.tensor_scalar(eq2_sb[:], lg_sb[sl][:], m2s_sb[:, 0:1], None, ALU.is_equal))
                step(v.tensor_sub(d21_sb[:], m2s_sb[:], m1_sb[:]))
                step(v.reduce_sum(n2_sb[:], eq2_sb[:], axis=AX))
                step(v.tensor_scalar(tie_sb[:], n1_sb[:], 1.5, None, ALU.is_ge))
                step(v.reciprocal(r1_sb[:], n1_sb[:]))
                step(v.reciprocal(r2_sb[:], n2_sb[:]))
                v.wait_ge(sm["cA"].h, cAb + tt + 1)
                step(v.tensor_scalar(w1_sb[:], tie_sb[:], -1.0, 1.0, ALU.mult, ALU.add))
                step(v.tensor_mul(w2_sb[:], w2_sb[:], w1_sb[:]))
                step(v.tensor_scalar(w1_sb[:], w2_sb[:], -1.0, 1.0, ALU.mult, ALU.add))
                step(v.tensor_mul(w1_sb[:], w1_sb[:], r1_sb[:]))
                step(v.tensor_mul(w2_sb[:], w2_sb[:], r2_sb[:]))
                step(v.tensor_scalar(wfa_sb[:], eq1_sb[:], w1_sb[:, 0:1], None, ALU.mult))
                step(v.scalar_tensor_tensor(wf_all[:, tt * NEXP:(tt + 1) * NEXP], eq2_sb[:],
                                            w2_sb[:, 0:1], wfa_sb[:], ALU.mult, ALU.add))
            for tt in range(NT128):
                v.wait_ge(sm["pPE"].h, pPEb + NT128 + tt + 1)
                n += 1
                v.tensor_copy(wfT_sb[0:8, tt * 128:(tt + 1) * 128], ps_wfT[:]).then_inc(sm["cV"].h, 1)
                v.wait_ge(sm["cV"].h, n)
            for th in range(NTH):
                v.wait_ge(sm["pPE"].h, pPEb + 2 * NT128 + th + 1)
                n += 1
                v.tensor_copy(wrep_sb[:, th * 512:(th + 1) * 512], ps_wrep[:]).then_inc(sm["cV"].h, 1)
                v.wait_ge(sm["cV"].h, n)
            for th in range(NTH):
                v.wait_ge(sm["pPE"].h, pPEb + 2 * NT128 + NTH + th + 1)
                n += 1
                v.tensor_mul(mid_sb[:, th * 512:(th + 1) * 512], ps_low[:],
                             wrep_sb[:, th * 512:(th + 1) * 512]).then_inc(sm["cV"].h, 1)
                v.wait_ge(sm["cV"].h, n)

        def f_scal2(sc):
            for tt in range(NT128):
                sc.wait_ge(sm["cV"].h, cVb + tt * NSTEP + 8)
                sc.activation(w2_sb[:], d21_sb[:], AF.Sigmoid).then_inc(sm["cA"].h, 1)

        block.tensor(f_pe2)
        block.vector(f_vec2)
        block.scalar(f_scal2)
        sm["pPE"].v += 2 * NT128 + 2 * NTH
        sm["cV"].v += NSTEP * NT128 + NT128 + 2 * NTH
        sm["cA"].v += NT128
        BAR()
        FR(ps_low, ps_wrep, ps_wfT, ps_g)

        ps_o = [AL(nc.psum_tensor(f"ps_fo{tag}_{i}", [128, 512], F32)) for i in range(4)]
        pPEb, cVb, cAb, cPLb = sm["pPE"].v, sm["cV"].v, sm["cA"].v, sm["cPL"].v
        dwb = [sm["dw0"].v, sm["dw1"].v]
        dxb = [sm["dx0"].v, sm["dx1"].v]
        dstb = [sm[f"dst{j}"].v for j in range(4)]
        NIT = C * NTH
        CAK = 2 if do_stats else 1

        def it(oc, th):
            return oc * NTH + th

        def pool_copies(g, po):
            for th in range(NTH):
                k = it(po, th)
                b4 = k % 4
                g.wait_ge(sm["cV"].h, cVb + k + 1)
                if k >= 4:
                    g.wait_ge(sm["pPE"].h, pPEb + k + 1)
                g.tensor_copy(xbs[b4][:], x1b[b4][:]).then_inc(sm["cPL"].h, 1)

        def f_gps_w(g):
            for oc in range(C):
                sl = oc % 2
                if oc >= 2:
                    g.wait_ge(sm["pPE"].h, pPEb + it(oc - 2, NTH - 1) + 1)
                g.dma_start(
                    out=bwb[sl][:].rearrange("p (c o) -> p c o", c=C),
                    in_=basewT[:, oc * 128:(oc + 1) * 128].rearrange("(c p) o -> p c o", p=128),
                ).then_inc(sm[f"dw{sl}"].h, 16)
                if do_stats and oc >= 1:
                    pool_copies(g, oc - 1)
            if do_stats:
                pool_copies(g, C - 1)

        def f_sync(s):
            if dst_wait is not None:
                for j in range(4):
                    s.wait_ge(sm[f"dst{j}"].h, dst_wait[j])
            for oc in range(C):
                sl = oc % 2
                if oc >= 2:
                    s.wait_ge(sm["cV"].h, cVb + it(oc - 2, NTH - 1) + 1)
                s.dma_start(out=xcb[sl][:], in_=resid_ap[oc * 128:(oc + 1) * 128, :]).then_inc(sm[f"dx{sl}"].h, 16)
                for th in range(NTH):
                    b4 = it(oc, th) % 4
                    s.wait_ge(sm["cV"].h, cVb + it(oc, th) + 1)
                    s.dma_start(out=out_ap[oc * 128:(oc + 1) * 128, th * 512:(th + 1) * 512],
                                in_=x1b[b4][:]).then_inc(sm[f"dst{b4}"].h, 16)

        def f_pe(t):
            for oc in range(C):
                sl = oc % 2
                t.wait_ge(sm[f"dw{sl}"].h, dwb[sl] + (oc // 2 + 1) * 16)
                for th in range(NTH):
                    k = it(oc, th)
                    if k >= 4:
                        t.wait_ge(sm["cA"].h, cAb + CAK * (k - 4) + 1)
                for c in range(C):
                    for th in range(NTH):
                        b4 = it(oc, th) % 4
                        t.matmul(ps_o[b4][:], bwb[sl][:, c * 128:(c + 1) * 128],
                                 xnT[:, c * TC + th * 512: c * TC + th * 512 + 512],
                                 start=(c == 0), stop=False)
                for th in range(NTH):
                    b4 = it(oc, th) % 4
                    t.matmul(ps_o[b4][:], lbt_sb[:, oc * 128:(oc + 1) * 128],
                             mid_sb[:, th * 512:(th + 1) * 512],
                             start=False, stop=True).then_inc(sm["pPE"].h, 1)
                if do_stats and oc >= 1:
                    po = oc - 1
                    for th in range(NTH):
                        b4 = it(po, th) % 4
                        t.wait_ge(sm["cPL"].h, cPLb + it(po, th) + 1)
                        t.wait_ge(sm["cA"].h, cAb + CAK * it(po, th) + 2)
                        t.matmul(ps_s1[0:1, th * 512:(th + 1) * 512], onesb[:], xbs[b4][:],
                                 start=(po == 0), stop=(po == C - 1))
                        t.matmul(ps_s2[0:1, th * 512:(th + 1) * 512], onesb[:], xqs[b4][:],
                                 start=(po == 0), stop=(po == C - 1))
            if do_stats:
                po = C - 1
                for th in range(NTH):
                    b4 = it(po, th) % 4
                    t.wait_ge(sm["cPL"].h, cPLb + it(po, th) + 1)
                    t.wait_ge(sm["cA"].h, cAb + CAK * it(po, th) + 2)
                    t.matmul(ps_s1[0:1, th * 512:(th + 1) * 512], onesb[:], xbs[b4][:],
                             start=(po == 0), stop=(po == C - 1))
                    ins = t.matmul(ps_s2[0:1, th * 512:(th + 1) * 512], onesb[:], xqs[b4][:],
                                   start=(po == 0), stop=(po == C - 1))
                ins.then_inc(sm["pPE"].h, 1)

        def f_scal(sc):
            for oc in range(C):
                for th in range(NTH):
                    k = it(oc, th)
                    b4 = k % 4
                    sc.wait_ge(sm["pPE"].h, pPEb + k + 1)
                    if k >= 4:
                        sc.wait_ge(sm["cV"].h, cVb + (k - 4) + 1)
                    sc.activation(t1b[b4][:], ps_o[b4][:], AF.Identity,
                                  bias=bias05[:, oc:oc + 1], scale=0.5).then_inc(sm["cA"].h, 1)
                    if do_stats:
                        sc.wait_ge(sm["cV"].h, cVb + k + 1)
                        sc.activation(xqs[b4][:], x1b[b4][:], AF.Square).then_inc(sm["cA"].h, 1)

        def f_vec(v):
            for oc in range(C):
                sl = oc % 2
                v.wait_ge(sm[f"dx{sl}"].h, dxb[sl] + (oc // 2 + 1) * 16)
                for th in range(NTH):
                    k = it(oc, th)
                    b4 = k % 4
                    v.wait_ge(sm["cA"].h, cAb + CAK * k + 1)
                    if k >= 4:
                        v.wait_ge(sm[f"dst{b4}"].h, dstb[b4] + ((k - 4) // 4 + 1) * 16)
                        if do_stats:
                            v.wait_ge(sm["cA"].h, cAb + CAK * (k - 4) + 2)
                            v.wait_ge(sm["cPL"].h, cPLb + (k - 4) + 1)
                    v.tensor_add(x1b[b4][:], t1b[b4][:],
                                 xcb[sl][:, th * 512:(th + 1) * 512]).then_inc(sm["cV"].h, 1)

        block.gpsimd(f_gps_w)
        block.sync(f_sync)
        block.tensor(f_pe)
        block.scalar(f_scal)
        block.vector(f_vec)

        sm["dw0"].v += 16 * ((C + 1) // 2)
        sm["dw1"].v += 16 * (C // 2)
        sm["dx0"].v += 16 * ((C + 1) // 2)
        sm["dx1"].v += 16 * (C // 2)
        for j in range(4):
            sm[f"dst{j}"].v += 16 * ((NIT - j + 3) // 4)
        sm["pPE"].v += NIT + (1 if do_stats else 0)
        sm["cA"].v += NIT * CAK
        sm["cV"].v += NIT
        if do_stats:
            sm["cPL"].v += NIT
        BAR()
        FR(*ps_o[::-1])

    ps_rb1 = AL(nc.psum_tensor("ps_rb1", [128, TC], F32))
    ps_mb1 = AL(nc.psum_tensor("ps_mb1", [128, TC], F32))
    ps_s1a = AL(nc.psum_tensor("ps_s1a", [1, TC], F32))
    ps_s2a = AL(nc.psum_tensor("ps_s2a", [1, TC], F32))
    ln_stream_stats(x_f, ps_s1a, ps_s2a)
    ln_rowmath_bcast(ps_s1a, ps_s2a, ps_rb1, ps_mb1)
    BAR()
    FR(ps_s2a, ps_s1a)
    xn1T = AL(nc.sbuf_tensor("xn1T", [128, C * TC], BF16))
    ln_apply(x_f, xn1T, ps_rb1, ps_mb1)
    BAR()

    FR(ps_mb1, ps_rb1)
    ps_s1b = AL(nc.psum_tensor("ps_s1b", [1, TC], F32))
    ps_s2b = AL(nc.psum_tensor("ps_s2b", [1, TC], F32))
    ffn(xn1T, x_f, xs1, ps_s1b, ps_s2b)
    dst_after_ffn1 = {j: sm[f"dst{j}"].v for j in range(4)}
    ps_rb2 = AL(nc.psum_tensor("ps_rb2", [128, TC], F32))
    ps_mb2 = AL(nc.psum_tensor("ps_mb2", [128, TC], F32))
    ln_rowmath_bcast(ps_s1b, ps_s2b, ps_rb2, ps_mb2)
    BAR()

    FR(xn1T)
    wq_sb = AL(nc.sbuf_tensor("wq_sb", [128, C * 512], BF16))
    def ph4_wq(g):
        g.dma_start(out=wq_sb[:].rearrange("p (c e) -> p c e", c=C),
                    in_=wq_in.rearrange("(c p) e -> p c e", p=128)).then_inc(sm["dwq"].h, 16)
    block.gpsimd(ph4_wq)
    sm["dwq"].v += 16
    xn2T = AL(nc.sbuf_tensor("xn2T", [128, C * TC], BF16))
    ln_apply(xs1, xn2T, ps_rb2, ps_mb2, also_ag=ag_x_in, dst_wait=dst_after_ffn1)
    dag_after_x = sm["dag"].v
    BAR()
    FR(ps_mb2, ps_rb2, ps_s2b, ps_s1b)

    ps_k = AL(nc.psum_tensor("ps_k", [128, 512], F32))
    ps_v = AL(nc.psum_tensor("ps_v", [128, 128], F32))
    pPEb, cVb = sm["pPE"].v, sm["cV"].v
    dmgb = sm["dmiscg"].v

    def ph4_gps(g):
        g.dma_start(out=bwb[0][:].rearrange("p (c e) -> p c e", c=C),
                    in_=wkv_in[:, 0:128].rearrange("(c p) e -> p c e", p=128)).then_inc(sm["dmiscg"].h, 16)
        g.dma_start(out=bwb[1][:].rearrange("p (c e) -> p c e", c=C),
                    in_=wkv_in[:, 128:256].rearrange("(c p) e -> p c e", p=128)).then_inc(sm["dmiscg"].h, 16)

    def ph4_pe(t):
        t.wait_ge(sm["dmiscg"].h, dmgb + 32)
        for th in range(NTH):
            if th > 0:
                t.wait_ge(sm["cV"].h, cVb + th)
            for c in range(C):
                ins = t.matmul(ps_k[:], bwb[0][:, c * 128:(c + 1) * 128],
                               xn2T[:, c * TC + th * 512: c * TC + th * 512 + 512],
                               start=(c == 0), stop=(c == C - 1))
            ins.then_inc(sm["pPE"].h, 1)
        for tt in range(NT128):
            if tt > 0:
                t.wait_ge(sm["cV"].h, cVb + NTH + tt)
            for c in range(C):
                ins = t.matmul(ps_v[:], xn2T[:, c * TC + tt * 128: c * TC + (tt + 1) * 128],
                               bwb[1][:, c * 128:(c + 1) * 128],
                               start=(c == 0), stop=(c == C - 1))
            ins.then_inc(sm["pPE"].h, 1)

    def ph4_vec(v):
        for th in range(NTH):
            v.wait_ge(sm["pPE"].h, pPEb + th + 1)
            v.tensor_copy(kc_sb[:, th * 512:(th + 1) * 512], ps_k[:]).then_inc(sm["cV"].h, 1)
        for tt in range(NT128):
            v.wait_ge(sm["pPE"].h, pPEb + NTH + tt + 1)
            v.tensor_copy(vc_sb[:, tt * 128:(tt + 1) * 128], ps_v[:]).then_inc(sm["cV"].h, 1)

    def ph4_gps2(g):
        g.wait_ge(sm["cV"].h, cVb + NTH)
        g.dma_start(out=ag_k_in, in_=kc_sb[:]).then_inc(sm["dag"].h, 16)
        g.wait_ge(sm["cV"].h, cVb + NTH + NT128)
        g.dma_start(out=agv_in.rearrange("(g p) d -> p g d", p=128),
                    in_=vc_sb[:].rearrange("p (g d) -> p g d", d=128)).then_inc(sm["dag"].h, 16)
        g.wait_ge(sm["dag"].h, dag_after_x + 32)
        g.collective_compute("AllGather", ALU.bypass, replica_groups=[list(range(NCORE))],
                             ins=[ag_k_in], outs=[ag_k_out]).then_inc(sm["cc"].h, 1)
        g.collective_compute("AllGather", ALU.bypass, replica_groups=[list(range(NCORE))],
                             ins=[agv_in], outs=[agv_out]).then_inc(sm["cc"].h, 1)

    block.gpsimd(ph4_gps)
    sm["dmiscg"].v += 32
    block.tensor(ph4_pe)
    block.vector(ph4_vec)
    block.gpsimd(ph4_gps2)
    sm["pPE"].v += NTH + NT128
    sm["cV"].v += NTH + NT128
    sm["dag"].v += 32
    sm["cc"].v += 2
    dag_kv_done = sm["dag"].v
    cc_x0, cc_k, cc_v = sm["cc"].v - 6, sm["cc"].v - 1, sm["cc"].v
    BAR()
    FR(ps_v, ps_k)

    FR(xn2T)
    xgc = [AL(nc.sbuf_tensor(f"xgc{i}", [128, TC], BF16)) for i in range(4)]
    qt_sb = [AL(nc.sbuf_tensor(f"qt{i}", [128, 512], BF16)) for i in range(2)]
    ps_q = [AL(nc.psum_tensor(f"ps_q{i}", [128, 512], F32)) for i in range(4 * NTH)]
    pPEb, cAb = sm["pPE"].v, sm["cA"].v
    dxb = [sm[f"dx{j}"].v for j in range(4)]
    dqb = [sm["dq0"].v, sm["dq1"].v]
    dmgb = sm["dmiscg"].v
    NPQ = 4 * NTH

    def ph5_gps(g):
        for r in range(NCORE):
            for k in range(NPQ):
                hh, th = k // NTH, k % NTH
                kk = r * NPQ + k
                g.wait_ge(sm["cA"].h, cAb + kk + 1)
                g.dma_start(out=qdram[hh * 128:(hh + 1) * 128,
                                      r * TC + th * 512: r * TC + th * 512 + 512],
                            in_=qt_sb[kk % 2][:]).then_inc(sm[f"dq{kk % 2}"].h, 16)

    def ph5_sync(s):
        for r in range(NCORE):
            for c in range(C):
                idx = r * C + c
                sl = idx % 4
                if r == 0:
                    s.wait_ge(sm["cc"].h, cc_x0 + c // 8 + 1)
                if idx >= 4:
                    s.wait_ge(sm["pPE"].h, pPEb + (idx - 4) + 1)
                q, cq = c // 8, c % 8
                s.dma_start(out=xgc[sl][:],
                            in_=ag_xo[q][r * 1024 + cq * 128: r * 1024 + (cq + 1) * 128, :]).then_inc(sm[f"dx{sl}"].h, 16)

    def ph5_pe(t):
        t.wait_ge(sm["dwq"].h, 16)
        for r in range(NCORE):
            for c in range(C):
                idx = r * C + c
                sl = idx % 2
                sl4 = idx % 4
                t.wait_ge(sm[f"dx{sl4}"].h, dxb[sl4] + (idx // 4 + 1) * 16)
                for k in range(NPQ):
                    if c == 0 and r > 0:
                        t.wait_ge(sm["cA"].h, cAb + (r - 1) * NPQ + k + 1)
                    hh, th = k // NTH, k % NTH
                    ins = t.matmul(ps_q[k][:], wq_sb[:, c * 512 + hh * 128: c * 512 + (hh + 1) * 128],
                                   xgc[sl4][:, th * 512:(th + 1) * 512],
                                   start=(c == 0), stop=(c == C - 1))
                ins.then_inc(sm["pPE"].h, 1)

    def ph5_scal(sc):
        for r in range(NCORE):
            sc.wait_ge(sm["pPE"].h, pPEb + r * C + C)
            for k in range(NPQ):
                kk = r * NPQ + k
                if kk >= 2:
                    sc.wait_ge(sm[f"dq{kk % 2}"].h, dqb[kk % 2] + ((kk - 2) // 2 + 1) * 16)
                sc.activation(qt_sb[kk % 2][:], ps_q[k][:], AF.Copy,
                              scale=QSCALE).then_inc(sm["cA"].h, 1)

    block.gpsimd(ph5_gps)
    block.sync(ph5_sync)
    block.tensor(ph5_pe)
    block.scalar(ph5_scal)
    sm["dq0"].v += 16 * ((NCORE * NPQ + 1) // 2)
    sm["dq1"].v += 16 * ((NCORE * NPQ) // 2)
    for j in range(4):
        sm[f"dx{j}"].v += 16 * ((NCORE * C - j + 3) // 4)
    sm["pPE"].v += NCORE * C
    sm["cA"].v += NCORE * NPQ
    dq_total = (sm["dq0"].v, sm["dq1"].v)
    BAR()
    FR(*ps_q[::-1])
    FR(qt_sb[1], qt_sb[0], xgc[3], xgc[2], xgc[1], xgc[0], wq_sb)

    kf_sb = AL(nc.sbuf_tensor("kf_sb", [128, S], BF16))
    vt_sb = AL(nc.sbuf_tensor("vt_sb", [128, S], BF16))
    dm_sb = AL(nc.sbuf_tensor("dm_sb", [128, 2048], F32))
    dm2_sb = AL(nc.sbuf_tensor("dm2_sb", [128, 2048], F32))
    sD0 = AL(nc.sbuf_tensor("sD0", [128, 512], F32))
    sDm = AL(nc.sbuf_tensor("sDm", [128, 2048], F32))
    sDm2 = AL(nc.sbuf_tensor("sDm2", [128, 2048], F32))
    dT_sb = AL(nc.sbuf_tensor("dT_sb", [128, 72], F32))
    bvt_sb = AL(nc.sbuf_tensor("bvt_sb", [128, 72], F32))
    prob_sb = [AL(nc.sbuf_tensor(f"prob{i}", [128, 512], BF16)) for i in range(4)]
    qt6 = [AL(nc.sbuf_tensor(f"qt6_{i}", [128, 512], BF16)) for i in range(2)]
    zb_sb = [AL(nc.sbuf_tensor(f"zb{i}", [128, 512], F32)) for i in range(2)]
    lnz_sb = [AL(nc.sbuf_tensor(f"lnz{i}", [128, 512], F32)) for i in range(2)]
    ao_sb = [AL(nc.sbuf_tensor(f"ao{i}", [128, 512], BF16)) for i in range(2)]
    ps_sc = [AL(nc.psum_tensor(f"ps_sc{i}", [128, 512], F32)) for i in range(4)]
    ps_ao = [AL(nc.psum_tensor(f"ps_ao{i}", [128, 512], F32)) for i in range(2)]
    ps_z = [AL(nc.psum_tensor(f"ps_z{i}", [128, 512], F32)) for i in range(2)]

    sched = []
    for hh in (3, 2, 1, 0):
        for qt in range(NQT):
            sched.append((hh, qt, list(range(kt_lo(hh, qt), 4 * qt + 4))))
    NG = len(sched)
    NSC = [len(k) for (_, _, k) in sched]
    CSC = [0]
    for nsc in NSC:
        CSC.append(CSC[-1] + nsc)

    pPEb, pPEvb = sm["pPE"].v, sm["pPEv"].v
    cVb, cAb = sm["cV"].v, sm["cA"].v
    daob = [sm["dao0"].v, sm["dao1"].v]
    dkob = sm["dko"].v
    dqtb = [sm["dqt0"].v, sm["dqt1"].v]

    def is_phi(hh, m):
        return hh >= 2 and m >= 1

    cv_at, ca_at, pev_at, cv_after_group, hh_prep_at, ca_z = {}, {}, {}, {}, {}, {}
    cv_run, ca_run, pev_run = cVb, cAb, pPEvb
    for g, (hh, qt, kts) in enumerate(sched):
        if qt == 0:
            cv_run += 5
            hh_prep_at[hh] = cv_run
        for j, kt in enumerate(kts):
            m = 4 * qt - kt
            if not is_phi(hh, m):
                cv_run += 1
                cv_at[(g, j)] = cv_run
            ca_run += 1
            ca_at[(g, j)] = ca_run
        pev_run += len(kts)
        pev_at[g] = pev_run
        ca_run += 2
        ca_z[g] = ca_run
        cv_run += 1
        cv_after_group[g] = cv_run

    def ph6_sync(s):
        s.wait_ge(sm["dq0"].h, dq_total[0])
        s.wait_ge(sm["dq1"].h, dq_total[1])
        s.dma_start(out=dm_sb[:], in_=dm_in).then_inc(sm["dmisc"].h, 16)
        s.dma_start(out=dm2_sb[:], in_=dm2_in).then_inc(sm["dmisc"].h, 16)
        s.wait_ge(sm["cc"].h, cc_k)
        for r in range(NCORE):
            s.dma_start(out=kf_sb[:, r * TC:(r + 1) * TC],
                        in_=ag_k_out[r * 128:(r + 1) * 128, :]).then_inc(sm["dko"].h, 16)
        s.wait_ge(sm["cc"].h, cc_v)
        s.dma_start(out=vt_sb[:].rearrange("p (g d) -> p g d", d=128),
                    in_=agv_out.rearrange("(g p) d -> p g d", p=128)).then_inc(sm["dko"].h, 16)
        for g, (hh, qt, kts) in enumerate(sched):
            if g >= 2:
                s.wait_ge(sm["pPE"].h, pPEb + CSC[g - 1])
            s.dma_start(out=qt6[g % 2][:],
                        in_=qdram[hh * 128:(hh + 1) * 128, qt * 512:(qt + 1) * 512]).then_inc(sm[f"dqt{g % 2}"].h, 16)

    def ph6_vec(v):
        v.wait_ge(sm["dmisc"].h, 96)
        v.wait_ge(sm["dq0"].h, dq_total[0])
        v.wait_ge(sm["dq1"].h, dq_total[1])
        for g, (hh, qt, kts) in enumerate(sched):
            if qt == 0:
                if g > 0:
                    pg = g - 1
                    v.wait_ge(sm["cA"].h, ca_at[(pg, len(sched[pg][2]) - 1)])
                v.tensor_scalar(sD0[:], consts_sb[:, O_D0:O_D0 + 512], slope4[:, hh:hh + 1], None, ALU.mult).then_inc(sm["cV"].h, 1)
                v.tensor_scalar(sDm[:], dm_sb[:], slope4[:, hh:hh + 1], None, ALU.mult).then_inc(sm["cV"].h, 1)
                v.tensor_scalar(sDm2[:], dm2_sb[:], slope4[:, hh:hh + 1], None, ALU.mult).then_inc(sm["cV"].h, 1)
                v.tensor_scalar(dT_sb[:], consts_sb[:, O_DB:O_DB + 72], slope4[:, hh:hh + 1], None, ALU.mult).then_inc(sm["cV"].h, 1)
                v.tensor_scalar(bvt_sb[:], consts_sb[:, O_CB:O_CB + 72], slope4[:, hh:hh + 1], -5.0, ALU.mult, ALU.add).then_inc(sm["cV"].h, 1)
                v.wait_ge(sm["cV"].h, hh_prep_at[hh])
            for j, kt in enumerate(kts):
                sl = j % 4
                m = 4 * qt - kt
                if is_phi(hh, m):
                    continue
                v.wait_ge(sm["pPE"].h, pPEb + CSC[g] + j + 1)
                if m <= 0:
                    dmt = sDm2 if hh >= 2 else sDm
                    ins = v.tensor_add(ps_sc[sl][:], ps_sc[sl][:], dmt[:, (-m) * 512:(-m + 1) * 512])
                else:
                    ins = v.scalar_tensor_tensor(ps_sc[sl][:], ps_sc[sl][:], dT_sb[:, m:m + 1],
                                                 sD0[:], ALU.add, ALU.add)
                ins.then_inc(sm["cV"].h, 1)
            sl2 = g % 2
            v.wait_ge(sm["pPEv"].h, pev_at[g])
            v.wait_ge(sm["cA"].h, ca_z[g])
            if g >= 2:
                v.wait_ge(sm[f"dao{sl2}"].h, daob[sl2] + ((g - 2) // 2 + 1) * 16)
            v.tensor_mul(ao_sb[sl2][:], ps_ao[sl2][:], zb_sb[sl2][:]).then_inc(sm["cV"].h, 1)

    def ph6_scal(sc):
        for g, (hh, qt, kts) in enumerate(sched):
            for j, kt in enumerate(kts):
                sl = j % 4
                m = 4 * qt - kt
                if is_phi(hh, m):
                    sc.wait_ge(sm["pPE"].h, pPEb + CSC[g] + j + 1)
                    sc.wait_ge(sm["cV"].h, hh_prep_at[hh])
                    bias_ap = bvt_sb[:, m:m + 1]
                else:
                    sc.wait_ge(sm["cV"].h, cv_at[(g, j)])
                    bias_ap = c5_sb[:, 0:1]
                if j >= 4:
                    sc.wait_ge(sm["pPEv"].h, pPEvb + CSC[g] + (j - 4) + 1)
                elif g >= 1:
                    sc.wait_ge(sm["pPEv"].h, pev_at[g - 1])
                sc.activation(prob_sb[sl][:], ps_sc[sl][:], AF.Exp,
                              bias=bias_ap).then_inc(sm["cA"].h, 1)
            sl2 = g % 2
            sc.wait_ge(sm["pPEv"].h, pev_at[g])
            if g >= 2:
                sc.wait_ge(sm["cV"].h, cv_after_group[g - 2])
            sc.activation(lnz_sb[sl2][:], ps_z[sl2][:], AF.Ln).then_inc(sm["cA"].h, 1)
            sc.wait_ge(sm["cA"].h, ca_z[g] - 1)
            sc.activation(zb_sb[sl2][:], lnz_sb[sl2][:], AF.Exp, scale=-1.0).then_inc(sm["cA"].h, 1)

    def ph6_pe(t):
        t.wait_ge(sm["dko"].h, dkob + 16 * (NCORE + 1))
        for g, (hh, qt, kts) in enumerate(sched):
            sl2 = g % 2
            nk = len(kts)
            t.wait_ge(sm[f"dqt{g % 2}"].h, dqtb[g % 2] + (g // 2 + 1) * 16)
            LV = 2
            for j in range(nk + LV):
                if j < nk:
                    kt = kts[j]
                    sl = j % 4
                    if j >= 4:
                        t.wait_ge(sm["cA"].h, ca_at[(g, j - 4)])
                    elif g >= 1:
                        pg = g - 1
                        t.wait_ge(sm["cA"].h, ca_at[(pg, len(sched[pg][2]) - 1)])
                    t.matmul(ps_sc[sl][:], kf_sb[:, kt * 128:(kt + 1) * 128],
                             qt6[g % 2][:], start=True, stop=True).then_inc(sm["pPE"].h, 1)
                jj = j - LV
                if jj >= 0:
                    kt = kts[jj]
                    sl = jj % 4
                    t.wait_ge(sm["cA"].h, ca_at[(g, jj)])
                    if jj == 0 and g >= 2:
                        t.wait_ge(sm["cV"].h, cv_after_group[g - 2])
                        t.wait_ge(sm["cA"].h, ca_z[g - 2])
                    t.matmul(ps_ao[sl2][:], vt_sb[:, kt * 128:(kt + 1) * 128], prob_sb[sl][:],
                             start=(jj == 0), stop=(jj == nk - 1))
                    t.matmul(ps_z[sl2][:], ones128b[:], prob_sb[sl][:],
                             start=(jj == 0), stop=(jj == nk - 1)).then_inc(sm["pPEv"].h, 1)

    def ph6_gps(g_):
        for g, (hh, qt, kts) in enumerate(sched):
            sl2 = g % 2
            g_.wait_ge(sm["cV"].h, cv_after_group[g])
            g_.dma_start(out=ag_o_in[hh * 128:(hh + 1) * 128, qt * 512:(qt + 1) * 512],
                         in_=ao_sb[sl2][:]).then_inc(sm[f"dao{sl2}"].h, 16)
            if qt == NQT - 1:
                ng = g + 1
                g_.wait_ge(sm["dao0"].h, daob[0] + 16 * ((ng + 1) // 2))
                g_.wait_ge(sm["dao1"].h, daob[1] + 16 * (ng // 2))
                g_.collective_compute(
                    "AllGather", ALU.bypass, replica_groups=[list(range(NCORE))],
                    ins=[ag_o_in[hh * 128:(hh + 1) * 128, :]],
                    outs=[ag_oo[hh]]).then_inc(sm["cc"].h, 1)

    block.sync(ph6_sync)
    block.tensor(ph6_pe)
    block.vector(ph6_vec)
    block.scalar(ph6_scal)
    block.gpsimd(ph6_gps)
    sm["dko"].v += 16 * (NCORE + 1)
    sm["dmisc"].v += 32
    sm["dqt0"].v += 16 * ((NG + 1) // 2)
    sm["dqt1"].v += 16 * (NG // 2)
    sm["pPE"].v += CSC[-1]
    sm["pPEv"].v = pev_run
    sm["cV"].v = cv_run
    sm["cA"].v = ca_run
    sm["dao0"].v += 16 * ((NG + 1) // 2)
    sm["dao1"].v += 16 * (NG // 2)
    sm["cc"].v += 4
    cc_o = sm["cc"].v
    BAR()
    FR(*ps_z[::-1]); FR(*ps_ao[::-1]); FR(*ps_sc[::-1])
    FR(ao_sb[1], ao_sb[0], lnz_sb[1], lnz_sb[0], zb_sb[1], zb_sb[0],
       qt6[1], qt6[0], prob_sb[3], prob_sb[2], prob_sb[1], prob_sb[0],
       bvt_sb, dT_sb, sDm2, sDm, sD0, dm2_sb, dm_sb, vt_sb, kf_sb)

    attn_sb = AL(nc.sbuf_tensor("attn_sb", [128, C * TC], BF16))
    dxgb7 = sm["dxg"].v

    def ph7_sync_pre(s):
        pid = s.partition_id()
        for i, hh in enumerate((3, 2, 1, 0)):
            s.wait_ge(sm["cc"].h, cc_o - 3 + i)
            out_v = attn_sb[:].rearrange("p (r h t) -> p r h t", h=4, t=TC)[:, :, hh, :]
            in_v = ag_oo[hh][:, bass.ts(pid, TC)].rearrange("(r p) t -> p r t", p=128)
            s.dma_start(out=out_v, in_=in_v).then_inc(sm["dxg"].h, 16)
    block.sync(ph7_sync_pre)
    sm["dxg"].v += 64

    ps_s1c = AL(nc.psum_tensor("ps_s1c", [1, TC], F32))
    ps_s2c = AL(nc.psum_tensor("ps_s2c", [1, TC], F32))
    ps_o7 = [AL(nc.psum_tensor(f"ps_wo{i}", [128, 512], F32)) for i in range(4)]
    pPEb, cVb, cAb, cPLb = sm["pPE"].v, sm["cV"].v, sm["cA"].v, sm["cPL"].v
    dwb = [sm["dw0"].v, sm["dw1"].v]
    dxb = [sm["dx0"].v, sm["dx1"].v]
    dstb = [sm[f"dst{j}"].v for j in range(4)]
    NIT = C * NTH

    def it7(oc, th):
        return oc * NTH + th

    def ph7_pool_copies(g, po):
        for th in range(NTH):
            k = it7(po, th)
            b4 = k % 4
            g.wait_ge(sm["cV"].h, cVb + k + 1)
            if k >= 4:
                g.wait_ge(sm["pPE"].h, pPEb + k + 1)
            g.tensor_copy(xbs[b4][:], x1b[b4][:]).then_inc(sm["cPL"].h, 1)

    def ph7_gps_w(g):
        for oc in range(C):
            sl = oc % 2
            if oc >= 2:
                g.wait_ge(sm["pPE"].h, pPEb + it7(oc - 2, NTH - 1) + 1)
            g.dma_start(
                out=bwb[sl][:].rearrange("p (c o) -> p c o", c=C),
                in_=wc_p[:, oc * 128:(oc + 1) * 128].rearrange("(c p) o -> p c o", p=128),
            ).then_inc(sm[f"dw{sl}"].h, 16)
            if oc >= 1:
                ph7_pool_copies(g, oc - 1)
        ph7_pool_copies(g, C - 1)

    def ph7_sync(s):
        for j in range(4):
            s.wait_ge(sm[f"dst{j}"].h, dst_after_ffn1[j])
        for oc in range(C):
            sl = oc % 2
            if oc >= 2:
                s.wait_ge(sm["cV"].h, cVb + it7(oc - 2, NTH - 1) + 1)
            s.dma_start(out=xcb[sl][:], in_=xs1[oc * 128:(oc + 1) * 128, :]).then_inc(sm[f"dx{sl}"].h, 16)
            for th in range(NTH):
                b4 = it7(oc, th) % 4
                s.wait_ge(sm["cV"].h, cVb + it7(oc, th) + 1)
                s.dma_start(out=xs2[oc * 128:(oc + 1) * 128, th * 512:(th + 1) * 512],
                            in_=x1b[b4][:]).then_inc(sm[f"dst{b4}"].h, 16)

    def ph7_pe(t):
        t.wait_ge(sm["dxg"].h, dxgb7 + 64)
        for oc in range(C):
            sl = oc % 2
            t.wait_ge(sm[f"dw{sl}"].h, dwb[sl] + (oc // 2 + 1) * 16)
            for th in range(NTH):
                k = it7(oc, th)
                if k >= 4:
                    t.wait_ge(sm["cV"].h, cVb + (k - 4) + 1)
            for c in range(C):
                for th in range(NTH):
                    b4 = it7(oc, th) % 4
                    ins = t.matmul(ps_o7[b4][:], bwb[sl][:, c * 128:(c + 1) * 128],
                                   attn_sb[:, c * TC + th * 512: c * TC + th * 512 + 512],
                                   start=(c == 0), stop=(c == C - 1))
                    if c == C - 1:
                        ins.then_inc(sm["pPE"].h, 1)
            if oc >= 1:
                po = oc - 1
                for th in range(NTH):
                    b4 = it7(po, th) % 4
                    t.wait_ge(sm["cPL"].h, cPLb + it7(po, th) + 1)
                    t.wait_ge(sm["cA"].h, cAb + it7(po, th) + 1)
                    t.matmul(ps_s1c[0:1, th * 512:(th + 1) * 512], onesb[:], xbs[b4][:],
                             start=(po == 0), stop=(po == C - 1))
                    t.matmul(ps_s2c[0:1, th * 512:(th + 1) * 512], onesb[:], xqs[b4][:],
                             start=(po == 0), stop=(po == C - 1))
        po = C - 1
        for th in range(NTH):
            b4 = it7(po, th) % 4
            t.wait_ge(sm["cPL"].h, cPLb + it7(po, th) + 1)
            t.wait_ge(sm["cA"].h, cAb + it7(po, th) + 1)
            t.matmul(ps_s1c[0:1, th * 512:(th + 1) * 512], onesb[:], xbs[b4][:],
                     start=(po == 0), stop=(po == C - 1))
            ins = t.matmul(ps_s2c[0:1, th * 512:(th + 1) * 512], onesb[:], xqs[b4][:],
                           start=(po == 0), stop=(po == C - 1))
        ins.then_inc(sm["pPE"].h, 1)

    def ph7_vec(v):
        for oc in range(C):
            sl = oc % 2
            v.wait_ge(sm[f"dx{sl}"].h, dxb[sl] + (oc // 2 + 1) * 16)
            for th in range(NTH):
                k = it7(oc, th)
                b4 = k % 4
                v.wait_ge(sm["pPE"].h, pPEb + k + 1)
                if k >= 4:
                    v.wait_ge(sm[f"dst{b4}"].h, dstb[b4] + ((k - 4) // 4 + 1) * 16)
                    v.wait_ge(sm["cPL"].h, cPLb + (k - 4) + 1)
                    v.wait_ge(sm["cA"].h, cAb + (k - 4) + 1)
                v.scalar_tensor_tensor(x1b[b4][:], ps_o7[b4][:], 1.0,
                                       xcb[sl][:, th * 512:(th + 1) * 512],
                                       ALU.mult, ALU.add).then_inc(sm["cV"].h, 1)

    def ph7_scal(sc):
        for oc in range(C):
            for th in range(NTH):
                k = it7(oc, th)
                b4 = k % 4
                sc.wait_ge(sm["cV"].h, cVb + k + 1)
                sc.activation(xqs[b4][:], x1b[b4][:], AF.Square).then_inc(sm["cA"].h, 1)

    block.gpsimd(ph7_gps_w)
    block.sync(ph7_sync)
    block.tensor(ph7_pe)
    block.vector(ph7_vec)
    block.scalar(ph7_scal)
    sm["dw0"].v += 16 * ((C + 1) // 2)
    sm["dw1"].v += 16 * (C // 2)
    sm["dx0"].v += 16 * ((C + 1) // 2)
    sm["dx1"].v += 16 * (C // 2)
    for j in range(4):
        sm[f"dst{j}"].v += 16 * ((NIT - j + 3) // 4)
    sm["pPE"].v += NIT + 1
    sm["cV"].v += NIT
    sm["cA"].v += NIT
    sm["cPL"].v += NIT
    dst_after_wc = {j: sm[f"dst{j}"].v for j in range(4)}
    BAR()
    FR(*ps_o7[::-1])
    ps_rb3 = AL(nc.psum_tensor("ps_rb3", [128, TC], F32))
    ps_mb3 = AL(nc.psum_tensor("ps_mb3", [128, TC], F32))
    ln_rowmath_bcast(ps_s1c, ps_s2c, ps_rb3, ps_mb3)
    BAR()
    ln_apply(xs2, attn_sb, ps_rb3, ps_mb3, dst_wait=dst_after_wc)
    BAR()
    FR(ps_mb3, ps_rb3, ps_s2c, ps_s1c)

    ffn(attn_sb, xs2, out_f, None, None, dst_wait=dst_after_wc)

    dst_total = {j: sm[f"dst{j}"].v for j in range(4)}

    def fin_sync(s):
        for j in range(4):
            s.wait_ge(sm[f"dst{j}"].h, dst_total[j])
    block.sync(fin_sync)

    blk_cm.__exit__(None, None, None)
    for space in ("psum", "sbuf"):
        while _stacks[space]:
            h = _stacks[space].pop()
            _cm_of.pop(id(h)).__exit__(None, None, None)
    return nc


def _consts_np():
    ident = np.eye(128, dtype=np.float32)
    allones = np.ones((128, 128), dtype=np.float32)
    kk = np.arange(128, dtype=np.float32)[:, None]
    qq = np.arange(512, dtype=np.float32)[None, :]
    D0 = (kk - qq).astype(np.float32)
    deltaB = np.zeros((128, 72), dtype=np.float32)
    for mm in range(72):
        deltaB[:, mm] = -128.0 * mm
    CB = np.zeros((128, 72), dtype=np.float32)
    for mm in range(72):
        CB[:, mm] = np.arange(128) - 128.0 * mm - 511.0
    return np.ascontiguousarray(
        np.concatenate([ident, allones, D0, deltaB, CB], axis=1))


def _dm_np():
    kk = np.arange(128, dtype=np.float32)[:, None]
    qq = np.arange(512, dtype=np.float32)[None, :]
    Dms = []
    for jm in range(4):
        d = kk - qq + 128.0 * jm
        Dms.append(np.where(d <= 0.0, d, -1e13).astype(np.float32))
    return np.ascontiguousarray(np.concatenate(Dms, axis=1))


def _dm2_np():
    kk = np.arange(128, dtype=np.float32)[:, None]
    qq = np.arange(512, dtype=np.float32)[None, :]
    Dms = []
    for jm in range(4):
        valid = (kk - qq + 128.0 * jm) <= 0.0
        val = np.broadcast_to(kk + 128.0 * jm - 511.0, (128, 512))
        Dms.append(np.where(valid, val, -1e13).astype(np.float32))
    return np.ascontiguousarray(np.concatenate(Dms, axis=1))


def _e2_np():
    e2 = np.zeros((128, 128), dtype=np.float32)
    for e in range(NEXP):
        e2[e, e * R:(e + 1) * R] = 2.0
    return e2


def shard_inputs(S, L0_attn_hidden_states, L0_attn_w_qkv, L0_attn_w_c,
                 L0_ffn_gate_w, L0_ffn_base_w, L0_ffn_base_b, L0_ffn_lora_A, L0_ffn_lora_B):
    TC = S // NCORE
    x = np.ascontiguousarray(np.asarray(L0_attn_hidden_states, dtype=np.float32).reshape(S, D))
    wqkv = np.asarray(L0_attn_w_qkv, dtype=np.float32)
    wc = np.asarray(L0_attn_w_c, dtype=np.float32)
    gate_w = np.asarray(L0_ffn_gate_w, dtype=np.float32)
    base_w = np.asarray(L0_ffn_base_w, dtype=np.float32)
    base_b = np.asarray(L0_ffn_base_b, dtype=np.float32)
    lora_A = np.asarray(L0_ffn_lora_A, dtype=np.float32)
    lora_B = np.asarray(L0_ffn_lora_B, dtype=np.float32)

    basewT_np = np.ascontiguousarray(base_w.T)
    wkv_np = np.ascontiguousarray(wqkv[:, D:D + 256])
    gate_np = np.ascontiguousarray(gate_w.T)
    lat_np = np.ascontiguousarray(lora_A.reshape(NEXP * R, D).T)
    lbt_np = np.ascontiguousarray(lora_B.transpose(0, 2, 1).reshape(NEXP * R, D))
    bias_np = np.ascontiguousarray(base_b.reshape(C, 128).T)
    consts_np = _consts_np()
    e2_np = _e2_np()
    dm_np = _dm_np()
    dm2_np = _dm2_np()

    perm = np.zeros(D, dtype=np.int64)
    for r in range(NCORE):
        for hh in range(4):
            h = 8 * hh + r
            perm[r * 512 + hh * 128: r * 512 + (hh + 1) * 128] = np.arange(h * DH, (h + 1) * DH)
    wc_p_np = np.ascontiguousarray(wc[perm, :])

    in_maps = []
    for i in range(NCORE):
        heads = [8 * j + i for j in range(4)]
        wq_i = np.ascontiguousarray(
            np.concatenate([wqkv[:, h * DH:(h + 1) * DH] for h in heads], axis=1))
        slope4_np = np.zeros((128, 4), dtype=np.float32)
        for j, h in enumerate(heads):
            slope4_np[:, j] = SLOPES[h]
        x_i = np.ascontiguousarray(x[i * TC:(i + 1) * TC, :].T)
        in_maps.append(dict(
            x_f=x_i, basewT=basewT_np, wkv=wkv_np, wq=wq_i, wc_p=wc_p_np,
            gate_wT=gate_np, lora_AT=lat_np, lora_BT=lbt_np,
            bias_fm=bias_np, slope4=slope4_np, consts=consts_np, e2_in=e2_np, dm_in=dm_np, dm2_in=dm2_np,
        ))
    return in_maps


def assemble_output(S, results):
    TC = S // NCORE
    out = np.empty((S, D), dtype=np.float32)
    for i in range(NCORE):
        out[i * TC:(i + 1) * TC, :] = np.asarray(results[i]["out_f"]).T
    return out.reshape(1, S, D)


def kernel(**inputs):
    S = int(np.asarray(inputs["L0_attn_hidden_states"]).shape[1])
    in_maps = shard_inputs(S, **inputs)
    nc = build_program(S=S)
    res = run_bass_kernel_spmd(nc, in_maps, core_ids=list(range(NCORE)))
    return assemble_output(S, res.results)


# version 11
# speedup vs baseline: 1.0656x; 1.0034x over previous
import math
import numpy as np

import concourse.bass as bass
import concourse.mybir as mybir
from concourse.bass_utils import run_bass_kernel_spmd

F32 = mybir.dt.float32
BF16 = mybir.dt.bfloat16
AF = mybir.ActivationFunctionType
ALU = mybir.AluOpType
AX = mybir.AxisListType.X

D = 4096
C = D // 128
NCORE = 8
H = 32
DH = 128
NEXP = 8
R = 16
EPS = 1e-5
QSCALE = 1.0 / math.sqrt(DH)
T_WIN = 70.0

SLOPES = [2.0 ** (-(h + 1) / 4.0) for h in range(H)]
SLOT_W = [T_WIN / SLOPES[8 * j + 7] for j in range(4)]


def kt_lo(slot, qt):
    return max(0, int(math.floor((512.0 * qt - SLOT_W[slot]) / 128.0)))


class Sem:
    def __init__(self, nc, name):
        self._cm = nc.semaphore(name)
        self.h = self._cm.__enter__()
        self.v = 0


def build_program(S=8192):
    TC = S // NCORE
    NTH = TC // 512
    NQT = S // 512
    NT128 = TC // 128
    assert TC % 512 == 0

    nc = bass.Bass("TRN2", target_bir_lowering=False, debug=False)

    x_f = nc.dram_tensor("x_f", [D, TC], F32, kind="ExternalInput").ap()
    basewT = nc.dram_tensor("basewT", [D, D], F32, kind="ExternalInput").ap()
    wkv_in = nc.dram_tensor("wkv", [D, 256], F32, kind="ExternalInput").ap()
    wq_in = nc.dram_tensor("wq", [D, 512], F32, kind="ExternalInput").ap()
    wc_p = nc.dram_tensor("wc_p", [D, D], F32, kind="ExternalInput").ap()
    gate_wT = nc.dram_tensor("gate_wT", [D, NEXP], F32, kind="ExternalInput").ap()
    lora_AT = nc.dram_tensor("lora_AT", [D, NEXP * R], F32, kind="ExternalInput").ap()
    lora_BT = nc.dram_tensor("lora_BT", [NEXP * R, D], F32, kind="ExternalInput").ap()
    bias_fm = nc.dram_tensor("bias_fm", [128, C], F32, kind="ExternalInput").ap()
    slope4_in = nc.dram_tensor("slope4", [128, 4], F32, kind="ExternalInput").ap()
    e2_in = nc.dram_tensor("e2_in", [128, 128], F32, kind="ExternalInput").ap()
    NCONST = 128 + 128 + 512 + 72 + 72
    consts_in = nc.dram_tensor("consts", [128, NCONST], F32, kind="ExternalInput").ap()
    dm_in = nc.dram_tensor("dm_in", [128, 4 * 512], F32, kind="ExternalInput").ap()
    dm2_in = nc.dram_tensor("dm2_in", [128, 4 * 512], F32, kind="ExternalInput").ap()
    O_ID, O_ONE, O_D0, O_DB, O_CB = 0, 128, 256, 768, 840

    out_f = nc.dram_tensor("out_f", [D, TC], F32, kind="ExternalOutput").ap()

    xs1 = nc.dram_tensor("xs1", [D, TC], F32).ap()
    xs2 = nc.dram_tensor("xs2", [D, TC], F32).ap()
    qdram = nc.dram_tensor("qdram", [512, S], BF16).ap()
    ag_x_in = nc.dram_tensor("ag_x_in", [D, TC], BF16).ap()
    ag_xo = [nc.dram_tensor(f"ag_xo{q}", [NCORE * 1024, TC], BF16, addr_space="Shared").ap()
             for q in range(4)]
    ag_k_in = nc.dram_tensor("ag_k_in", [128, TC], BF16).ap()
    ag_k_out = nc.dram_tensor("ag_k_out", [NCORE * 128, TC], BF16, addr_space="Shared").ap()
    agv_in = nc.dram_tensor("agv_in", [TC, 128], BF16).ap()
    agv_out = nc.dram_tensor("agv_out", [S, 128], BF16, addr_space="Shared").ap()
    ag_o_in = nc.dram_tensor("ag_o_in", [512, S], BF16).ap()
    ag_oo = [nc.dram_tensor(f"ag_oo{h}", [NCORE * 128, S], BF16, addr_space="Shared").ap()
             for h in range(4)]

    _stacks = {"sbuf": [], "psum": []}
    _cm_of = {}

    def AL(cm):
        h = cm.__enter__()
        space = "psum" if "PSum" in type(h).__name__ else "sbuf"
        _stacks[space].append(h)
        _cm_of[id(h)] = cm
        return h

    def FR(*hs):
        for h in hs:
            space = "psum" if "PSum" in type(h).__name__ else "sbuf"
            top = _stacks[space].pop()
            assert top is h, f"non-LIFO free: {getattr(h, 'name', h)} vs {getattr(top, 'name', top)}"
            _cm_of.pop(id(h)).__exit__(None, None, None)

    consts_sb = AL(nc.sbuf_tensor("consts_sb", [128, NCONST], F32))
    e2_sb = AL(nc.sbuf_tensor("e2_sb", [128, 128], F32))
    bias_sb = AL(nc.sbuf_tensor("bias_sb", [128, C], F32))
    bias05 = AL(nc.sbuf_tensor("bias05", [128, C], F32))
    slope4 = AL(nc.sbuf_tensor("slope4_sb", [128, 4], F32))
    onesb = AL(nc.sbuf_tensor("onesb", [128, 1], BF16))
    ones128b = AL(nc.sbuf_tensor("ones128b", [128, 128], BF16))
    c5_sb = AL(nc.sbuf_tensor("c5_sb", [128, 1], F32))
    gate_sb = AL(nc.sbuf_tensor("gate_sb", [128, C * NEXP], BF16))
    lat_sb = AL(nc.sbuf_tensor("lat_sb", [128, C * 128], BF16))
    lbt_sb = AL(nc.sbuf_tensor("lbt_sb", [128, D], BF16))
    xcb = [AL(nc.sbuf_tensor(f"xcb{i}", [128, TC], F32)) for i in range(4)]
    bwb = [AL(nc.sbuf_tensor(f"bwb{i}", [128, C * 128], BF16)) for i in range(2)]
    t1b = [AL(nc.sbuf_tensor(f"t1b{i}", [128, 512], F32)) for i in range(4)]
    x1b = [AL(nc.sbuf_tensor(f"x1b{i}", [128, 512], F32)) for i in range(4)]
    xbs = [AL(nc.sbuf_tensor(f"xbs{i}", [128, 512], BF16)) for i in range(4)]
    xqs = [AL(nc.sbuf_tensor(f"xqs{i}", [128, 512], BF16)) for i in range(4)]
    mean_r = AL(nc.sbuf_tensor("mean_r", [1, TC], F32))
    m2_r = AL(nc.sbuf_tensor("m2_r", [1, TC], F32))
    var_r = AL(nc.sbuf_tensor("var_r", [1, TC], F32))
    rstd_r = AL(nc.sbuf_tensor("rstd_r", [1, TC], F32))
    lg_sb = [AL(nc.sbuf_tensor(f"lg{i}", [128, NEXP], F32)) for i in range(2)]
    m1_sb = AL(nc.sbuf_tensor("m1_sb", [128, 1], F32))
    m2s_sb = AL(nc.sbuf_tensor("m2s_sb", [128, 1], F32))
    eq1_sb = AL(nc.sbuf_tensor("eq1_sb", [128, NEXP], F32))
    eq2_sb = AL(nc.sbuf_tensor("eq2_sb", [128, NEXP], F32))
    msk_sb = AL(nc.sbuf_tensor("msk_sb", [128, NEXP], F32))
    n1_sb = AL(nc.sbuf_tensor("n1_sb", [128, 1], F32))
    n2_sb = AL(nc.sbuf_tensor("n2_sb", [128, 1], F32))
    d21_sb = AL(nc.sbuf_tensor("d21_sb", [128, 1], F32))
    w2_sb = AL(nc.sbuf_tensor("w2_sb", [128, 1], F32))
    w1_sb = AL(nc.sbuf_tensor("w1_sb", [128, 1], F32))
    tie_sb = AL(nc.sbuf_tensor("tie_sb", [128, 1], F32))
    r1_sb = AL(nc.sbuf_tensor("r1_sb", [128, 1], F32))
    r2_sb = AL(nc.sbuf_tensor("r2_sb", [128, 1], F32))
    wfa_sb = AL(nc.sbuf_tensor("wfa_sb", [128, NEXP], F32))
    wf_all = AL(nc.sbuf_tensor("wf_all", [128, NT128 * NEXP], F32))
    wfT_sb = AL(nc.sbuf_tensor("wfT_sb", [8, TC], F32))
    wrep_sb = AL(nc.sbuf_tensor("wrep_sb", [128, TC], F32))
    mid_sb = AL(nc.sbuf_tensor("mid_sb", [128, TC], BF16))
    kc_sb = AL(nc.sbuf_tensor("kc_sb", [128, TC], BF16))
    vc_sb = AL(nc.sbuf_tensor("vc_sb", [128, TC], BF16))

    sm = {n: Sem(nc, n) for n in [
        "dmisc", "dmiscg", "dx0", "dx1", "dx2", "dx3", "dw0", "dw1",
        "dst0", "dst1", "dst2", "dst3", "dag", "dao0", "dao1",
        "dxg", "dwq", "dko", "dqt0", "dqt1", "dq0", "dq1", "cc", "pPE", "pPEv", "cV", "cA", "cPL"]}

    blk_cm = nc.Block()
    block = blk_cm.__enter__()

    bar = Sem(nc, "bar")

    def BAR():
        bar.v += 5
        target = bar.v
        for sec in (block.sync, block.gpsimd, block.tensor, block.vector, block.scalar):
            def f(eng, target=target):
                eng.drain().then_inc(bar.h, 1)
                eng.wait_ge(bar.h, target)
            sec(f)

    def ph0_sync(s):
        s.dma_start(out=consts_sb[:], in_=consts_in).then_inc(sm["dmisc"].h, 16)
        s.dma_start(out=e2_sb[:], in_=e2_in).then_inc(sm["dmisc"].h, 16)
        s.dma_start(out=bias_sb[:], in_=bias_fm).then_inc(sm["dmisc"].h, 16)
        s.dma_start(out=slope4[:], in_=slope4_in).then_inc(sm["dmisc"].h, 16)
    sm["dmisc"].v += 64

    def ph0_gps(g):
        g.dma_start(out=gate_sb[:].rearrange("p (c e) -> p c e", c=C),
                    in_=gate_wT.rearrange("(c p) e -> p c e", p=128)).then_inc(sm["dmiscg"].h, 16)
        g.dma_start(out=lat_sb[:].rearrange("p (c r) -> p c r", c=C),
                    in_=lora_AT.rearrange("(c p) r -> p c r", p=128)).then_inc(sm["dmiscg"].h, 16)
        g.dma_start(out=lbt_sb[:], in_=lora_BT).then_inc(sm["dmiscg"].h, 16)
        g.memset(onesb[:], 1.0)
        g.memset(ones128b[:], 1.0)
        g.memset(c5_sb[:], -5.0)
    sm["dmiscg"].v += 48

    def ph0_vec(v):
        v.wait_ge(sm["dmisc"].h, 64)
        v.tensor_scalar_mul(bias05[:], bias_sb[:], 0.5)

    block.sync(ph0_sync)
    block.gpsimd(ph0_gps)
    block.vector(ph0_vec)
    BAR()

    def ln_stream_stats(src_ap, ps_s1, ps_s2, dst_wait=None):
        dxb = [sm[f"dx{j}"].v for j in range(4)]
        cPLb, cAb, pPEb = sm["cPL"].v, sm["cA"].v, sm["pPE"].v

        def bi_of(c, th):
            return (c % 2) * 2 + th if NTH > 1 else (c % 2)

        def f_sync(s):
            if dst_wait is not None:
                for j in range(4):
                    s.wait_ge(sm[f"dst{j}"].h, dst_wait[j])
            for c in range(C):
                sl = c % 4
                if c >= 4:
                    s.wait_ge(sm["cPL"].h, cPLb + (c - 4) + 1)
                    s.wait_ge(sm["cA"].h, cAb + (c - 4) + 1)
                s.dma_start(out=xcb[sl][:], in_=src_ap[c * 128:(c + 1) * 128, :]).then_inc(sm[f"dx{sl}"].h, 16)

        def f_pool(g):
            for c in range(C):
                sl = c % 4
                g.wait_ge(sm[f"dx{sl}"].h, dxb[sl] + (c // 4 + 1) * 16)
                if c >= 2:
                    g.wait_ge(sm["pPE"].h, pPEb + (c - 2) + 1)
                for th in range(NTH):
                    ins = g.tensor_copy(xbs[bi_of(c, th)][:], xcb[sl][:, th * 512:(th + 1) * 512])
                    if th == NTH - 1:
                        ins.then_inc(sm["cPL"].h, 1)

        def f_scal(sc):
            for c in range(C):
                sl = c % 4
                sc.wait_ge(sm[f"dx{sl}"].h, dxb[sl] + (c // 4 + 1) * 16)
                if c >= 2:
                    sc.wait_ge(sm["pPE"].h, pPEb + (c - 2) + 1)
                for th in range(NTH):
                    ins = sc.activation(xqs[bi_of(c, th)][:], xcb[sl][:, th * 512:(th + 1) * 512], AF.Square)
                    if th == NTH - 1:
                        ins.then_inc(sm["cA"].h, 1)

        def f_pe(t):
            for c in range(C):
                t.wait_ge(sm["cPL"].h, cPLb + c + 1)
                t.wait_ge(sm["cA"].h, cAb + c + 1)
                for th in range(NTH):
                    bi = bi_of(c, th)
                    t.matmul(ps_s1[0:1, th * 512:(th + 1) * 512], onesb[:], xbs[bi][:],
                             start=(c == 0), stop=(c == C - 1))
                    ins = t.matmul(ps_s2[0:1, th * 512:(th + 1) * 512], onesb[:], xqs[bi][:],
                                   start=(c == 0), stop=(c == C - 1))
                ins.then_inc(sm["pPE"].h, 1)

        block.sync(f_sync)
        block.gpsimd(f_pool)
        block.scalar(f_scal)
        block.tensor(f_pe)
        for j in range(4):
            sm[f"dx{j}"].v += 16 * ((C - j + 3) // 4)
        sm["cPL"].v += C
        sm["cA"].v += C
        sm["pPE"].v += C

    def ln_rowmath_bcast(ps_s1, ps_s2, ps_rb, ps_mb):
        pPEb, cVb, cAb = sm["pPE"].v, sm["cV"].v, sm["cA"].v

        def f_vec(v):
            v.wait_ge(sm["pPE"].h, pPEb)
            v.tensor_scalar_mul(mean_r[:], ps_s1[:], 1.0 / D).then_inc(sm["cV"].h, 1)
            v.wait_ge(sm["cA"].h, cAb + 1)
            v.scalar_tensor_tensor(var_r[:], ps_s2[:], 1.0 / D, m2_r[:], ALU.mult, ALU.subtract).then_inc(sm["cV"].h, 1)
            v.wait_ge(sm["cV"].h, cVb + 2)
            v.tensor_scalar_add(var_r[:], var_r[:], EPS).then_inc(sm["cV"].h, 1)
            v.wait_ge(sm["cA"].h, cAb + 3)
            v.tensor_mul(mean_r[:], mean_r[:], rstd_r[:]).then_inc(sm["cV"].h, 1)

        def f_scal(sc):
            sc.wait_ge(sm["pPE"].h, pPEb)
            sc.activation(m2_r[:], ps_s1[:], AF.Square, scale=1.0 / D).then_inc(sm["cA"].h, 1)
            sc.wait_ge(sm["cV"].h, cVb + 3)
            sc.activation(m2_r[:], var_r[:], AF.Ln).then_inc(sm["cA"].h, 1)
            sc.wait_ge(sm["cA"].h, cAb + 2)
            sc.activation(rstd_r[:], m2_r[:], AF.Exp, scale=-0.5).then_inc(sm["cA"].h, 1)

        def f_pe(t):
            t.wait_ge(sm["cV"].h, cVb + 4)
            for th in range(NTH):
                t.matmul(ps_rb[:, th * 512:(th + 1) * 512], consts_sb[0:1, O_ONE:O_ONE + 128],
                         rstd_r[0:1, th * 512:(th + 1) * 512], start=True, stop=True)
                ins = t.matmul(ps_mb[:, th * 512:(th + 1) * 512], consts_sb[0:1, O_ONE:O_ONE + 128],
                               mean_r[0:1, th * 512:(th + 1) * 512], start=True, stop=True)
            ins.then_inc(sm["pPE"].h, 1)

        block.vector(f_vec)
        block.scalar(f_scal)
        block.tensor(f_pe)
        sm["cV"].v += 4
        sm["cA"].v += 3
        sm["pPE"].v += 1

    def ln_apply(src_ap, xnT, ps_rb, ps_mb, also_ag=None, dst_wait=None):
        dxb = [sm[f"dx{j}"].v for j in range(4)]
        cVb, pPEb = sm["cV"].v, sm["pPE"].v

        def f_sync(s):
            if dst_wait is not None:
                for j in range(4):
                    s.wait_ge(sm[f"dst{j}"].h, dst_wait[j])
            for c in range(C):
                sl = c % 4
                if c >= 4:
                    s.wait_ge(sm["cV"].h, cVb + 2 * (c - 4) + 2)
                s.dma_start(out=xcb[sl][:], in_=src_ap[c * 128:(c + 1) * 128, :]).then_inc(sm[f"dx{sl}"].h, 16)

        def f_vec(v):
            v.wait_ge(sm["pPE"].h, pPEb)
            for c in range(C):
                sl = c % 4
                v.wait_ge(sm[f"dx{sl}"].h, dxb[sl] + (c // 4 + 1) * 16)
                v.tensor_mul(xcb[sl][:], xcb[sl][:], ps_rb[:]).then_inc(sm["cV"].h, 1)
                v.wait_ge(sm["cV"].h, cVb + 2 * c + 1)
                v.tensor_sub(xnT[:, c * TC:(c + 1) * TC], xcb[sl][:], ps_mb[:]).then_inc(sm["cV"].h, 1)

        block.sync(f_sync)
        block.vector(f_vec)
        for j in range(4):
            sm[f"dx{j}"].v += 16 * ((C - j + 3) // 4)
        sm["cV"].v += 2 * C

        if also_ag is not None:
            dagb = sm["dag"].v

            def f_gps(g):
                for c in range(C):
                    g.wait_ge(sm["cV"].h, cVb + 2 * c + 2)
                    g.dma_start(out=also_ag[c * 128:(c + 1) * 128, :],
                                in_=xnT[:, c * TC:(c + 1) * TC]).then_inc(sm["dag"].h, 16)
                    if c % 8 == 7:
                        q = c // 8
                        g.wait_ge(sm["dag"].h, dagb + 16 * (c + 1))
                        g.collective_compute(
                            "AllGather", ALU.bypass, replica_groups=[list(range(NCORE))],
                            ins=[also_ag[q * 1024:(q + 1) * 1024, :]],
                            outs=[ag_xo[q]]).then_inc(sm["cc"].h, 1)
            block.gpsimd(f_gps)
            sm["dag"].v += 16 * C
            sm["cc"].v += 4

    _ffn_n = [0]

    def ffn(xnT, resid_ap, out_ap, ps_s1, ps_s2, dst_wait=None):
        tag = _ffn_n[0]
        _ffn_n[0] += 1
        do_stats = ps_s1 is not None
        ps_g = AL(nc.psum_tensor(f"ps_g{tag}", [128, NEXP], F32))
        ps_wfT = AL(nc.psum_tensor(f"ps_wfT{tag}", [8, 128], F32))
        ps_wrep = AL(nc.psum_tensor(f"ps_wrep{tag}", [128, 512], F32))
        ps_low = AL(nc.psum_tensor(f"ps_low{tag}", [128, 512], F32))
        pPEb, cVb, cAb = sm["pPE"].v, sm["cV"].v, sm["cA"].v
        NSTEP = 19

        def f_pe2(t):
            t.wait_ge(sm["dmiscg"].h, 48)
            t.wait_ge(sm["dmisc"].h, 64)
            for tt in range(NT128):
                if tt > 0:
                    t.wait_ge(sm["cV"].h, cVb + NSTEP * (tt - 1) + 1)
                for c in range(C):
                    ins = t.matmul(ps_g[:], xnT[:, c * TC + tt * 128: c * TC + (tt + 1) * 128],
                                   gate_sb[:, c * NEXP:(c + 1) * NEXP],
                                   start=(c == 0), stop=(c == C - 1))
                ins.then_inc(sm["pPE"].h, 1)
            for tt in range(NT128):
                t.wait_ge(sm["cV"].h, cVb + NSTEP * tt + NSTEP)
                if tt > 0:
                    t.wait_ge(sm["cV"].h, cVb + NSTEP * NT128 + tt)
                t.transpose(ps_wfT[:], wf_all[:, tt * NEXP:(tt + 1) * NEXP],
                            consts_sb[:, O_ID:O_ID + 128]).then_inc(sm["pPE"].h, 1)
            for th in range(NTH):
                t.wait_ge(sm["cV"].h, cVb + NSTEP * NT128 + NT128)
                if th > 0:
                    t.wait_ge(sm["cV"].h, cVb + NSTEP * NT128 + NT128 + th)
                t.matmul(ps_wrep[:], e2_sb[0:8, :], wfT_sb[0:8, th * 512:(th + 1) * 512],
                         start=True, stop=True).then_inc(sm["pPE"].h, 1)
            for th in range(NTH):
                t.wait_ge(sm["cV"].h, cVb + NSTEP * NT128 + NT128 + NTH)
                if th > 0:
                    t.wait_ge(sm["cV"].h, cVb + NSTEP * NT128 + NT128 + NTH + th)
                for c in range(C):
                    ins = t.matmul(ps_low[:], lat_sb[:, c * 128:(c + 1) * 128],
                                   xnT[:, c * TC + th * 512: c * TC + th * 512 + 512],
                                   start=(c == 0), stop=(c == C - 1))
                ins.then_inc(sm["pPE"].h, 1)

        def f_vec2(v):
            n = cVb
            for tt in range(NT128):
                sl = tt % 2
                v.wait_ge(sm["pPE"].h, pPEb + tt + 1)

                def step(ins):
                    nonlocal n
                    n += 1
                    ins.then_inc(sm["cV"].h, 1)
                    v.wait_ge(sm["cV"].h, n)
                step(v.tensor_copy(lg_sb[sl][:], ps_g[:]))
                step(v.reduce_max(m1_sb[:], lg_sb[sl][:], axis=AX))
                step(v.tensor_scalar(eq1_sb[:], lg_sb[sl][:], m1_sb[:, 0:1], None, ALU.is_ge))
                step(v.scalar_tensor_tensor(msk_sb[:], eq1_sb[:], -1e30, lg_sb[sl][:], ALU.mult, ALU.add))
                step(v.reduce_sum(n1_sb[:], eq1_sb[:], axis=AX))
                step(v.reduce_max(m2s_sb[:], msk_sb[:], axis=AX))
                step(v.tensor_scalar(eq2_sb[:], lg_sb[sl][:], m2s_sb[:, 0:1], None, ALU.is_equal))
                step(v.tensor_sub(d21_sb[:], m2s_sb[:], m1_sb[:]))
                step(v.reduce_sum(n2_sb[:], eq2_sb[:], axis=AX))
                step(v.tensor_scalar(tie_sb[:], n1_sb[:], 1.5, None, ALU.is_ge))
                step(v.reciprocal(r1_sb[:], n1_sb[:]))
                step(v.reciprocal(r2_sb[:], n2_sb[:]))
                v.wait_ge(sm["cA"].h, cAb + tt + 1)
                step(v.tensor_scalar(w1_sb[:], tie_sb[:], -1.0, 1.0, ALU.mult, ALU.add))
                step(v.tensor_mul(w2_sb[:], w2_sb[:], w1_sb[:]))
                step(v.tensor_scalar(w1_sb[:], w2_sb[:], -1.0, 1.0, ALU.mult, ALU.add))
                step(v.tensor_mul(w1_sb[:], w1_sb[:], r1_sb[:]))
                step(v.tensor_mul(w2_sb[:], w2_sb[:], r2_sb[:]))
                step(v.tensor_scalar(wfa_sb[:], eq1_sb[:], w1_sb[:, 0:1], None, ALU.mult))
                step(v.scalar_tensor_tensor(wf_all[:, tt * NEXP:(tt + 1) * NEXP], eq2_sb[:],
                                            w2_sb[:, 0:1], wfa_sb[:], ALU.mult, ALU.add))
            for tt in range(NT128):
                v.wait_ge(sm["pPE"].h, pPEb + NT128 + tt + 1)
                n += 1
                v.tensor_copy(wfT_sb[0:8, tt * 128:(tt + 1) * 128], ps_wfT[:]).then_inc(sm["cV"].h, 1)
                v.wait_ge(sm["cV"].h, n)
            for th in range(NTH):
                v.wait_ge(sm["pPE"].h, pPEb + 2 * NT128 + th + 1)
                n += 1
                v.tensor_copy(wrep_sb[:, th * 512:(th + 1) * 512], ps_wrep[:]).then_inc(sm["cV"].h, 1)
                v.wait_ge(sm["cV"].h, n)
            for th in range(NTH):
                v.wait_ge(sm["pPE"].h, pPEb + 2 * NT128 + NTH + th + 1)
                n += 1
                v.tensor_mul(mid_sb[:, th * 512:(th + 1) * 512], ps_low[:],
                             wrep_sb[:, th * 512:(th + 1) * 512]).then_inc(sm["cV"].h, 1)
                v.wait_ge(sm["cV"].h, n)

        def f_scal2(sc):
            for tt in range(NT128):
                sc.wait_ge(sm["cV"].h, cVb + tt * NSTEP + 8)
                sc.activation(w2_sb[:], d21_sb[:], AF.Sigmoid).then_inc(sm["cA"].h, 1)

        block.tensor(f_pe2)
        block.vector(f_vec2)
        block.scalar(f_scal2)
        sm["pPE"].v += 2 * NT128 + 2 * NTH
        sm["cV"].v += NSTEP * NT128 + NT128 + 2 * NTH
        sm["cA"].v += NT128
        BAR()
        FR(ps_low, ps_wrep, ps_wfT, ps_g)

        ps_o = [AL(nc.psum_tensor(f"ps_fo{tag}_{i}", [128, 512], F32)) for i in range(4)]
        pPEb, cVb, cAb, cPLb = sm["pPE"].v, sm["cV"].v, sm["cA"].v, sm["cPL"].v
        dwb = [sm["dw0"].v, sm["dw1"].v]
        dxb = [sm["dx0"].v, sm["dx1"].v]
        dstb = [sm[f"dst{j}"].v for j in range(4)]
        NIT = C * NTH
        CAK = 2 if do_stats else 1

        def it(oc, th):
            return oc * NTH + th

        def pool_copies(g, po):
            for th in range(NTH):
                k = it(po, th)
                b4 = k % 4
                g.wait_ge(sm["cV"].h, cVb + k + 1)
                if k >= 4:
                    g.wait_ge(sm["pPE"].h, pPEb + k + 1)
                g.tensor_copy(xbs[b4][:], x1b[b4][:]).then_inc(sm["cPL"].h, 1)

        def f_gps_w(g):
            for oc in range(C):
                sl = oc % 2
                if oc >= 2:
                    g.wait_ge(sm["pPE"].h, pPEb + it(oc - 2, NTH - 1) + 1)
                g.dma_start(
                    out=bwb[sl][:].rearrange("p (c o) -> p c o", c=C),
                    in_=basewT[:, oc * 128:(oc + 1) * 128].rearrange("(c p) o -> p c o", p=128),
                ).then_inc(sm[f"dw{sl}"].h, 16)
                if do_stats and oc >= 1:
                    pool_copies(g, oc - 1)
            if do_stats:
                pool_copies(g, C - 1)

        def f_sync(s):
            if dst_wait is not None:
                for j in range(4):
                    s.wait_ge(sm[f"dst{j}"].h, dst_wait[j])
            for oc in range(C):
                sl = oc % 2
                if oc >= 2:
                    s.wait_ge(sm["cV"].h, cVb + it(oc - 2, NTH - 1) + 1)
                s.dma_start(out=xcb[sl][:], in_=resid_ap[oc * 128:(oc + 1) * 128, :]).then_inc(sm[f"dx{sl}"].h, 16)
                for th in range(NTH):
                    b4 = it(oc, th) % 4
                    s.wait_ge(sm["cV"].h, cVb + it(oc, th) + 1)
                    s.dma_start(out=out_ap[oc * 128:(oc + 1) * 128, th * 512:(th + 1) * 512],
                                in_=x1b[b4][:]).then_inc(sm[f"dst{b4}"].h, 16)

        def f_pe(t):
            for oc in range(C):
                sl = oc % 2
                t.wait_ge(sm[f"dw{sl}"].h, dwb[sl] + (oc // 2 + 1) * 16)
                for th in range(NTH):
                    k = it(oc, th)
                    if k >= 4:
                        t.wait_ge(sm["cA"].h, cAb + CAK * (k - 4) + 1)
                for c in range(C):
                    for th in range(NTH):
                        b4 = it(oc, th) % 4
                        t.matmul(ps_o[b4][:], bwb[sl][:, c * 128:(c + 1) * 128],
                                 xnT[:, c * TC + th * 512: c * TC + th * 512 + 512],
                                 start=(c == 0), stop=False)
                for th in range(NTH):
                    b4 = it(oc, th) % 4
                    t.matmul(ps_o[b4][:], lbt_sb[:, oc * 128:(oc + 1) * 128],
                             mid_sb[:, th * 512:(th + 1) * 512],
                             start=False, stop=True).then_inc(sm["pPE"].h, 1)
                if do_stats and oc >= 1:
                    po = oc - 1
                    for th in range(NTH):
                        b4 = it(po, th) % 4
                        t.wait_ge(sm["cPL"].h, cPLb + it(po, th) + 1)
                        t.wait_ge(sm["cA"].h, cAb + CAK * it(po, th) + 2)
                        t.matmul(ps_s1[0:1, th * 512:(th + 1) * 512], onesb[:], xbs[b4][:],
                                 start=(po == 0), stop=(po == C - 1))
                        t.matmul(ps_s2[0:1, th * 512:(th + 1) * 512], onesb[:], xqs[b4][:],
                                 start=(po == 0), stop=(po == C - 1))
            if do_stats:
                po = C - 1
                for th in range(NTH):
                    b4 = it(po, th) % 4
                    t.wait_ge(sm["cPL"].h, cPLb + it(po, th) + 1)
                    t.wait_ge(sm["cA"].h, cAb + CAK * it(po, th) + 2)
                    t.matmul(ps_s1[0:1, th * 512:(th + 1) * 512], onesb[:], xbs[b4][:],
                             start=(po == 0), stop=(po == C - 1))
                    ins = t.matmul(ps_s2[0:1, th * 512:(th + 1) * 512], onesb[:], xqs[b4][:],
                                   start=(po == 0), stop=(po == C - 1))
                ins.then_inc(sm["pPE"].h, 1)

        def f_scal(sc):
            for oc in range(C):
                for th in range(NTH):
                    k = it(oc, th)
                    b4 = k % 4
                    sc.wait_ge(sm["pPE"].h, pPEb + k + 1)
                    if k >= 4:
                        sc.wait_ge(sm["cV"].h, cVb + (k - 4) + 1)
                    sc.activation(t1b[b4][:], ps_o[b4][:], AF.Identity,
                                  bias=bias05[:, oc:oc + 1], scale=0.5).then_inc(sm["cA"].h, 1)
                    if do_stats:
                        sc.wait_ge(sm["cV"].h, cVb + k + 1)
                        sc.activation(xqs[b4][:], x1b[b4][:], AF.Square).then_inc(sm["cA"].h, 1)

        def f_vec(v):
            for oc in range(C):
                sl = oc % 2
                v.wait_ge(sm[f"dx{sl}"].h, dxb[sl] + (oc // 2 + 1) * 16)
                for th in range(NTH):
                    k = it(oc, th)
                    b4 = k % 4
                    v.wait_ge(sm["cA"].h, cAb + CAK * k + 1)
                    if k >= 4:
                        v.wait_ge(sm[f"dst{b4}"].h, dstb[b4] + ((k - 4) // 4 + 1) * 16)
                        if do_stats:
                            v.wait_ge(sm["cA"].h, cAb + CAK * (k - 4) + 2)
                            v.wait_ge(sm["cPL"].h, cPLb + (k - 4) + 1)
                    v.tensor_add(x1b[b4][:], t1b[b4][:],
                                 xcb[sl][:, th * 512:(th + 1) * 512]).then_inc(sm["cV"].h, 1)

        block.gpsimd(f_gps_w)
        block.sync(f_sync)
        block.tensor(f_pe)
        block.scalar(f_scal)
        block.vector(f_vec)

        sm["dw0"].v += 16 * ((C + 1) // 2)
        sm["dw1"].v += 16 * (C // 2)
        sm["dx0"].v += 16 * ((C + 1) // 2)
        sm["dx1"].v += 16 * (C // 2)
        for j in range(4):
            sm[f"dst{j}"].v += 16 * ((NIT - j + 3) // 4)
        sm["pPE"].v += NIT + (1 if do_stats else 0)
        sm["cA"].v += NIT * CAK
        sm["cV"].v += NIT
        if do_stats:
            sm["cPL"].v += NIT
        BAR()
        FR(*ps_o[::-1])

    ps_rb1 = AL(nc.psum_tensor("ps_rb1", [128, TC], F32))
    ps_mb1 = AL(nc.psum_tensor("ps_mb1", [128, TC], F32))
    ps_s1a = AL(nc.psum_tensor("ps_s1a", [1, TC], F32))
    ps_s2a = AL(nc.psum_tensor("ps_s2a", [1, TC], F32))
    ln_stream_stats(x_f, ps_s1a, ps_s2a)
    ln_rowmath_bcast(ps_s1a, ps_s2a, ps_rb1, ps_mb1)
    BAR()
    FR(ps_s2a, ps_s1a)
    xn1T = AL(nc.sbuf_tensor("xn1T", [128, C * TC], BF16))
    ln_apply(x_f, xn1T, ps_rb1, ps_mb1)
    BAR()

    FR(ps_mb1, ps_rb1)
    ps_s1b = AL(nc.psum_tensor("ps_s1b", [1, TC], F32))
    ps_s2b = AL(nc.psum_tensor("ps_s2b", [1, TC], F32))
    ffn(xn1T, x_f, xs1, ps_s1b, ps_s2b)
    dst_after_ffn1 = {j: sm[f"dst{j}"].v for j in range(4)}
    ps_rb2 = AL(nc.psum_tensor("ps_rb2", [128, TC], F32))
    ps_mb2 = AL(nc.psum_tensor("ps_mb2", [128, TC], F32))
    ln_rowmath_bcast(ps_s1b, ps_s2b, ps_rb2, ps_mb2)
    BAR()

    FR(xn1T)
    wq_sb = AL(nc.sbuf_tensor("wq_sb", [128, C * 512], BF16))
    def ph4_wq(g):
        g.dma_start(out=wq_sb[:].rearrange("p (c e) -> p c e", c=C),
                    in_=wq_in.rearrange("(c p) e -> p c e", p=128)).then_inc(sm["dwq"].h, 16)
    block.gpsimd(ph4_wq)
    sm["dwq"].v += 16
    xn2T = AL(nc.sbuf_tensor("xn2T", [128, C * TC], BF16))
    ln_apply(xs1, xn2T, ps_rb2, ps_mb2, also_ag=ag_x_in, dst_wait=dst_after_ffn1)
    dag_after_x = sm["dag"].v
    BAR()
    FR(ps_mb2, ps_rb2, ps_s2b, ps_s1b)

    ps_k = AL(nc.psum_tensor("ps_k", [128, 512], F32))
    ps_v = AL(nc.psum_tensor("ps_v", [128, 128], F32))
    pPEb, cVb = sm["pPE"].v, sm["cV"].v
    dmgb = sm["dmiscg"].v

    def ph4_gps(g):
        g.dma_start(out=bwb[0][:].rearrange("p (c e) -> p c e", c=C),
                    in_=wkv_in[:, 0:128].rearrange("(c p) e -> p c e", p=128)).then_inc(sm["dmiscg"].h, 16)
        g.dma_start(out=bwb[1][:].rearrange("p (c e) -> p c e", c=C),
                    in_=wkv_in[:, 128:256].rearrange("(c p) e -> p c e", p=128)).then_inc(sm["dmiscg"].h, 16)

    def ph4_pe(t):
        t.wait_ge(sm["dmiscg"].h, dmgb + 32)
        for th in range(NTH):
            if th > 0:
                t.wait_ge(sm["cV"].h, cVb + th)
            for c in range(C):
                ins = t.matmul(ps_k[:], bwb[0][:, c * 128:(c + 1) * 128],
                               xn2T[:, c * TC + th * 512: c * TC + th * 512 + 512],
                               start=(c == 0), stop=(c == C - 1))
            ins.then_inc(sm["pPE"].h, 1)
        for tt in range(NT128):
            if tt > 0:
                t.wait_ge(sm["cV"].h, cVb + NTH + tt)
            for c in range(C):
                ins = t.matmul(ps_v[:], xn2T[:, c * TC + tt * 128: c * TC + (tt + 1) * 128],
                               bwb[1][:, c * 128:(c + 1) * 128],
                               start=(c == 0), stop=(c == C - 1))
            ins.then_inc(sm["pPE"].h, 1)

    def ph4_vec(v):
        for th in range(NTH):
            v.wait_ge(sm["pPE"].h, pPEb + th + 1)
            v.tensor_copy(kc_sb[:, th * 512:(th + 1) * 512], ps_k[:]).then_inc(sm["cV"].h, 1)
        for tt in range(NT128):
            v.wait_ge(sm["pPE"].h, pPEb + NTH + tt + 1)
            v.tensor_copy(vc_sb[:, tt * 128:(tt + 1) * 128], ps_v[:]).then_inc(sm["cV"].h, 1)

    def ph4_gps2(g):
        g.wait_ge(sm["cV"].h, cVb + NTH)
        g.dma_start(out=ag_k_in, in_=kc_sb[:]).then_inc(sm["dag"].h, 16)
        g.wait_ge(sm["cV"].h, cVb + NTH + NT128)
        g.dma_start(out=agv_in.rearrange("(g p) d -> p g d", p=128),
                    in_=vc_sb[:].rearrange("p (g d) -> p g d", d=128)).then_inc(sm["dag"].h, 16)
        g.wait_ge(sm["dag"].h, dag_after_x + 32)
        g.collective_compute("AllGather", ALU.bypass, replica_groups=[list(range(NCORE))],
                             ins=[ag_k_in], outs=[ag_k_out]).then_inc(sm["cc"].h, 1)
        g.collective_compute("AllGather", ALU.bypass, replica_groups=[list(range(NCORE))],
                             ins=[agv_in], outs=[agv_out]).then_inc(sm["cc"].h, 1)

    block.gpsimd(ph4_gps)
    sm["dmiscg"].v += 32
    block.tensor(ph4_pe)
    block.vector(ph4_vec)
    block.gpsimd(ph4_gps2)
    sm["pPE"].v += NTH + NT128
    sm["cV"].v += NTH + NT128
    sm["dag"].v += 32
    sm["cc"].v += 2
    dag_kv_done = sm["dag"].v
    cc_x0, cc_k, cc_v = sm["cc"].v - 6, sm["cc"].v - 1, sm["cc"].v
    BAR()
    FR(ps_v, ps_k)

    FR(xn2T)
    xgc = [AL(nc.sbuf_tensor(f"xgc{i}", [128, TC], BF16)) for i in range(4)]
    qt_sb = [AL(nc.sbuf_tensor(f"qt{i}", [128, 512], BF16)) for i in range(2)]
    ps_q = [AL(nc.psum_tensor(f"ps_q{i}", [128, 512], F32)) for i in range(4 * NTH)]
    pPEb, cAb = sm["pPE"].v, sm["cA"].v
    dxb = [sm[f"dx{j}"].v for j in range(4)]
    dqb = [sm["dq0"].v, sm["dq1"].v]
    dmgb = sm["dmiscg"].v
    NPQ = 4 * NTH

    def ph5_gps(g):
        for r in range(NCORE):
            for k in range(NPQ):
                hh, th = k // NTH, k % NTH
                kk = r * NPQ + k
                g.wait_ge(sm["cA"].h, cAb + kk + 1)
                g.dma_start(out=qdram[hh * 128:(hh + 1) * 128,
                                      r * TC + th * 512: r * TC + th * 512 + 512],
                            in_=qt_sb[kk % 2][:]).then_inc(sm[f"dq{kk % 2}"].h, 16)

    def ph5_sync(s):
        for r in range(NCORE):
            for c in range(C):
                idx = r * C + c
                sl = idx % 4
                if r == 0:
                    s.wait_ge(sm["cc"].h, cc_x0 + c // 8 + 1)
                if idx >= 4:
                    s.wait_ge(sm["pPE"].h, pPEb + (idx - 4) + 1)
                q, cq = c // 8, c % 8
                s.dma_start(out=xgc[sl][:],
                            in_=ag_xo[q][r * 1024 + cq * 128: r * 1024 + (cq + 1) * 128, :]).then_inc(sm[f"dx{sl}"].h, 16)

    def ph5_pe(t):
        t.wait_ge(sm["dwq"].h, 16)
        for r in range(NCORE):
            for c in range(C):
                idx = r * C + c
                sl = idx % 2
                sl4 = idx % 4
                t.wait_ge(sm[f"dx{sl4}"].h, dxb[sl4] + (idx // 4 + 1) * 16)
                for k in range(NPQ):
                    if c == 0 and r > 0:
                        t.wait_ge(sm["cA"].h, cAb + (r - 1) * NPQ + k + 1)
                    hh, th = k // NTH, k % NTH
                    ins = t.matmul(ps_q[k][:], wq_sb[:, c * 512 + hh * 128: c * 512 + (hh + 1) * 128],
                                   xgc[sl4][:, th * 512:(th + 1) * 512],
                                   start=(c == 0), stop=(c == C - 1))
                ins.then_inc(sm["pPE"].h, 1)

    def ph5_scal(sc):
        for r in range(NCORE):
            sc.wait_ge(sm["pPE"].h, pPEb + r * C + C)
            for k in range(NPQ):
                kk = r * NPQ + k
                if kk >= 2:
                    sc.wait_ge(sm[f"dq{kk % 2}"].h, dqb[kk % 2] + ((kk - 2) // 2 + 1) * 16)
                sc.activation(qt_sb[kk % 2][:], ps_q[k][:], AF.Copy,
                              scale=QSCALE).then_inc(sm["cA"].h, 1)

    block.gpsimd(ph5_gps)
    block.sync(ph5_sync)
    block.tensor(ph5_pe)
    block.scalar(ph5_scal)
    sm["dq0"].v += 16 * ((NCORE * NPQ + 1) // 2)
    sm["dq1"].v += 16 * ((NCORE * NPQ) // 2)
    for j in range(4):
        sm[f"dx{j}"].v += 16 * ((NCORE * C - j + 3) // 4)
    sm["pPE"].v += NCORE * C
    sm["cA"].v += NCORE * NPQ
    dq_total = (sm["dq0"].v, sm["dq1"].v)
    BAR()
    FR(*ps_q[::-1])
    FR(qt_sb[1], qt_sb[0], xgc[3], xgc[2], xgc[1], xgc[0], wq_sb)

    kf_sb = AL(nc.sbuf_tensor("kf_sb", [128, S], BF16))
    vt_sb = AL(nc.sbuf_tensor("vt_sb", [128, S], BF16))
    dm_sb = AL(nc.sbuf_tensor("dm_sb", [128, 2048], F32))
    dm2_sb = AL(nc.sbuf_tensor("dm2_sb", [128, 2048], F32))
    sD0 = AL(nc.sbuf_tensor("sD0", [128, 512], F32))
    sDm = AL(nc.sbuf_tensor("sDm", [128, 2048], F32))
    sDm2 = AL(nc.sbuf_tensor("sDm2", [128, 2048], F32))
    dT_sb = AL(nc.sbuf_tensor("dT_sb", [128, 72], F32))
    bvt_sb = AL(nc.sbuf_tensor("bvt_sb", [128, 72], F32))
    prob_sb = [AL(nc.sbuf_tensor(f"prob{i}", [128, 512], BF16)) for i in range(4)]
    qt6 = [AL(nc.sbuf_tensor(f"qt6_{i}", [128, 512], BF16)) for i in range(2)]
    zb_sb = [AL(nc.sbuf_tensor(f"zb{i}", [128, 512], F32)) for i in range(2)]
    lnz_sb = [AL(nc.sbuf_tensor(f"lnz{i}", [128, 512], F32)) for i in range(2)]
    ao_sb = [AL(nc.sbuf_tensor(f"ao{i}", [128, 512], BF16)) for i in range(2)]
    ps_sc = [AL(nc.psum_tensor(f"ps_sc{i}", [128, 512], F32)) for i in range(4)]
    ps_ao = [AL(nc.psum_tensor(f"ps_ao{i}", [128, 512], F32)) for i in range(2)]
    ps_z = [AL(nc.psum_tensor(f"ps_z{i}", [128, 512], F32)) for i in range(2)]

    sched = []
    for hh in (3, 2, 1, 0):
        for qt in range(NQT):
            sched.append((hh, qt, list(range(kt_lo(hh, qt), 4 * qt + 4))))
    NG = len(sched)
    NSC = [len(k) for (_, _, k) in sched]
    CSC = [0]
    for nsc in NSC:
        CSC.append(CSC[-1] + nsc)

    pPEb, pPEvb = sm["pPE"].v, sm["pPEv"].v
    cVb, cAb = sm["cV"].v, sm["cA"].v
    daob = [sm["dao0"].v, sm["dao1"].v]
    dkob = sm["dko"].v
    dqtb = [sm["dqt0"].v, sm["dqt1"].v]

    def is_phi(hh, m):
        return hh >= 2 and m >= 1

    cv_at, ca_at, pev_at, cv_after_group, hh_prep_at, ca_z = {}, {}, {}, {}, {}, {}
    cv_run, ca_run, pev_run = cVb, cAb, pPEvb
    for g, (hh, qt, kts) in enumerate(sched):
        if qt == 0:
            cv_run += 5
            hh_prep_at[hh] = cv_run
        for j, kt in enumerate(kts):
            m = 4 * qt - kt
            if not is_phi(hh, m):
                cv_run += 1
                cv_at[(g, j)] = cv_run
            ca_run += 1
            ca_at[(g, j)] = ca_run
        pev_run += len(kts)
        pev_at[g] = pev_run
        ca_run += 2
        ca_z[g] = ca_run
        cv_run += 1
        cv_after_group[g] = cv_run

    def ph6_sync(s):
        s.wait_ge(sm["dq0"].h, dq_total[0])
        s.wait_ge(sm["dq1"].h, dq_total[1])
        s.dma_start(out=dm_sb[:], in_=dm_in).then_inc(sm["dmisc"].h, 16)
        s.dma_start(out=dm2_sb[:], in_=dm2_in).then_inc(sm["dmisc"].h, 16)
        s.wait_ge(sm["cc"].h, cc_k)
        for r in range(NCORE):
            s.dma_start(out=kf_sb[:, r * TC:(r + 1) * TC],
                        in_=ag_k_out[r * 128:(r + 1) * 128, :]).then_inc(sm["dko"].h, 16)
        s.wait_ge(sm["cc"].h, cc_v)
        s.dma_start(out=vt_sb[:].rearrange("p (g d) -> p g d", d=128),
                    in_=agv_out.rearrange("(g p) d -> p g d", p=128)).then_inc(sm["dko"].h, 16)
        for g, (hh, qt, kts) in enumerate(sched):
            if g >= 2:
                s.wait_ge(sm["pPE"].h, pPEb + CSC[g - 1])
            s.dma_start(out=qt6[g % 2][:],
                        in_=qdram[hh * 128:(hh + 1) * 128, qt * 512:(qt + 1) * 512]).then_inc(sm[f"dqt{g % 2}"].h, 16)

    def ph6_vec(v):
        v.wait_ge(sm["dmisc"].h, 96)
        v.wait_ge(sm["dq0"].h, dq_total[0])
        v.wait_ge(sm["dq1"].h, dq_total[1])
        for g, (hh, qt, kts) in enumerate(sched):
            if qt == 0:
                if g > 0:
                    pg = g - 1
                    v.wait_ge(sm["cA"].h, ca_at[(pg, len(sched[pg][2]) - 1)])
                v.tensor_scalar(sD0[:], consts_sb[:, O_D0:O_D0 + 512], slope4[:, hh:hh + 1], None, ALU.mult).then_inc(sm["cV"].h, 1)
                v.tensor_scalar(sDm[:], dm_sb[:], slope4[:, hh:hh + 1], None, ALU.mult).then_inc(sm["cV"].h, 1)
                v.tensor_scalar(sDm2[:], dm2_sb[:], slope4[:, hh:hh + 1], None, ALU.mult).then_inc(sm["cV"].h, 1)
                v.tensor_scalar(dT_sb[:], consts_sb[:, O_DB:O_DB + 72], slope4[:, hh:hh + 1], None, ALU.mult).then_inc(sm["cV"].h, 1)
                v.tensor_scalar(bvt_sb[:], consts_sb[:, O_CB:O_CB + 72], slope4[:, hh:hh + 1], -5.0, ALU.mult, ALU.add).then_inc(sm["cV"].h, 1)
                v.wait_ge(sm["cV"].h, hh_prep_at[hh])
            for j, kt in enumerate(kts):
                sl = j % 4
                m = 4 * qt - kt
                if is_phi(hh, m):
                    continue
                v.wait_ge(sm["pPE"].h, pPEb + CSC[g] + j + 1)
                if m <= 0:
                    dmt = sDm2 if hh >= 2 else sDm
                    ins = v.tensor_add(ps_sc[sl][:], ps_sc[sl][:], dmt[:, (-m) * 512:(-m + 1) * 512])
                else:
                    ins = v.scalar_tensor_tensor(ps_sc[sl][:], ps_sc[sl][:], dT_sb[:, m:m + 1],
                                                 sD0[:], ALU.add, ALU.add)
                ins.then_inc(sm["cV"].h, 1)
            sl2 = g % 2
            v.wait_ge(sm["pPEv"].h, pev_at[g])
            v.wait_ge(sm["cA"].h, ca_z[g])
            if g >= 2:
                v.wait_ge(sm[f"dao{sl2}"].h, daob[sl2] + ((g - 2) // 2 + 1) * 16)
            v.tensor_mul(ao_sb[sl2][:], ps_ao[sl2][:], zb_sb[sl2][:]).then_inc(sm["cV"].h, 1)

    def ph6_scal(sc):
        for g, (hh, qt, kts) in enumerate(sched):
            for j, kt in enumerate(kts):
                sl = j % 4
                m = 4 * qt - kt
                if is_phi(hh, m):
                    sc.wait_ge(sm["pPE"].h, pPEb + CSC[g] + j + 1)
                    sc.wait_ge(sm["cV"].h, hh_prep_at[hh])
                    bias_ap = bvt_sb[:, m:m + 1]
                else:
                    sc.wait_ge(sm["cV"].h, cv_at[(g, j)])
                    bias_ap = c5_sb[:, 0:1]
                if j >= 4:
                    sc.wait_ge(sm["pPEv"].h, pPEvb + CSC[g] + (j - 4) + 1)
                elif g >= 1:
                    sc.wait_ge(sm["pPEv"].h, pev_at[g - 1])
                sc.activation(prob_sb[sl][:], ps_sc[sl][:], AF.Exp,
                              bias=bias_ap).then_inc(sm["cA"].h, 1)
            sl2 = g % 2
            sc.wait_ge(sm["pPEv"].h, pev_at[g])
            if g >= 2:
                sc.wait_ge(sm["cV"].h, cv_after_group[g - 2])
            sc.activation(lnz_sb[sl2][:], ps_z[sl2][:], AF.Ln).then_inc(sm["cA"].h, 1)
            sc.wait_ge(sm["cA"].h, ca_z[g] - 1)
            sc.activation(zb_sb[sl2][:], lnz_sb[sl2][:], AF.Exp, scale=-1.0).then_inc(sm["cA"].h, 1)

    def ph6_pe(t):
        t.wait_ge(sm["dko"].h, dkob + 16 * (NCORE + 1))
        for g, (hh, qt, kts) in enumerate(sched):
            sl2 = g % 2
            nk = len(kts)
            t.wait_ge(sm[f"dqt{g % 2}"].h, dqtb[g % 2] + (g // 2 + 1) * 16)
            LV = 2
            for j in range(nk + LV):
                if j < nk:
                    kt = kts[j]
                    sl = j % 4
                    if j >= 4:
                        t.wait_ge(sm["cA"].h, ca_at[(g, j - 4)])
                    elif g >= 1:
                        pg = g - 1
                        t.wait_ge(sm["cA"].h, ca_at[(pg, len(sched[pg][2]) - 1)])
                    t.matmul(ps_sc[sl][:], kf_sb[:, kt * 128:(kt + 1) * 128],
                             qt6[g % 2][:], start=True, stop=True).then_inc(sm["pPE"].h, 1)
                jj = j - LV
                if jj >= 0:
                    kt = kts[jj]
                    sl = jj % 4
                    t.wait_ge(sm["cA"].h, ca_at[(g, jj)])
                    if jj == 0 and g >= 2:
                        t.wait_ge(sm["cV"].h, cv_after_group[g - 2])
                        t.wait_ge(sm["cA"].h, ca_z[g - 2])
                    t.matmul(ps_ao[sl2][:], vt_sb[:, kt * 128:(kt + 1) * 128], prob_sb[sl][:],
                             start=(jj == 0), stop=(jj == nk - 1))
                    t.matmul(ps_z[sl2][:], ones128b[:], prob_sb[sl][:],
                             start=(jj == 0), stop=(jj == nk - 1)).then_inc(sm["pPEv"].h, 1)

    def ph6_gps(g_):
        for g, (hh, qt, kts) in enumerate(sched):
            sl2 = g % 2
            g_.wait_ge(sm["cV"].h, cv_after_group[g])
            g_.dma_start(out=ag_o_in[hh * 128:(hh + 1) * 128, qt * 512:(qt + 1) * 512],
                         in_=ao_sb[sl2][:]).then_inc(sm[f"dao{sl2}"].h, 16)
            if qt == NQT - 1:
                ng = g + 1
                g_.wait_ge(sm["dao0"].h, daob[0] + 16 * ((ng + 1) // 2))
                g_.wait_ge(sm["dao1"].h, daob[1] + 16 * (ng // 2))
                g_.collective_compute(
                    "AllGather", ALU.bypass, replica_groups=[list(range(NCORE))],
                    ins=[ag_o_in[hh * 128:(hh + 1) * 128, :]],
                    outs=[ag_oo[hh]]).then_inc(sm["cc"].h, 1)

    block.sync(ph6_sync)
    block.tensor(ph6_pe)
    block.vector(ph6_vec)
    block.scalar(ph6_scal)
    block.gpsimd(ph6_gps)
    sm["dko"].v += 16 * (NCORE + 1)
    sm["dmisc"].v += 32
    sm["dqt0"].v += 16 * ((NG + 1) // 2)
    sm["dqt1"].v += 16 * (NG // 2)
    sm["pPE"].v += CSC[-1]
    sm["pPEv"].v = pev_run
    sm["cV"].v = cv_run
    sm["cA"].v = ca_run
    sm["dao0"].v += 16 * ((NG + 1) // 2)
    sm["dao1"].v += 16 * (NG // 2)
    sm["cc"].v += 4
    cc_o = sm["cc"].v
    BAR()
    FR(*ps_z[::-1]); FR(*ps_ao[::-1]); FR(*ps_sc[::-1])
    FR(ao_sb[1], ao_sb[0], lnz_sb[1], lnz_sb[0], zb_sb[1], zb_sb[0],
       qt6[1], qt6[0], prob_sb[3], prob_sb[2], prob_sb[1], prob_sb[0],
       bvt_sb, dT_sb, sDm2, sDm, sD0, dm2_sb, dm_sb, vt_sb, kf_sb)

    attn_sb = AL(nc.sbuf_tensor("attn_sb", [128, C * TC], BF16))
    dxgb7 = sm["dxg"].v

    at_sems = {3: "dxg", 2: "dmisc", 1: "dqt0", 0: "dqt1"}
    at_base = {hh: sm[at_sems[hh]].v for hh in range(4)}

    def ph7_sync_pre(s):
        pid = s.partition_id()
        for i, hh in enumerate((3, 2, 1, 0)):
            s.wait_ge(sm["cc"].h, cc_o - 3 + i)
            out_v = attn_sb[:].rearrange("p (r h t) -> p r h t", h=4, t=TC)[:, :, hh, :]
            in_v = ag_oo[hh][:, bass.ts(pid, TC)].rearrange("(r p) t -> p r t", p=128)
            s.dma_start(out=out_v, in_=in_v).then_inc(sm[at_sems[hh]].h, 16)
    block.sync(ph7_sync_pre)
    for hh in range(4):
        sm[at_sems[hh]].v += 16

    ps_s1c = AL(nc.psum_tensor("ps_s1c", [1, TC], F32))
    ps_s2c = AL(nc.psum_tensor("ps_s2c", [1, TC], F32))
    ps_o7 = [AL(nc.psum_tensor(f"ps_wo{i}", [128, 512], F32)) for i in range(4)]
    pPEb, cVb, cAb, cPLb = sm["pPE"].v, sm["cV"].v, sm["cA"].v, sm["cPL"].v
    dwb = [sm["dw0"].v, sm["dw1"].v]
    dxb = [sm["dx0"].v, sm["dx1"].v]
    dstb = [sm[f"dst{j}"].v for j in range(4)]
    NIT = C * NTH

    def it7(oc, th):
        return oc * NTH + th

    def ph7_pool_copies(g, po):
        for th in range(NTH):
            k = it7(po, th)
            b4 = k % 4
            g.wait_ge(sm["cV"].h, cVb + k + 1)
            if k >= 4:
                g.wait_ge(sm["pPE"].h, pPEb + k + 1)
            g.tensor_copy(xbs[b4][:], x1b[b4][:]).then_inc(sm["cPL"].h, 1)

    def ph7_gps_w(g):
        for oc in range(C):
            sl = oc % 2
            if oc >= 2:
                g.wait_ge(sm["pPE"].h, pPEb + it7(oc - 2, NTH - 1) + 1)
            g.dma_start(
                out=bwb[sl][:].rearrange("p (c o) -> p c o", c=C),
                in_=wc_p[:, oc * 128:(oc + 1) * 128].rearrange("(c p) o -> p c o", p=128),
            ).then_inc(sm[f"dw{sl}"].h, 16)
            if oc >= 1:
                ph7_pool_copies(g, oc - 1)
        ph7_pool_copies(g, C - 1)

    def ph7_sync(s):
        for j in range(4):
            s.wait_ge(sm[f"dst{j}"].h, dst_after_ffn1[j])
        for oc in range(C):
            sl = oc % 2
            if oc >= 2:
                s.wait_ge(sm["cV"].h, cVb + it7(oc - 2, NTH - 1) + 1)
            s.dma_start(out=xcb[sl][:], in_=xs1[oc * 128:(oc + 1) * 128, :]).then_inc(sm[f"dx{sl}"].h, 16)
            for th in range(NTH):
                b4 = it7(oc, th) % 4
                s.wait_ge(sm["cV"].h, cVb + it7(oc, th) + 1)
                s.dma_start(out=xs2[oc * 128:(oc + 1) * 128, th * 512:(th + 1) * 512],
                            in_=x1b[b4][:]).then_inc(sm[f"dst{b4}"].h, 16)

    cs_order = sorted(range(C), key=lambda c: (-(c % 4), c))

    def ph7_pe(t):
        for oc in range(C):
            sl = oc % 2
            t.wait_ge(sm[f"dw{sl}"].h, dwb[sl] + (oc // 2 + 1) * 16)
            for th in range(NTH):
                k = it7(oc, th)
                if k >= 4:
                    t.wait_ge(sm["cV"].h, cVb + (k - 4) + 1)
            for ci, c in enumerate(cs_order):
                if oc == 0 and (ci == 0 or c % 4 != cs_order[ci - 1] % 4):
                    t.wait_ge(sm[at_sems[c % 4]].h, at_base[c % 4] + 16)
                for th in range(NTH):
                    b4 = it7(oc, th) % 4
                    ins = t.matmul(ps_o7[b4][:], bwb[sl][:, c * 128:(c + 1) * 128],
                                   attn_sb[:, c * TC + th * 512: c * TC + th * 512 + 512],
                                   start=(ci == 0), stop=(ci == C - 1))
                    if ci == C - 1:
                        ins.then_inc(sm["pPE"].h, 1)
            if oc >= 1:
                po = oc - 1
                for th in range(NTH):
                    b4 = it7(po, th) % 4
                    t.wait_ge(sm["cPL"].h, cPLb + it7(po, th) + 1)
                    t.wait_ge(sm["cA"].h, cAb + it7(po, th) + 1)
                    t.matmul(ps_s1c[0:1, th * 512:(th + 1) * 512], onesb[:], xbs[b4][:],
                             start=(po == 0), stop=(po == C - 1))
                    t.matmul(ps_s2c[0:1, th * 512:(th + 1) * 512], onesb[:], xqs[b4][:],
                             start=(po == 0), stop=(po == C - 1))
        po = C - 1
        for th in range(NTH):
            b4 = it7(po, th) % 4
            t.wait_ge(sm["cPL"].h, cPLb + it7(po, th) + 1)
            t.wait_ge(sm["cA"].h, cAb + it7(po, th) + 1)
            t.matmul(ps_s1c[0:1, th * 512:(th + 1) * 512], onesb[:], xbs[b4][:],
                     start=(po == 0), stop=(po == C - 1))
            ins = t.matmul(ps_s2c[0:1, th * 512:(th + 1) * 512], onesb[:], xqs[b4][:],
                           start=(po == 0), stop=(po == C - 1))
        ins.then_inc(sm["pPE"].h, 1)

    def ph7_vec(v):
        for oc in range(C):
            sl = oc % 2
            v.wait_ge(sm[f"dx{sl}"].h, dxb[sl] + (oc // 2 + 1) * 16)
            for th in range(NTH):
                k = it7(oc, th)
                b4 = k % 4
                v.wait_ge(sm["pPE"].h, pPEb + k + 1)
                if k >= 4:
                    v.wait_ge(sm[f"dst{b4}"].h, dstb[b4] + ((k - 4) // 4 + 1) * 16)
                    v.wait_ge(sm["cPL"].h, cPLb + (k - 4) + 1)
                    v.wait_ge(sm["cA"].h, cAb + (k - 4) + 1)
                v.scalar_tensor_tensor(x1b[b4][:], ps_o7[b4][:], 1.0,
                                       xcb[sl][:, th * 512:(th + 1) * 512],
                                       ALU.mult, ALU.add).then_inc(sm["cV"].h, 1)

    def ph7_scal(sc):
        for oc in range(C):
            for th in range(NTH):
                k = it7(oc, th)
                b4 = k % 4
                sc.wait_ge(sm["cV"].h, cVb + k + 1)
                sc.activation(xqs[b4][:], x1b[b4][:], AF.Square).then_inc(sm["cA"].h, 1)

    block.gpsimd(ph7_gps_w)
    block.sync(ph7_sync)
    block.tensor(ph7_pe)
    block.vector(ph7_vec)
    block.scalar(ph7_scal)
    sm["dw0"].v += 16 * ((C + 1) // 2)
    sm["dw1"].v += 16 * (C // 2)
    sm["dx0"].v += 16 * ((C + 1) // 2)
    sm["dx1"].v += 16 * (C // 2)
    for j in range(4):
        sm[f"dst{j}"].v += 16 * ((NIT - j + 3) // 4)
    sm["pPE"].v += NIT + 1
    sm["cV"].v += NIT
    sm["cA"].v += NIT
    sm["cPL"].v += NIT
    dst_after_wc = {j: sm[f"dst{j}"].v for j in range(4)}
    BAR()
    FR(*ps_o7[::-1])
    ps_rb3 = AL(nc.psum_tensor("ps_rb3", [128, TC], F32))
    ps_mb3 = AL(nc.psum_tensor("ps_mb3", [128, TC], F32))
    ln_rowmath_bcast(ps_s1c, ps_s2c, ps_rb3, ps_mb3)
    BAR()
    ln_apply(xs2, attn_sb, ps_rb3, ps_mb3, dst_wait=dst_after_wc)
    BAR()
    FR(ps_mb3, ps_rb3, ps_s2c, ps_s1c)

    ffn(attn_sb, xs2, out_f, None, None, dst_wait=dst_after_wc)

    dst_total = {j: sm[f"dst{j}"].v for j in range(4)}

    def fin_sync(s):
        for j in range(4):
            s.wait_ge(sm[f"dst{j}"].h, dst_total[j])
    block.sync(fin_sync)

    blk_cm.__exit__(None, None, None)
    for space in ("psum", "sbuf"):
        while _stacks[space]:
            h = _stacks[space].pop()
            _cm_of.pop(id(h)).__exit__(None, None, None)
    return nc


def _consts_np():
    ident = np.eye(128, dtype=np.float32)
    allones = np.ones((128, 128), dtype=np.float32)
    kk = np.arange(128, dtype=np.float32)[:, None]
    qq = np.arange(512, dtype=np.float32)[None, :]
    D0 = (kk - qq).astype(np.float32)
    deltaB = np.zeros((128, 72), dtype=np.float32)
    for mm in range(72):
        deltaB[:, mm] = -128.0 * mm
    CB = np.zeros((128, 72), dtype=np.float32)
    for mm in range(72):
        CB[:, mm] = np.arange(128) - 128.0 * mm - 511.0
    return np.ascontiguousarray(
        np.concatenate([ident, allones, D0, deltaB, CB], axis=1))


def _dm_np():
    kk = np.arange(128, dtype=np.float32)[:, None]
    qq = np.arange(512, dtype=np.float32)[None, :]
    Dms = []
    for jm in range(4):
        d = kk - qq + 128.0 * jm
        Dms.append(np.where(d <= 0.0, d, -1e13).astype(np.float32))
    return np.ascontiguousarray(np.concatenate(Dms, axis=1))


def _dm2_np():
    kk = np.arange(128, dtype=np.float32)[:, None]
    qq = np.arange(512, dtype=np.float32)[None, :]
    Dms = []
    for jm in range(4):
        valid = (kk - qq + 128.0 * jm) <= 0.0
        val = np.broadcast_to(kk + 128.0 * jm - 511.0, (128, 512))
        Dms.append(np.where(valid, val, -1e13).astype(np.float32))
    return np.ascontiguousarray(np.concatenate(Dms, axis=1))


def _e2_np():
    e2 = np.zeros((128, 128), dtype=np.float32)
    for e in range(NEXP):
        e2[e, e * R:(e + 1) * R] = 2.0
    return e2


def shard_inputs(S, L0_attn_hidden_states, L0_attn_w_qkv, L0_attn_w_c,
                 L0_ffn_gate_w, L0_ffn_base_w, L0_ffn_base_b, L0_ffn_lora_A, L0_ffn_lora_B):
    TC = S // NCORE
    x = np.ascontiguousarray(np.asarray(L0_attn_hidden_states, dtype=np.float32).reshape(S, D))
    wqkv = np.asarray(L0_attn_w_qkv, dtype=np.float32)
    wc = np.asarray(L0_attn_w_c, dtype=np.float32)
    gate_w = np.asarray(L0_ffn_gate_w, dtype=np.float32)
    base_w = np.asarray(L0_ffn_base_w, dtype=np.float32)
    base_b = np.asarray(L0_ffn_base_b, dtype=np.float32)
    lora_A = np.asarray(L0_ffn_lora_A, dtype=np.float32)
    lora_B = np.asarray(L0_ffn_lora_B, dtype=np.float32)

    basewT_np = np.ascontiguousarray(base_w.T)
    wkv_np = np.ascontiguousarray(wqkv[:, D:D + 256])
    gate_np = np.ascontiguousarray(gate_w.T)
    lat_np = np.ascontiguousarray(lora_A.reshape(NEXP * R, D).T)
    lbt_np = np.ascontiguousarray(lora_B.transpose(0, 2, 1).reshape(NEXP * R, D))
    bias_np = np.ascontiguousarray(base_b.reshape(C, 128).T)
    consts_np = _consts_np()
    e2_np = _e2_np()
    dm_np = _dm_np()
    dm2_np = _dm2_np()

    perm = np.zeros(D, dtype=np.int64)
    for r in range(NCORE):
        for hh in range(4):
            h = 8 * hh + r
            perm[r * 512 + hh * 128: r * 512 + (hh + 1) * 128] = np.arange(h * DH, (h + 1) * DH)
    wc_p_np = np.ascontiguousarray(wc[perm, :])

    in_maps = []
    for i in range(NCORE):
        heads = [8 * j + i for j in range(4)]
        wq_i = np.ascontiguousarray(
            np.concatenate([wqkv[:, h * DH:(h + 1) * DH] for h in heads], axis=1))
        slope4_np = np.zeros((128, 4), dtype=np.float32)
        for j, h in enumerate(heads):
            slope4_np[:, j] = SLOPES[h]
        x_i = np.ascontiguousarray(x[i * TC:(i + 1) * TC, :].T)
        in_maps.append(dict(
            x_f=x_i, basewT=basewT_np, wkv=wkv_np, wq=wq_i, wc_p=wc_p_np,
            gate_wT=gate_np, lora_AT=lat_np, lora_BT=lbt_np,
            bias_fm=bias_np, slope4=slope4_np, consts=consts_np, e2_in=e2_np, dm_in=dm_np, dm2_in=dm2_np,
        ))
    return in_maps


def assemble_output(S, results):
    TC = S // NCORE
    out = np.empty((S, D), dtype=np.float32)
    for i in range(NCORE):
        out[i * TC:(i + 1) * TC, :] = np.asarray(results[i]["out_f"]).T
    return out.reshape(1, S, D)


def kernel(**inputs):
    S = int(np.asarray(inputs["L0_attn_hidden_states"]).shape[1])
    in_maps = shard_inputs(S, **inputs)
    nc = build_program(S=S)
    res = run_bass_kernel_spmd(nc, in_maps, core_ids=list(range(NCORE)))
    return assemble_output(S, res.results)


# version 13
# speedup vs baseline: 1.0696x; 1.0037x over previous
import math
import numpy as np

import concourse.bass as bass
import concourse.mybir as mybir
from concourse.bass_utils import run_bass_kernel_spmd

F32 = mybir.dt.float32
BF16 = mybir.dt.bfloat16
AF = mybir.ActivationFunctionType
ALU = mybir.AluOpType
AX = mybir.AxisListType.X

D = 4096
C = D // 128
NCORE = 8
H = 32
DH = 128
NEXP = 8
R = 16
EPS = 1e-5
QSCALE = 1.0 / math.sqrt(DH)
T_WIN = 70.0

SLOPES = [2.0 ** (-(h + 1) / 4.0) for h in range(H)]
SLOT_W = [T_WIN / SLOPES[8 * j + 7] for j in range(4)]


def kt_lo(slot, qt):
    return max(0, int(math.floor((512.0 * qt - SLOT_W[slot]) / 128.0)))


class Sem:
    def __init__(self, nc, name):
        self._cm = nc.semaphore(name)
        self.h = self._cm.__enter__()
        self.v = 0


def build_program(S=8192):
    TC = S // NCORE
    NTH = TC // 512
    NQT = S // 512
    NT128 = TC // 128
    assert TC % 512 == 0

    nc = bass.Bass("TRN2", target_bir_lowering=False, debug=False)

    x_f = nc.dram_tensor("x_f", [D, TC], F32, kind="ExternalInput").ap()
    basewT = nc.dram_tensor("basewT", [D, D], F32, kind="ExternalInput").ap()
    wkv_in = nc.dram_tensor("wkv", [D, 256], F32, kind="ExternalInput").ap()
    wq_in = nc.dram_tensor("wq", [D, 512], F32, kind="ExternalInput").ap()
    wc_p = nc.dram_tensor("wc_p", [D, D], F32, kind="ExternalInput").ap()
    gate_wT = nc.dram_tensor("gate_wT", [D, NEXP], F32, kind="ExternalInput").ap()
    lora_AT = nc.dram_tensor("lora_AT", [D, NEXP * R], F32, kind="ExternalInput").ap()
    lora_BT = nc.dram_tensor("lora_BT", [NEXP * R, D], F32, kind="ExternalInput").ap()
    bias_fm = nc.dram_tensor("bias_fm", [128, C], F32, kind="ExternalInput").ap()
    slope4_in = nc.dram_tensor("slope4", [128, 4], F32, kind="ExternalInput").ap()
    e2_in = nc.dram_tensor("e2_in", [128, 128], F32, kind="ExternalInput").ap()
    NCONST = 128 + 128
    consts_in = nc.dram_tensor("consts", [128, NCONST], F32, kind="ExternalInput").ap()
    DMLEN = 2048 + 512 + 72 + 72
    dm_in = nc.dram_tensor("dm_in", [128, DMLEN], F32, kind="ExternalInput").ap()
    dm2_in = nc.dram_tensor("dm2_in", [128, 4 * 512], F32, kind="ExternalInput").ap()
    O_ID, O_ONE = 0, 128
    M_DM, M_D0, M_DB, M_CB = 0, 2048, 2560, 2632

    out_f = nc.dram_tensor("out_f", [D, TC], F32, kind="ExternalOutput").ap()

    xs1 = nc.dram_tensor("xs1", [D, TC], F32).ap()
    xs2 = nc.dram_tensor("xs2", [D, TC], F32).ap()
    qdram = nc.dram_tensor("qdram", [512, S], BF16).ap()
    ag_x_in = nc.dram_tensor("ag_x_in", [D, TC], BF16).ap()
    ag_xo = [nc.dram_tensor(f"ag_xo{q}", [NCORE * 1024, TC], BF16, addr_space="Shared").ap()
             for q in range(4)]
    ag_k_in = nc.dram_tensor("ag_k_in", [128, TC], BF16).ap()
    ag_k_out = nc.dram_tensor("ag_k_out", [NCORE * 128, TC], BF16, addr_space="Shared").ap()
    agv_in = nc.dram_tensor("agv_in", [TC, 128], BF16).ap()
    agv_out = nc.dram_tensor("agv_out", [S, 128], BF16, addr_space="Shared").ap()
    ag_o_in = nc.dram_tensor("ag_o_in", [512, S], BF16).ap()
    ag_oo = [nc.dram_tensor(f"ag_oo{h}", [NCORE * 128, S], BF16, addr_space="Shared").ap()
             for h in range(4)]

    _stacks = {"sbuf": [], "psum": []}
    _cm_of = {}

    def AL(cm):
        h = cm.__enter__()
        space = "psum" if "PSum" in type(h).__name__ else "sbuf"
        _stacks[space].append(h)
        _cm_of[id(h)] = cm
        return h

    def FR(*hs):
        for h in hs:
            space = "psum" if "PSum" in type(h).__name__ else "sbuf"
            top = _stacks[space].pop()
            assert top is h, f"non-LIFO free: {getattr(h, 'name', h)} vs {getattr(top, 'name', top)}"
            _cm_of.pop(id(h)).__exit__(None, None, None)

    consts_sb = AL(nc.sbuf_tensor("consts_sb", [128, NCONST], F32))
    e2_sb = AL(nc.sbuf_tensor("e2_sb", [128, 128], BF16))
    bias05 = AL(nc.sbuf_tensor("bias05", [128, C], F32))
    slope4 = AL(nc.sbuf_tensor("slope4_sb", [128, 4], F32))
    onesb = AL(nc.sbuf_tensor("onesb", [128, 1], BF16))
    ones128b = AL(nc.sbuf_tensor("ones128b", [128, 128], BF16))
    c5_sb = AL(nc.sbuf_tensor("c5_sb", [128, 1], F32))
    gate_sb = AL(nc.sbuf_tensor("gate_sb", [128, C * NEXP], BF16))
    lat_sb = AL(nc.sbuf_tensor("lat_sb", [128, C * 128], BF16))
    lbt_sb = AL(nc.sbuf_tensor("lbt_sb", [128, D], BF16))
    xcb = [AL(nc.sbuf_tensor(f"xcb{i}", [128, TC], F32)) for i in range(4)]
    bwb = [AL(nc.sbuf_tensor(f"bwb{i}", [128, C * 128], BF16)) for i in range(2)]
    x1b = [AL(nc.sbuf_tensor(f"x1b{i}", [128, 512], F32)) for i in range(4)]
    mean_r = AL(nc.sbuf_tensor("mean_r", [1, TC], F32))
    rstd_r = AL(nc.sbuf_tensor("rstd_r", [1, TC], F32))
    m2_r = xcb[0][0:1, 0:TC]
    lg_sb = [AL(nc.sbuf_tensor(f"lg{i}", [128, NEXP], F32)) for i in range(2)]
    m1_sb = AL(nc.sbuf_tensor("m1_sb", [128, 1], F32))
    m2s_sb = AL(nc.sbuf_tensor("m2s_sb", [128, 1], F32))
    eq1_sb = AL(nc.sbuf_tensor("eq1_sb", [128, NEXP], F32))
    eq2_sb = AL(nc.sbuf_tensor("eq2_sb", [128, NEXP], F32))
    msk_sb = AL(nc.sbuf_tensor("msk_sb", [128, NEXP], F32))
    n1_sb = AL(nc.sbuf_tensor("n1_sb", [128, 1], F32))
    n2_sb = AL(nc.sbuf_tensor("n2_sb", [128, 1], F32))
    d21_sb = AL(nc.sbuf_tensor("d21_sb", [128, 1], F32))
    w2_sb = AL(nc.sbuf_tensor("w2_sb", [128, 1], F32))
    w1_sb = AL(nc.sbuf_tensor("w1_sb", [128, 1], F32))
    tie_sb = AL(nc.sbuf_tensor("tie_sb", [128, 1], F32))
    r1_sb = AL(nc.sbuf_tensor("r1_sb", [128, 1], F32))
    r2_sb = AL(nc.sbuf_tensor("r2_sb", [128, 1], F32))
    wfa_sb = AL(nc.sbuf_tensor("wfa_sb", [128, NEXP], F32))
    wf_all = AL(nc.sbuf_tensor("wf_all", [128, NT128 * NEXP], F32))
    wfT_sb = AL(nc.sbuf_tensor("wfT_sb", [8, TC], BF16))
    wrep_sb = AL(nc.sbuf_tensor("wrep_sb", [128, TC], BF16))
    mid_sb = AL(nc.sbuf_tensor("mid_sb", [128, TC], BF16))
    kc_sb = AL(nc.sbuf_tensor("kc_sb", [128, TC], BF16))
    vc_sb = AL(nc.sbuf_tensor("vc_sb", [128, TC], BF16))
    if TC >= 1024:
        t1b = [xcb[2][:, 0:512], xcb[2][:, 512:1024], xcb[3][:, 0:512], xcb[3][:, 512:1024]]
        xqs = [kc_sb[:, 0:512], kc_sb[:, 512:1024], vc_sb[:, 0:512], vc_sb[:, 512:1024]]
    else:
        t1b = [AL(nc.sbuf_tensor(f"t1b{i}", [128, 512], F32)) for i in range(4)]
        xqs = [AL(nc.sbuf_tensor(f"xqs{i}", [128, 512], BF16)) for i in range(4)]

    sm = {n: Sem(nc, n) for n in [
        "dmisc", "dmiscg", "dx0", "dx1", "dx2", "dx3", "dw0", "dw1",
        "dst0", "dst1", "dst2", "dst3", "dag", "dao0", "dao1",
        "dxg", "dwq", "dko", "dqt0", "dqt1", "dq0", "dq1", "cc", "pPE", "pPEv", "cV", "cA", "cPL"]}

    blk_cm = nc.Block()
    block = blk_cm.__enter__()

    bar = Sem(nc, "bar")

    def BAR():
        bar.v += 5
        target = bar.v
        for sec in (block.sync, block.gpsimd, block.tensor, block.vector, block.scalar):
            def f(eng, target=target):
                eng.drain().then_inc(bar.h, 1)
                eng.wait_ge(bar.h, target)
            sec(f)

    def ph0_sync(s):
        s.dma_start(out=consts_sb[:], in_=consts_in).then_inc(sm["dmisc"].h, 16)
        s.dma_start(out=bias05[:], in_=bias_fm).then_inc(sm["dmisc"].h, 16)
        s.dma_start(out=slope4[:], in_=slope4_in).then_inc(sm["dmisc"].h, 16)
    sm["dmisc"].v += 48

    def ph0_gps(g):
        g.dma_start(out=gate_sb[:].rearrange("p (c e) -> p c e", c=C),
                    in_=gate_wT.rearrange("(c p) e -> p c e", p=128)).then_inc(sm["dmiscg"].h, 16)
        g.dma_start(out=lat_sb[:].rearrange("p (c r) -> p c r", c=C),
                    in_=lora_AT.rearrange("(c p) r -> p c r", p=128)).then_inc(sm["dmiscg"].h, 16)
        g.dma_start(out=lbt_sb[:], in_=lora_BT).then_inc(sm["dmiscg"].h, 16)
        g.dma_start(out=e2_sb[:], in_=e2_in).then_inc(sm["dmiscg"].h, 16)
        g.memset(onesb[:], 1.0)
        g.memset(ones128b[:], 1.0)
        g.memset(c5_sb[:], -5.0)
    sm["dmiscg"].v += 64

    def ph0_vec(v):
        v.wait_ge(sm["dmisc"].h, 48)
        v.tensor_scalar_mul(bias05[:], bias05[:], 0.5)

    block.sync(ph0_sync)
    block.gpsimd(ph0_gps)
    block.vector(ph0_vec)
    BAR()

    def ln_stream_stats(src_ap, xbT, ps_s1, ps_s2, dst_wait=None):
        dxb = [sm[f"dx{j}"].v for j in range(4)]
        cPLb, cAb, pPEb = sm["cPL"].v, sm["cA"].v, sm["pPE"].v

        def xq_of(c, th):
            return (c % 2) * 2 + th if NTH > 1 else (c % 2)

        def f_sync(s):
            if dst_wait is not None:
                for j in range(4):
                    s.wait_ge(sm[f"dst{j}"].h, dst_wait[j])
            for c in range(C):
                sl = c % 4
                if c >= 4:
                    s.wait_ge(sm["cPL"].h, cPLb + (c - 4) + 1)
                    s.wait_ge(sm["cA"].h, cAb + (c - 4) + 1)
                s.dma_start(out=xcb[sl][:], in_=src_ap[c * 128:(c + 1) * 128, :]).then_inc(sm[f"dx{sl}"].h, 16)

        def f_pool(g):
            for c in range(C):
                sl = c % 4
                g.wait_ge(sm[f"dx{sl}"].h, dxb[sl] + (c // 4 + 1) * 16)
                ins = g.tensor_copy(xbT[:, c * TC:(c + 1) * TC], xcb[sl][:])
                ins.then_inc(sm["cPL"].h, 1)

        def f_scal(sc):
            for c in range(C):
                sl = c % 4
                sc.wait_ge(sm[f"dx{sl}"].h, dxb[sl] + (c // 4 + 1) * 16)
                if c >= 2:
                    sc.wait_ge(sm["pPE"].h, pPEb + (c - 2) + 1)
                for th in range(NTH):
                    ins = sc.activation(xqs[xq_of(c, th)][:], xcb[sl][:, th * 512:(th + 1) * 512], AF.Square)
                    if th == NTH - 1:
                        ins.then_inc(sm["cA"].h, 1)

        def f_pe(t):
            for c in range(C):
                t.wait_ge(sm["cPL"].h, cPLb + c + 1)
                t.wait_ge(sm["cA"].h, cAb + c + 1)
                for th in range(NTH):
                    t.matmul(ps_s1[0:1, th * 512:(th + 1) * 512], onesb[:],
                             xbT[:, c * TC + th * 512: c * TC + (th + 1) * 512],
                             start=(c == 0), stop=(c == C - 1))
                    ins = t.matmul(ps_s2[0:1, th * 512:(th + 1) * 512], onesb[:], xqs[xq_of(c, th)][:],
                                   start=(c == 0), stop=(c == C - 1))
                ins.then_inc(sm["pPE"].h, 1)

        block.sync(f_sync)
        block.gpsimd(f_pool)
        block.scalar(f_scal)
        block.tensor(f_pe)
        for j in range(4):
            sm[f"dx{j}"].v += 16 * ((C - j + 3) // 4)
        sm["cPL"].v += C
        sm["cA"].v += C
        sm["pPE"].v += C

    def ln_rowmath_bcast(ps_s1, ps_s2, ps_rb, ps_mb):
        pPEb, cVb, cAb = sm["pPE"].v, sm["cV"].v, sm["cA"].v

        def f_vec(v):
            v.wait_ge(sm["pPE"].h, pPEb)
            v.tensor_scalar_mul(mean_r[:], ps_s1[:], 1.0 / D).then_inc(sm["cV"].h, 1)
            v.wait_ge(sm["cA"].h, cAb + 1)
            v.scalar_tensor_tensor(rstd_r[:], ps_s2[:], 1.0 / D, rstd_r[:], ALU.mult, ALU.subtract).then_inc(sm["cV"].h, 1)
            v.wait_ge(sm["cV"].h, cVb + 2)
            v.tensor_scalar_add(rstd_r[:], rstd_r[:], EPS).then_inc(sm["cV"].h, 1)
            v.wait_ge(sm["cA"].h, cAb + 3)
            v.tensor_mul(mean_r[:], mean_r[:], rstd_r[:]).then_inc(sm["cV"].h, 1)

        def f_scal(sc):
            sc.wait_ge(sm["pPE"].h, pPEb)
            sc.activation(rstd_r[:], ps_s1[:], AF.Square, scale=1.0 / D).then_inc(sm["cA"].h, 1)
            sc.wait_ge(sm["cV"].h, cVb + 3)
            sc.activation(m2_r[:], rstd_r[:], AF.Ln).then_inc(sm["cA"].h, 1)
            sc.wait_ge(sm["cA"].h, cAb + 2)
            sc.activation(rstd_r[:], m2_r[:], AF.Exp, scale=-0.5).then_inc(sm["cA"].h, 1)

        def f_pe(t):
            t.wait_ge(sm["cV"].h, cVb + 4)
            for th in range(NTH):
                t.matmul(ps_rb[:, th * 512:(th + 1) * 512], consts_sb[0:1, O_ONE:O_ONE + 128],
                         rstd_r[0:1, th * 512:(th + 1) * 512], start=True, stop=True)
                ins = t.matmul(ps_mb[:, th * 512:(th + 1) * 512], consts_sb[0:1, O_ONE:O_ONE + 128],
                               mean_r[0:1, th * 512:(th + 1) * 512], start=True, stop=True)
            ins.then_inc(sm["pPE"].h, 1)

        block.vector(f_vec)
        block.scalar(f_scal)
        block.tensor(f_pe)
        sm["cV"].v += 4
        sm["cA"].v += 3
        sm["pPE"].v += 1

    def ln_apply(xbT, ps_rb, ps_mb, also_ag=None):
        cVb, pPEb = sm["cV"].v, sm["pPE"].v

        def f_vec(v):
            v.wait_ge(sm["pPE"].h, pPEb)
            for c in range(C):
                sl = c % 4
                v.tensor_mul(xcb[sl][:], xbT[:, c * TC:(c + 1) * TC], ps_rb[:]).then_inc(sm["cV"].h, 1)
                v.wait_ge(sm["cV"].h, cVb + 2 * c + 1)
                v.tensor_sub(xbT[:, c * TC:(c + 1) * TC], xcb[sl][:], ps_mb[:]).then_inc(sm["cV"].h, 1)

        block.vector(f_vec)
        sm["cV"].v += 2 * C

        if also_ag is not None:
            dagb = sm["dag"].v

            def f_gps(g):
                for c in range(C):
                    g.wait_ge(sm["cV"].h, cVb + 2 * c + 2)
                    g.dma_start(out=also_ag[c * 128:(c + 1) * 128, :],
                                in_=xbT[:, c * TC:(c + 1) * TC]).then_inc(sm["dag"].h, 16)
                    if c % 8 == 7:
                        q = c // 8
                        g.wait_ge(sm["dag"].h, dagb + 16 * (c + 1))
                        g.collective_compute(
                            "AllGather", ALU.bypass, replica_groups=[list(range(NCORE))],
                            ins=[also_ag[q * 1024:(q + 1) * 1024, :]],
                            outs=[ag_xo[q]]).then_inc(sm["cc"].h, 1)
            block.gpsimd(f_gps)
            sm["dag"].v += 16 * C
            sm["cc"].v += 4

    _ffn_n = [0]

    def ffn(xnT, resid_ap, out_ap, ps_s1, ps_s2, xbT2=None, dst_wait=None):
        tag = _ffn_n[0]
        _ffn_n[0] += 1
        do_stats = ps_s1 is not None
        ps_g = AL(nc.psum_tensor(f"ps_g{tag}", [128, NEXP], F32))
        ps_wfT = AL(nc.psum_tensor(f"ps_wfT{tag}", [8, 128], F32))
        ps_wrep = AL(nc.psum_tensor(f"ps_wrep{tag}", [128, 512], F32))
        ps_low = AL(nc.psum_tensor(f"ps_low{tag}", [128, 512], F32))
        pPEb, cVb, cAb = sm["pPE"].v, sm["cV"].v, sm["cA"].v
        NSTEP = 19

        def f_pe2(t):
            t.wait_ge(sm["dmiscg"].h, 64)
            t.wait_ge(sm["dmisc"].h, 48)
            for tt in range(NT128):
                if tt > 0:
                    t.wait_ge(sm["cV"].h, cVb + NSTEP * (tt - 1) + 1)
                for c in range(C):
                    ins = t.matmul(ps_g[:], xnT[:, c * TC + tt * 128: c * TC + (tt + 1) * 128],
                                   gate_sb[:, c * NEXP:(c + 1) * NEXP],
                                   start=(c == 0), stop=(c == C - 1))
                ins.then_inc(sm["pPE"].h, 1)
            for tt in range(NT128):
                t.wait_ge(sm["cV"].h, cVb + NSTEP * tt + NSTEP)
                if tt > 0:
                    t.wait_ge(sm["cV"].h, cVb + NSTEP * NT128 + tt)
                t.transpose(ps_wfT[:], wf_all[:, tt * NEXP:(tt + 1) * NEXP],
                            consts_sb[:, O_ID:O_ID + 128]).then_inc(sm["pPE"].h, 1)
            for th in range(NTH):
                t.wait_ge(sm["cV"].h, cVb + NSTEP * NT128 + NT128)
                if th > 0:
                    t.wait_ge(sm["cV"].h, cVb + NSTEP * NT128 + NT128 + th)
                t.matmul(ps_wrep[:], e2_sb[0:8, :], wfT_sb[0:8, th * 512:(th + 1) * 512],
                         start=True, stop=True).then_inc(sm["pPE"].h, 1)
            for th in range(NTH):
                t.wait_ge(sm["cV"].h, cVb + NSTEP * NT128 + NT128 + NTH)
                if th > 0:
                    t.wait_ge(sm["cV"].h, cVb + NSTEP * NT128 + NT128 + NTH + th)
                for c in range(C):
                    ins = t.matmul(ps_low[:], lat_sb[:, c * 128:(c + 1) * 128],
                                   xnT[:, c * TC + th * 512: c * TC + th * 512 + 512],
                                   start=(c == 0), stop=(c == C - 1))
                ins.then_inc(sm["pPE"].h, 1)

        def f_vec2(v):
            n = cVb
            for tt in range(NT128):
                sl = tt % 2
                v.wait_ge(sm["pPE"].h, pPEb + tt + 1)

                def step(ins):
                    nonlocal n
                    n += 1
                    ins.then_inc(sm["cV"].h, 1)
                    v.wait_ge(sm["cV"].h, n)
                step(v.tensor_copy(lg_sb[sl][:], ps_g[:]))
                step(v.reduce_max(m1_sb[:], lg_sb[sl][:], axis=AX))
                step(v.tensor_scalar(eq1_sb[:], lg_sb[sl][:], m1_sb[:, 0:1], None, ALU.is_ge))
                step(v.scalar_tensor_tensor(msk_sb[:], eq1_sb[:], -1e30, lg_sb[sl][:], ALU.mult, ALU.add))
                step(v.reduce_sum(n1_sb[:], eq1_sb[:], axis=AX))
                step(v.reduce_max(m2s_sb[:], msk_sb[:], axis=AX))
                step(v.tensor_scalar(eq2_sb[:], lg_sb[sl][:], m2s_sb[:, 0:1], None, ALU.is_equal))
                step(v.tensor_sub(d21_sb[:], m2s_sb[:], m1_sb[:]))
                step(v.reduce_sum(n2_sb[:], eq2_sb[:], axis=AX))
                step(v.tensor_scalar(tie_sb[:], n1_sb[:], 1.5, None, ALU.is_ge))
                step(v.reciprocal(r1_sb[:], n1_sb[:]))
                step(v.reciprocal(r2_sb[:], n2_sb[:]))
                v.wait_ge(sm["cA"].h, cAb + tt + 1)
                step(v.tensor_scalar(w1_sb[:], tie_sb[:], -1.0, 1.0, ALU.mult, ALU.add))
                step(v.tensor_mul(w2_sb[:], w2_sb[:], w1_sb[:]))
                step(v.tensor_scalar(w1_sb[:], w2_sb[:], -1.0, 1.0, ALU.mult, ALU.add))
                step(v.tensor_mul(w1_sb[:], w1_sb[:], r1_sb[:]))
                step(v.tensor_mul(w2_sb[:], w2_sb[:], r2_sb[:]))
                step(v.tensor_scalar(wfa_sb[:], eq1_sb[:], w1_sb[:, 0:1], None, ALU.mult))
                step(v.scalar_tensor_tensor(wf_all[:, tt * NEXP:(tt + 1) * NEXP], eq2_sb[:],
                                            w2_sb[:, 0:1], wfa_sb[:], ALU.mult, ALU.add))
            for tt in range(NT128):
                v.wait_ge(sm["pPE"].h, pPEb + NT128 + tt + 1)
                n += 1
                v.tensor_copy(wfT_sb[0:8, tt * 128:(tt + 1) * 128], ps_wfT[:]).then_inc(sm["cV"].h, 1)
                v.wait_ge(sm["cV"].h, n)
            for th in range(NTH):
                v.wait_ge(sm["pPE"].h, pPEb + 2 * NT128 + th + 1)
                n += 1
                v.tensor_copy(wrep_sb[:, th * 512:(th + 1) * 512], ps_wrep[:]).then_inc(sm["cV"].h, 1)
                v.wait_ge(sm["cV"].h, n)
            for th in range(NTH):
                v.wait_ge(sm["pPE"].h, pPEb + 2 * NT128 + NTH + th + 1)
                n += 1
                v.tensor_mul(mid_sb[:, th * 512:(th + 1) * 512], ps_low[:],
                             wrep_sb[:, th * 512:(th + 1) * 512]).then_inc(sm["cV"].h, 1)
                v.wait_ge(sm["cV"].h, n)

        def f_scal2(sc):
            for tt in range(NT128):
                sc.wait_ge(sm["cV"].h, cVb + tt * NSTEP + 8)
                sc.activation(w2_sb[:], d21_sb[:], AF.Sigmoid).then_inc(sm["cA"].h, 1)

        block.tensor(f_pe2)
        block.vector(f_vec2)
        block.scalar(f_scal2)
        sm["pPE"].v += 2 * NT128 + 2 * NTH
        sm["cV"].v += NSTEP * NT128 + NT128 + 2 * NTH
        sm["cA"].v += NT128
        BAR()
        FR(ps_low, ps_wrep, ps_wfT, ps_g)

        ps_o = [AL(nc.psum_tensor(f"ps_fo{tag}_{i}", [128, 512], F32)) for i in range(4)]
        pPEb, cVb, cAb, cPLb = sm["pPE"].v, sm["cV"].v, sm["cA"].v, sm["cPL"].v
        dwb = [sm["dw0"].v, sm["dw1"].v]
        dxb = [sm["dx0"].v, sm["dx1"].v]
        dstb = [sm[f"dst{j}"].v for j in range(4)]
        NIT = C * NTH
        CAK = 2 if do_stats else 1

        def it(oc, th):
            return oc * NTH + th

        def pool_copies(g, po):
            for th in range(NTH):
                k = it(po, th)
                b4 = k % 4
                g.wait_ge(sm["cV"].h, cVb + k + 1)
                g.tensor_copy(xbT2[:, po * TC + th * 512: po * TC + (th + 1) * 512],
                              x1b[b4][:]).then_inc(sm["cPL"].h, 1)

        def f_gps_w(g):
            for oc in range(C):
                sl = oc % 2
                if oc >= 2:
                    g.wait_ge(sm["pPE"].h, pPEb + it(oc - 2, NTH - 1) + 1)
                g.dma_start(
                    out=bwb[sl][:].rearrange("p (c o) -> p c o", c=C),
                    in_=basewT[:, oc * 128:(oc + 1) * 128].rearrange("(c p) o -> p c o", p=128),
                ).then_inc(sm[f"dw{sl}"].h, 16)
                if do_stats and oc >= 1:
                    pool_copies(g, oc - 1)
            if do_stats:
                pool_copies(g, C - 1)

        def f_sync(s):
            if dst_wait is not None:
                for j in range(4):
                    s.wait_ge(sm[f"dst{j}"].h, dst_wait[j])
            for oc in range(C):
                sl = oc % 2
                if oc >= 2:
                    s.wait_ge(sm["cV"].h, cVb + it(oc - 2, NTH - 1) + 1)
                s.dma_start(out=xcb[sl][:], in_=resid_ap[oc * 128:(oc + 1) * 128, :]).then_inc(sm[f"dx{sl}"].h, 16)
                for th in range(NTH):
                    b4 = it(oc, th) % 4
                    s.wait_ge(sm["cV"].h, cVb + it(oc, th) + 1)
                    s.dma_start(out=out_ap[oc * 128:(oc + 1) * 128, th * 512:(th + 1) * 512],
                                in_=x1b[b4][:]).then_inc(sm[f"dst{b4}"].h, 16)

        def f_pe(t):
            for oc in range(C):
                sl = oc % 2
                t.wait_ge(sm[f"dw{sl}"].h, dwb[sl] + (oc // 2 + 1) * 16)
                for th in range(NTH):
                    k = it(oc, th)
                    if k >= 4:
                        t.wait_ge(sm["cA"].h, cAb + CAK * (k - 4) + 1)
                for c in range(C):
                    for th in range(NTH):
                        b4 = it(oc, th) % 4
                        t.matmul(ps_o[b4][:], bwb[sl][:, c * 128:(c + 1) * 128],
                                 xnT[:, c * TC + th * 512: c * TC + th * 512 + 512],
                                 start=(c == 0), stop=False)
                for th in range(NTH):
                    b4 = it(oc, th) % 4
                    t.matmul(ps_o[b4][:], lbt_sb[:, oc * 128:(oc + 1) * 128],
                             mid_sb[:, th * 512:(th + 1) * 512],
                             start=False, stop=True).then_inc(sm["pPE"].h, 1)
                if do_stats and oc >= 1:
                    po = oc - 1
                    for th in range(NTH):
                        b4 = it(po, th) % 4
                        t.wait_ge(sm["cPL"].h, cPLb + it(po, th) + 1)
                        t.wait_ge(sm["cA"].h, cAb + CAK * it(po, th) + 2)
                        t.matmul(ps_s1[0:1, th * 512:(th + 1) * 512], onesb[:],
                                 xbT2[:, po * TC + th * 512: po * TC + (th + 1) * 512],
                                 start=(po == 0), stop=(po == C - 1))
                        t.matmul(ps_s2[0:1, th * 512:(th + 1) * 512], onesb[:], xqs[b4][:],
                                 start=(po == 0), stop=(po == C - 1))
            if do_stats:
                po = C - 1
                for th in range(NTH):
                    b4 = it(po, th) % 4
                    t.wait_ge(sm["cPL"].h, cPLb + it(po, th) + 1)
                    t.wait_ge(sm["cA"].h, cAb + CAK * it(po, th) + 2)
                    t.matmul(ps_s1[0:1, th * 512:(th + 1) * 512], onesb[:],
                             xbT2[:, po * TC + th * 512: po * TC + (th + 1) * 512],
                             start=(po == 0), stop=(po == C - 1))
                    ins = t.matmul(ps_s2[0:1, th * 512:(th + 1) * 512], onesb[:], xqs[b4][:],
                                   start=(po == 0), stop=(po == C - 1))
                ins.then_inc(sm["pPE"].h, 1)

        def f_scal(sc):
            for oc in range(C):
                for th in range(NTH):
                    k = it(oc, th)
                    b4 = k % 4
                    sc.wait_ge(sm["pPE"].h, pPEb + k + 1)
                    if k >= 4:
                        sc.wait_ge(sm["cV"].h, cVb + (k - 4) + 1)
                    sc.activation(t1b[b4][:], ps_o[b4][:], AF.Identity,
                                  bias=bias05[:, oc:oc + 1], scale=0.5).then_inc(sm["cA"].h, 1)
                    if do_stats:
                        sc.wait_ge(sm["cV"].h, cVb + k + 1)
                        sc.activation(xqs[b4][:], x1b[b4][:], AF.Square).then_inc(sm["cA"].h, 1)

        def f_vec(v):
            for oc in range(C):
                sl = oc % 2
                v.wait_ge(sm[f"dx{sl}"].h, dxb[sl] + (oc // 2 + 1) * 16)
                for th in range(NTH):
                    k = it(oc, th)
                    b4 = k % 4
                    v.wait_ge(sm["cA"].h, cAb + CAK * k + 1)
                    if k >= 4:
                        v.wait_ge(sm[f"dst{b4}"].h, dstb[b4] + ((k - 4) // 4 + 1) * 16)
                        if do_stats:
                            v.wait_ge(sm["cA"].h, cAb + CAK * (k - 4) + 2)
                            v.wait_ge(sm["cPL"].h, cPLb + (k - 4) + 1)
                    v.tensor_add(x1b[b4][:], t1b[b4][:],
                                 xcb[sl][:, th * 512:(th + 1) * 512]).then_inc(sm["cV"].h, 1)

        block.gpsimd(f_gps_w)
        block.sync(f_sync)
        block.tensor(f_pe)
        block.scalar(f_scal)
        block.vector(f_vec)

        sm["dw0"].v += 16 * ((C + 1) // 2)
        sm["dw1"].v += 16 * (C // 2)
        sm["dx0"].v += 16 * ((C + 1) // 2)
        sm["dx1"].v += 16 * (C // 2)
        for j in range(4):
            sm[f"dst{j}"].v += 16 * ((NIT - j + 3) // 4)
        sm["pPE"].v += NIT + (1 if do_stats else 0)
        sm["cA"].v += NIT * CAK
        sm["cV"].v += NIT
        if do_stats:
            sm["cPL"].v += NIT
        BAR()
        FR(*ps_o[::-1])

    ps_rb1 = AL(nc.psum_tensor("ps_rb1", [128, TC], F32))
    ps_mb1 = AL(nc.psum_tensor("ps_mb1", [128, TC], F32))
    ps_s1a = AL(nc.psum_tensor("ps_s1a", [1, TC], F32))
    ps_s2a = AL(nc.psum_tensor("ps_s2a", [1, TC], F32))
    xn2buf = AL(nc.sbuf_tensor("xn2buf", [128, C * TC], BF16))
    xn1T = AL(nc.sbuf_tensor("xn1T", [128, C * TC], BF16))
    ln_stream_stats(x_f, xn1T, ps_s1a, ps_s2a)
    ln_rowmath_bcast(ps_s1a, ps_s2a, ps_rb1, ps_mb1)
    BAR()
    FR(ps_s2a, ps_s1a)
    ln_apply(xn1T, ps_rb1, ps_mb1)
    BAR()

    FR(ps_mb1, ps_rb1)
    ps_s1b = AL(nc.psum_tensor("ps_s1b", [1, TC], F32))
    ps_s2b = AL(nc.psum_tensor("ps_s2b", [1, TC], F32))
    ffn(xn1T, x_f, xs1, ps_s1b, ps_s2b, xbT2=xn2buf)
    dst_after_ffn1 = {j: sm[f"dst{j}"].v for j in range(4)}
    ps_rb2 = AL(nc.psum_tensor("ps_rb2", [128, TC], F32))
    ps_mb2 = AL(nc.psum_tensor("ps_mb2", [128, TC], F32))
    ln_rowmath_bcast(ps_s1b, ps_s2b, ps_rb2, ps_mb2)
    BAR()

    FR(xn1T)
    xn2T = xn2buf
    wq_sb = AL(nc.sbuf_tensor("wq_sb", [128, C * 512], BF16))

    def ph4_wq(g):
        g.dma_start(out=wq_sb[:].rearrange("p (c e) -> p c e", c=C),
                    in_=wq_in.rearrange("(c p) e -> p c e", p=128)).then_inc(sm["dwq"].h, 16)
    block.gpsimd(ph4_wq)
    sm["dwq"].v += 16
    ln_apply(xn2T, ps_rb2, ps_mb2, also_ag=ag_x_in)
    dag_after_x = sm["dag"].v
    BAR()
    FR(ps_mb2, ps_rb2, ps_s2b, ps_s1b)

    ps_k = AL(nc.psum_tensor("ps_k", [128, 512], F32))
    ps_v = AL(nc.psum_tensor("ps_v", [128, 128], F32))
    pPEb, cVb = sm["pPE"].v, sm["cV"].v
    dmgb = sm["dmiscg"].v

    def ph4_gps(g):
        g.dma_start(out=bwb[0][:].rearrange("p (c e) -> p c e", c=C),
                    in_=wkv_in[:, 0:128].rearrange("(c p) e -> p c e", p=128)).then_inc(sm["dmiscg"].h, 16)
        g.dma_start(out=bwb[1][:].rearrange("p (c e) -> p c e", c=C),
                    in_=wkv_in[:, 128:256].rearrange("(c p) e -> p c e", p=128)).then_inc(sm["dmiscg"].h, 16)

    def ph4_pe(t):
        t.wait_ge(sm["dmiscg"].h, dmgb + 32)
        for th in range(NTH):
            if th > 0:
                t.wait_ge(sm["cV"].h, cVb + th)
            for c in range(C):
                ins = t.matmul(ps_k[:], bwb[0][:, c * 128:(c + 1) * 128],
                               xn2T[:, c * TC + th * 512: c * TC + th * 512 + 512],
                               start=(c == 0), stop=(c == C - 1))
            ins.then_inc(sm["pPE"].h, 1)
        for tt in range(NT128):
            if tt > 0:
                t.wait_ge(sm["cV"].h, cVb + NTH + tt)
            for c in range(C):
                ins = t.matmul(ps_v[:], xn2T[:, c * TC + tt * 128: c * TC + (tt + 1) * 128],
                               bwb[1][:, c * 128:(c + 1) * 128],
                               start=(c == 0), stop=(c == C - 1))
            ins.then_inc(sm["pPE"].h, 1)

    def ph4_vec(v):
        for th in range(NTH):
            v.wait_ge(sm["pPE"].h, pPEb + th + 1)
            v.tensor_copy(kc_sb[:, th * 512:(th + 1) * 512], ps_k[:]).then_inc(sm["cV"].h, 1)
        for tt in range(NT128):
            v.wait_ge(sm["pPE"].h, pPEb + NTH + tt + 1)
            v.tensor_copy(vc_sb[:, tt * 128:(tt + 1) * 128], ps_v[:]).then_inc(sm["cV"].h, 1)

    def ph4_gps2(g):
        g.wait_ge(sm["cV"].h, cVb + NTH)
        g.dma_start(out=ag_k_in, in_=kc_sb[:]).then_inc(sm["dag"].h, 16)
        g.wait_ge(sm["cV"].h, cVb + NTH + NT128)
        g.dma_start(out=agv_in.rearrange("(g p) d -> p g d", p=128),
                    in_=vc_sb[:].rearrange("p (g d) -> p g d", d=128)).then_inc(sm["dag"].h, 16)
        g.wait_ge(sm["dag"].h, dag_after_x + 32)
        g.collective_compute("AllGather", ALU.bypass, replica_groups=[list(range(NCORE))],
                             ins=[ag_k_in], outs=[ag_k_out]).then_inc(sm["cc"].h, 1)
        g.collective_compute("AllGather", ALU.bypass, replica_groups=[list(range(NCORE))],
                             ins=[agv_in], outs=[agv_out]).then_inc(sm["cc"].h, 1)

    block.gpsimd(ph4_gps)
    sm["dmiscg"].v += 32
    block.tensor(ph4_pe)
    block.vector(ph4_vec)
    block.gpsimd(ph4_gps2)
    sm["pPE"].v += NTH + NT128
    sm["cV"].v += NTH + NT128
    sm["dag"].v += 32
    sm["cc"].v += 2
    dag_kv_done = sm["dag"].v
    cc_x0, cc_k, cc_v = sm["cc"].v - 6, sm["cc"].v - 1, sm["cc"].v
    BAR()
    FR(ps_v, ps_k)

    xgc = [AL(nc.sbuf_tensor(f"xgc{i}", [128, TC], BF16)) for i in range(4)]
    qt_sb = [AL(nc.sbuf_tensor(f"qt{i}", [128, 512], BF16)) for i in range(2)]
    ps_q = [AL(nc.psum_tensor(f"ps_q{i}", [128, 512], F32)) for i in range(4 * NTH)]
    pPEb, cAb = sm["pPE"].v, sm["cA"].v
    dxb = [sm[f"dx{j}"].v for j in range(4)]
    dqb = [sm["dq0"].v, sm["dq1"].v]
    dmgb = sm["dmiscg"].v
    NPQ = 4 * NTH

    def ph5_gps(g):
        for r in range(NCORE):
            for k in range(NPQ):
                hh, th = k // NTH, k % NTH
                kk = r * NPQ + k
                g.wait_ge(sm["cA"].h, cAb + kk + 1)
                g.dma_start(out=qdram[hh * 128:(hh + 1) * 128,
                                      r * TC + th * 512: r * TC + th * 512 + 512],
                            in_=qt_sb[kk % 2][:]).then_inc(sm[f"dq{kk % 2}"].h, 16)

    def ph5_sync(s):
        for r in range(NCORE):
            for c in range(C):
                idx = r * C + c
                sl = idx % 4
                if r == 0:
                    s.wait_ge(sm["cc"].h, cc_x0 + c // 8 + 1)
                if idx >= 4:
                    s.wait_ge(sm["pPE"].h, pPEb + (idx - 4) + 1)
                q, cq = c // 8, c % 8
                s.dma_start(out=xgc[sl][:],
                            in_=ag_xo[q][r * 1024 + cq * 128: r * 1024 + (cq + 1) * 128, :]).then_inc(sm[f"dx{sl}"].h, 16)

    def ph5_pe(t):
        t.wait_ge(sm["dwq"].h, 16)
        for r in range(NCORE):
            for c in range(C):
                idx = r * C + c
                sl = idx % 2
                sl4 = idx % 4
                t.wait_ge(sm[f"dx{sl4}"].h, dxb[sl4] + (idx // 4 + 1) * 16)
                for k in range(NPQ):
                    if c == 0 and r > 0:
                        t.wait_ge(sm["cA"].h, cAb + (r - 1) * NPQ + k + 1)
                    hh, th = k // NTH, k % NTH
                    ins = t.matmul(ps_q[k][:], wq_sb[:, c * 512 + hh * 128: c * 512 + (hh + 1) * 128],
                                   xgc[sl4][:, th * 512:(th + 1) * 512],
                                   start=(c == 0), stop=(c == C - 1))
                ins.then_inc(sm["pPE"].h, 1)

    def ph5_scal(sc):
        for r in range(NCORE):
            sc.wait_ge(sm["pPE"].h, pPEb + r * C + C)
            for k in range(NPQ):
                kk = r * NPQ + k
                if kk >= 2:
                    sc.wait_ge(sm[f"dq{kk % 2}"].h, dqb[kk % 2] + ((kk - 2) // 2 + 1) * 16)
                sc.activation(qt_sb[kk % 2][:], ps_q[k][:], AF.Copy,
                              scale=QSCALE).then_inc(sm["cA"].h, 1)

    block.gpsimd(ph5_gps)
    block.sync(ph5_sync)
    block.tensor(ph5_pe)
    block.scalar(ph5_scal)
    sm["dq0"].v += 16 * ((NCORE * NPQ + 1) // 2)
    sm["dq1"].v += 16 * ((NCORE * NPQ) // 2)
    for j in range(4):
        sm[f"dx{j}"].v += 16 * ((NCORE * C - j + 3) // 4)
    sm["pPE"].v += NCORE * C
    sm["cA"].v += NCORE * NPQ
    dq_total = (sm["dq0"].v, sm["dq1"].v)
    BAR()
    FR(*ps_q[::-1])
    FR(qt_sb[1], qt_sb[0], xgc[3], xgc[2], xgc[1], xgc[0], wq_sb, xn2buf)

    kf_sb = AL(nc.sbuf_tensor("kf_sb", [128, S], BF16))
    vt_sb = AL(nc.sbuf_tensor("vt_sb", [128, S], BF16))
    dm_sb = AL(nc.sbuf_tensor("dm_sb", [128, DMLEN], F32))
    dm2_sb = AL(nc.sbuf_tensor("dm2_sb", [128, 2048], F32))
    sD0 = AL(nc.sbuf_tensor("sD0", [128, 512], F32))
    sDm = AL(nc.sbuf_tensor("sDm", [128, 2048], F32))
    sDm2 = AL(nc.sbuf_tensor("sDm2", [128, 2048], F32))
    dT_sb = AL(nc.sbuf_tensor("dT_sb", [128, 72], F32))
    bvt_sb = AL(nc.sbuf_tensor("bvt_sb", [128, 72], F32))
    prob_sb = [AL(nc.sbuf_tensor(f"prob{i}", [128, 512], BF16)) for i in range(4)]
    qt6 = [AL(nc.sbuf_tensor(f"qt6_{i}", [128, 512], BF16)) for i in range(2)]
    zb_sb = [AL(nc.sbuf_tensor(f"zb{i}", [128, 512], F32)) for i in range(2)]
    lnz_sb = [AL(nc.sbuf_tensor(f"lnz{i}", [128, 512], F32)) for i in range(2)]
    ao_sb = [AL(nc.sbuf_tensor(f"ao{i}", [128, 512], BF16)) for i in range(2)]
    ps_sc = [AL(nc.psum_tensor(f"ps_sc{i}", [128, 512], F32)) for i in range(4)]
    ps_ao = [AL(nc.psum_tensor(f"ps_ao{i}", [128, 512], F32)) for i in range(2)]
    ps_z = [AL(nc.psum_tensor(f"ps_z{i}", [128, 512], F32)) for i in range(2)]

    sched = []
    for hh in (3, 2, 1, 0):
        for qt in range(NQT):
            sched.append((hh, qt, list(range(kt_lo(hh, qt), 4 * qt + 4))))
    NG = len(sched)
    NSC = [len(k) for (_, _, k) in sched]
    CSC = [0]
    for nsc in NSC:
        CSC.append(CSC[-1] + nsc)

    pPEb, pPEvb = sm["pPE"].v, sm["pPEv"].v
    cVb, cAb = sm["cV"].v, sm["cA"].v
    daob = [sm["dao0"].v, sm["dao1"].v]
    dkob = sm["dko"].v
    dqtb = [sm["dqt0"].v, sm["dqt1"].v]

    def is_phi(hh, m):
        return hh >= 2 and m >= 1

    cv_at, ca_at, pev_at, cv_after_group, hh_prep_at, ca_z = {}, {}, {}, {}, {}, {}
    cv_run, ca_run, pev_run = cVb, cAb, pPEvb
    for g, (hh, qt, kts) in enumerate(sched):
        if qt == 0:
            cv_run += 5
            hh_prep_at[hh] = cv_run
        for j, kt in enumerate(kts):
            m = 4 * qt - kt
            if not is_phi(hh, m):
                cv_run += 1
                cv_at[(g, j)] = cv_run
            ca_run += 1
            ca_at[(g, j)] = ca_run
        pev_run += len(kts)
        pev_at[g] = pev_run
        ca_run += 2
        ca_z[g] = ca_run
        cv_run += 1
        cv_after_group[g] = cv_run

    def ph6_sync(s):
        s.wait_ge(sm["dq0"].h, dq_total[0])
        s.wait_ge(sm["dq1"].h, dq_total[1])
        s.dma_start(out=dm_sb[:], in_=dm_in).then_inc(sm["dmisc"].h, 16)
        s.dma_start(out=dm2_sb[:], in_=dm2_in).then_inc(sm["dmisc"].h, 16)
        s.wait_ge(sm["cc"].h, cc_k)
        for r in range(NCORE):
            s.dma_start(out=kf_sb[:, r * TC:(r + 1) * TC],
                        in_=ag_k_out[r * 128:(r + 1) * 128, :]).then_inc(sm["dko"].h, 16)
        s.wait_ge(sm["cc"].h, cc_v)
        s.dma_start(out=vt_sb[:].rearrange("p (g d) -> p g d", d=128),
                    in_=agv_out.rearrange("(g p) d -> p g d", p=128)).then_inc(sm["dko"].h, 16)
        for g, (hh, qt, kts) in enumerate(sched):
            if g >= 2:
                s.wait_ge(sm["pPE"].h, pPEb + CSC[g - 1])
            s.dma_start(out=qt6[g % 2][:],
                        in_=qdram[hh * 128:(hh + 1) * 128, qt * 512:(qt + 1) * 512]).then_inc(sm[f"dqt{g % 2}"].h, 16)

    def ph6_vec(v):
        v.wait_ge(sm["dmisc"].h, 80)
        v.wait_ge(sm["dq0"].h, dq_total[0])
        v.wait_ge(sm["dq1"].h, dq_total[1])
        for g, (hh, qt, kts) in enumerate(sched):
            if qt == 0:
                if g > 0:
                    pg = g - 1
                    v.wait_ge(sm["cA"].h, ca_at[(pg, len(sched[pg][2]) - 1)])
                v.tensor_scalar(sD0[:], dm_sb[:, M_D0:M_D0 + 512], slope4[:, hh:hh + 1], None, ALU.mult).then_inc(sm["cV"].h, 1)
                v.tensor_scalar(sDm[:], dm_sb[:, M_DM:M_DM + 2048], slope4[:, hh:hh + 1], None, ALU.mult).then_inc(sm["cV"].h, 1)
                v.tensor_scalar(sDm2[:], dm2_sb[:], slope4[:, hh:hh + 1], None, ALU.mult).then_inc(sm["cV"].h, 1)
                v.tensor_scalar(dT_sb[:], dm_sb[:, M_DB:M_DB + 72], slope4[:, hh:hh + 1], None, ALU.mult).then_inc(sm["cV"].h, 1)
                v.tensor_scalar(bvt_sb[:], dm_sb[:, M_CB:M_CB + 72], slope4[:, hh:hh + 1], -5.0, ALU.mult, ALU.add).then_inc(sm["cV"].h, 1)
                v.wait_ge(sm["cV"].h, hh_prep_at[hh])
            for j, kt in enumerate(kts):
                sl = j % 4
                m = 4 * qt - kt
                if is_phi(hh, m):
                    continue
                v.wait_ge(sm["pPE"].h, pPEb + CSC[g] + j + 1)
                if m <= 0:
                    dmt = sDm2 if hh >= 2 else sDm
                    ins = v.tensor_add(ps_sc[sl][:], ps_sc[sl][:], dmt[:, (-m) * 512:(-m + 1) * 512])
                else:
                    ins = v.scalar_tensor_tensor(ps_sc[sl][:], ps_sc[sl][:], dT_sb[:, m:m + 1],
                                                 sD0[:], ALU.add, ALU.add)
                ins.then_inc(sm["cV"].h, 1)
            sl2 = g % 2
            v.wait_ge(sm["pPEv"].h, pev_at[g])
            v.wait_ge(sm["cA"].h, ca_z[g])
            if g >= 2:
                v.wait_ge(sm[f"dao{sl2}"].h, daob[sl2] + ((g - 2) // 2 + 1) * 16)
            v.tensor_mul(ao_sb[sl2][:], ps_ao[sl2][:], zb_sb[sl2][:]).then_inc(sm["cV"].h, 1)

    def ph6_scal(sc):
        for g, (hh, qt, kts) in enumerate(sched):
            for j, kt in enumerate(kts):
                sl = j % 4
                m = 4 * qt - kt
                if is_phi(hh, m):
                    sc.wait_ge(sm["pPE"].h, pPEb + CSC[g] + j + 1)
                    sc.wait_ge(sm["cV"].h, hh_prep_at[hh])
                    bias_ap = bvt_sb[:, m:m + 1]
                else:
                    sc.wait_ge(sm["cV"].h, cv_at[(g, j)])
                    bias_ap = c5_sb[:, 0:1]
                if j >= 4:
                    sc.wait_ge(sm["pPEv"].h, pPEvb + CSC[g] + (j - 4) + 1)
                elif g >= 1:
                    sc.wait_ge(sm["pPEv"].h, pev_at[g - 1])
                sc.activation(prob_sb[sl][:], ps_sc[sl][:], AF.Exp,
                              bias=bias_ap).then_inc(sm["cA"].h, 1)
            sl2 = g % 2
            sc.wait_ge(sm["pPEv"].h, pev_at[g])
            if g >= 2:
                sc.wait_ge(sm["cV"].h, cv_after_group[g - 2])
            sc.activation(lnz_sb[sl2][:], ps_z[sl2][:], AF.Ln).then_inc(sm["cA"].h, 1)
            sc.wait_ge(sm["cA"].h, ca_z[g] - 1)
            sc.activation(zb_sb[sl2][:], lnz_sb[sl2][:], AF.Exp, scale=-1.0).then_inc(sm["cA"].h, 1)

    def ph6_pe(t):
        t.wait_ge(sm["dko"].h, dkob + 16 * (NCORE + 1))
        for g, (hh, qt, kts) in enumerate(sched):
            sl2 = g % 2
            nk = len(kts)
            t.wait_ge(sm[f"dqt{g % 2}"].h, dqtb[g % 2] + (g // 2 + 1) * 16)
            LV = 2
            for j in range(nk + LV):
                if j < nk:
                    kt = kts[j]
                    sl = j % 4
                    if j >= 4:
                        t.wait_ge(sm["cA"].h, ca_at[(g, j - 4)])
                    elif g >= 1:
                        pg = g - 1
                        t.wait_ge(sm["cA"].h, ca_at[(pg, len(sched[pg][2]) - 1)])
                    t.matmul(ps_sc[sl][:], kf_sb[:, kt * 128:(kt + 1) * 128],
                             qt6[g % 2][:], start=True, stop=True).then_inc(sm["pPE"].h, 1)
                jj = j - LV
                if jj >= 0:
                    kt = kts[jj]
                    sl = jj % 4
                    t.wait_ge(sm["cA"].h, ca_at[(g, jj)])
                    if jj == 0 and g >= 2:
                        t.wait_ge(sm["cV"].h, cv_after_group[g - 2])
                        t.wait_ge(sm["cA"].h, ca_z[g - 2])
                    t.matmul(ps_ao[sl2][:], vt_sb[:, kt * 128:(kt + 1) * 128], prob_sb[sl][:],
                             start=(jj == 0), stop=(jj == nk - 1))
                    t.matmul(ps_z[sl2][:], ones128b[:], prob_sb[sl][:],
                             start=(jj == 0), stop=(jj == nk - 1)).then_inc(sm["pPEv"].h, 1)

    def ph6_gps(g_):
        for g, (hh, qt, kts) in enumerate(sched):
            sl2 = g % 2
            g_.wait_ge(sm["cV"].h, cv_after_group[g])
            g_.dma_start(out=ag_o_in[hh * 128:(hh + 1) * 128, qt * 512:(qt + 1) * 512],
                         in_=ao_sb[sl2][:]).then_inc(sm[f"dao{sl2}"].h, 16)
            if qt == NQT - 1:
                ng = g + 1
                g_.wait_ge(sm["dao0"].h, daob[0] + 16 * ((ng + 1) // 2))
                g_.wait_ge(sm["dao1"].h, daob[1] + 16 * (ng // 2))
                g_.collective_compute(
                    "AllGather", ALU.bypass, replica_groups=[list(range(NCORE))],
                    ins=[ag_o_in[hh * 128:(hh + 1) * 128, :]],
                    outs=[ag_oo[hh]]).then_inc(sm["cc"].h, 1)

    block.sync(ph6_sync)
    block.tensor(ph6_pe)
    block.vector(ph6_vec)
    block.scalar(ph6_scal)
    block.gpsimd(ph6_gps)
    sm["dko"].v += 16 * (NCORE + 1)
    sm["dmisc"].v += 32
    sm["dqt0"].v += 16 * ((NG + 1) // 2)
    sm["dqt1"].v += 16 * (NG // 2)
    sm["pPE"].v += CSC[-1]
    sm["pPEv"].v = pev_run
    sm["cV"].v = cv_run
    sm["cA"].v = ca_run
    sm["dao0"].v += 16 * ((NG + 1) // 2)
    sm["dao1"].v += 16 * (NG // 2)
    sm["cc"].v += 4
    cc_o = sm["cc"].v
    BAR()
    FR(*ps_z[::-1]); FR(*ps_ao[::-1]); FR(*ps_sc[::-1])
    FR(ao_sb[1], ao_sb[0], lnz_sb[1], lnz_sb[0], zb_sb[1], zb_sb[0],
       qt6[1], qt6[0], prob_sb[3], prob_sb[2], prob_sb[1], prob_sb[0],
       bvt_sb, dT_sb, sDm2, sDm, sD0, dm2_sb, dm_sb, vt_sb, kf_sb)

    attn_sb = AL(nc.sbuf_tensor("attn_sb", [128, C * TC], BF16))
    xb3 = AL(nc.sbuf_tensor("xb3", [128, C * TC], BF16))
    dxgb7 = sm["dxg"].v

    at_sems = {3: "dxg", 2: "dmisc", 1: "dqt0", 0: "dqt1"}
    at_base = {hh: sm[at_sems[hh]].v for hh in range(4)}

    def ph7_sync_pre(s):
        pid = s.partition_id()
        for i, hh in enumerate((3, 2, 1, 0)):
            s.wait_ge(sm["cc"].h, cc_o - 3 + i)
            out_v = attn_sb[:].rearrange("p (r h t) -> p r h t", h=4, t=TC)[:, :, hh, :]
            in_v = ag_oo[hh][:, bass.ts(pid, TC)].rearrange("(r p) t -> p r t", p=128)
            s.dma_start(out=out_v, in_=in_v).then_inc(sm[at_sems[hh]].h, 16)
    block.sync(ph7_sync_pre)
    for hh in range(4):
        sm[at_sems[hh]].v += 16

    ps_s1c = AL(nc.psum_tensor("ps_s1c", [1, TC], F32))
    ps_s2c = AL(nc.psum_tensor("ps_s2c", [1, TC], F32))
    ps_o7 = [AL(nc.psum_tensor(f"ps_wo{i}", [128, 512], F32)) for i in range(4)]
    pPEb, cVb, cAb, cPLb = sm["pPE"].v, sm["cV"].v, sm["cA"].v, sm["cPL"].v
    dwb = [sm["dw0"].v, sm["dw1"].v]
    dxb = [sm["dx0"].v, sm["dx1"].v]
    dstb = [sm[f"dst{j}"].v for j in range(4)]
    NIT = C * NTH

    def it7(oc, th):
        return oc * NTH + th

    def ph7_pool_copies(g, po):
        for th in range(NTH):
            k = it7(po, th)
            b4 = k % 4
            g.wait_ge(sm["cV"].h, cVb + k + 1)
            g.tensor_copy(xb3[:, po * TC + th * 512: po * TC + (th + 1) * 512],
                          x1b[b4][:]).then_inc(sm["cPL"].h, 1)

    def ph7_gps_w(g):
        for oc in range(C):
            sl = oc % 2
            if oc >= 2:
                g.wait_ge(sm["pPE"].h, pPEb + it7(oc - 2, NTH - 1) + 1)
            g.dma_start(
                out=bwb[sl][:].rearrange("p (c o) -> p c o", c=C),
                in_=wc_p[:, oc * 128:(oc + 1) * 128].rearrange("(c p) o -> p c o", p=128),
            ).then_inc(sm[f"dw{sl}"].h, 16)
            if oc >= 1:
                ph7_pool_copies(g, oc - 1)
        ph7_pool_copies(g, C - 1)

    def ph7_sync(s):
        for j in range(4):
            s.wait_ge(sm[f"dst{j}"].h, dst_after_ffn1[j])
        for oc in range(C):
            sl = oc % 2
            if oc >= 2:
                s.wait_ge(sm["cV"].h, cVb + it7(oc - 2, NTH - 1) + 1)
            s.dma_start(out=xcb[sl][:], in_=xs1[oc * 128:(oc + 1) * 128, :]).then_inc(sm[f"dx{sl}"].h, 16)
            for th in range(NTH):
                b4 = it7(oc, th) % 4
                s.wait_ge(sm["cV"].h, cVb + it7(oc, th) + 1)
                s.dma_start(out=xs2[oc * 128:(oc + 1) * 128, th * 512:(th + 1) * 512],
                            in_=x1b[b4][:]).then_inc(sm[f"dst{b4}"].h, 16)

    cs_order = sorted(range(C), key=lambda c: (-(c % 4), c))

    def ph7_pe(t):
        for oc in range(C):
            sl = oc % 2
            t.wait_ge(sm[f"dw{sl}"].h, dwb[sl] + (oc // 2 + 1) * 16)
            for th in range(NTH):
                k = it7(oc, th)
                if k >= 4:
                    t.wait_ge(sm["cV"].h, cVb + (k - 4) + 1)
            for ci, c in enumerate(cs_order):
                if oc == 0 and (ci == 0 or c % 4 != cs_order[ci - 1] % 4):
                    t.wait_ge(sm[at_sems[c % 4]].h, at_base[c % 4] + 16)
                for th in range(NTH):
                    b4 = it7(oc, th) % 4
                    ins = t.matmul(ps_o7[b4][:], bwb[sl][:, c * 128:(c + 1) * 128],
                                   attn_sb[:, c * TC + th * 512: c * TC + th * 512 + 512],
                                   start=(ci == 0), stop=(ci == C - 1))
                    if ci == C - 1:
                        ins.then_inc(sm["pPE"].h, 1)
            if oc >= 1:
                po = oc - 1
                for th in range(NTH):
                    b4 = it7(po, th) % 4
                    t.wait_ge(sm["cPL"].h, cPLb + it7(po, th) + 1)
                    t.wait_ge(sm["cA"].h, cAb + it7(po, th) + 1)
                    t.matmul(ps_s1c[0:1, th * 512:(th + 1) * 512], onesb[:],
                             xb3[:, po * TC + th * 512: po * TC + (th + 1) * 512],
                             start=(po == 0), stop=(po == C - 1))
                    t.matmul(ps_s2c[0:1, th * 512:(th + 1) * 512], onesb[:], xqs[b4][:],
                             start=(po == 0), stop=(po == C - 1))
        po = C - 1
        for th in range(NTH):
            b4 = it7(po, th) % 4
            t.wait_ge(sm["cPL"].h, cPLb + it7(po, th) + 1)
            t.wait_ge(sm["cA"].h, cAb + it7(po, th) + 1)
            t.matmul(ps_s1c[0:1, th * 512:(th + 1) * 512], onesb[:],
                     xb3[:, po * TC + th * 512: po * TC + (th + 1) * 512],
                     start=(po == 0), stop=(po == C - 1))
            ins = t.matmul(ps_s2c[0:1, th * 512:(th + 1) * 512], onesb[:], xqs[b4][:],
                           start=(po == 0), stop=(po == C - 1))
        ins.then_inc(sm["pPE"].h, 1)

    def ph7_vec(v):
        for oc in range(C):
            sl = oc % 2
            v.wait_ge(sm[f"dx{sl}"].h, dxb[sl] + (oc // 2 + 1) * 16)
            for th in range(NTH):
                k = it7(oc, th)
                b4 = k % 4
                v.wait_ge(sm["pPE"].h, pPEb + k + 1)
                if k >= 4:
                    v.wait_ge(sm[f"dst{b4}"].h, dstb[b4] + ((k - 4) // 4 + 1) * 16)
                    v.wait_ge(sm["cPL"].h, cPLb + (k - 4) + 1)
                    v.wait_ge(sm["cA"].h, cAb + (k - 4) + 1)
                v.scalar_tensor_tensor(x1b[b4][:], ps_o7[b4][:], 1.0,
                                       xcb[sl][:, th * 512:(th + 1) * 512],
                                       ALU.mult, ALU.add).then_inc(sm["cV"].h, 1)

    def ph7_scal(sc):
        for oc in range(C):
            for th in range(NTH):
                k = it7(oc, th)
                b4 = k % 4
                sc.wait_ge(sm["cV"].h, cVb + k + 1)
                sc.activation(xqs[b4][:], x1b[b4][:], AF.Square).then_inc(sm["cA"].h, 1)

    block.gpsimd(ph7_gps_w)
    block.sync(ph7_sync)
    block.tensor(ph7_pe)
    block.vector(ph7_vec)
    block.scalar(ph7_scal)
    sm["dw0"].v += 16 * ((C + 1) // 2)
    sm["dw1"].v += 16 * (C // 2)
    sm["dx0"].v += 16 * ((C + 1) // 2)
    sm["dx1"].v += 16 * (C // 2)
    for j in range(4):
        sm[f"dst{j}"].v += 16 * ((NIT - j + 3) // 4)
    sm["pPE"].v += NIT + 1
    sm["cV"].v += NIT
    sm["cA"].v += NIT
    sm["cPL"].v += NIT
    dst_after_wc = {j: sm[f"dst{j}"].v for j in range(4)}
    BAR()
    FR(*ps_o7[::-1])
    ps_rb3 = AL(nc.psum_tensor("ps_rb3", [128, TC], F32))
    ps_mb3 = AL(nc.psum_tensor("ps_mb3", [128, TC], F32))
    ln_rowmath_bcast(ps_s1c, ps_s2c, ps_rb3, ps_mb3)
    BAR()
    ln_apply(xb3, ps_rb3, ps_mb3)
    BAR()
    FR(ps_mb3, ps_rb3, ps_s2c, ps_s1c)

    ffn(xb3, xs2, out_f, None, None, dst_wait=dst_after_wc)

    dst_total = {j: sm[f"dst{j}"].v for j in range(4)}

    def fin_sync(s):
        for j in range(4):
            s.wait_ge(sm[f"dst{j}"].h, dst_total[j])
    block.sync(fin_sync)

    blk_cm.__exit__(None, None, None)
    for space in ("psum", "sbuf"):
        while _stacks[space]:
            h = _stacks[space].pop()
            _cm_of.pop(id(h)).__exit__(None, None, None)
    return nc


def _consts_np():
    ident = np.eye(128, dtype=np.float32)
    allones = np.ones((128, 128), dtype=np.float32)
    return np.ascontiguousarray(np.concatenate([ident, allones], axis=1))


def _dm_np():
    kk = np.arange(128, dtype=np.float32)[:, None]
    qq = np.arange(512, dtype=np.float32)[None, :]
    Dms = []
    for jm in range(4):
        d = kk - qq + 128.0 * jm
        Dms.append(np.where(d <= 0.0, d, -1e13).astype(np.float32))
    D0 = (kk - qq).astype(np.float32)
    deltaB = np.zeros((128, 72), dtype=np.float32)
    for mm in range(72):
        deltaB[:, mm] = -128.0 * mm
    CB = np.zeros((128, 72), dtype=np.float32)
    for mm in range(72):
        CB[:, mm] = np.arange(128) - 128.0 * mm - 511.0
    return np.ascontiguousarray(np.concatenate(Dms + [D0, deltaB, CB], axis=1))


def _dm2_np():
    kk = np.arange(128, dtype=np.float32)[:, None]
    qq = np.arange(512, dtype=np.float32)[None, :]
    Dms = []
    for jm in range(4):
        valid = (kk - qq + 128.0 * jm) <= 0.0
        val = np.broadcast_to(kk + 128.0 * jm - 511.0, (128, 512))
        Dms.append(np.where(valid, val, -1e13).astype(np.float32))
    return np.ascontiguousarray(np.concatenate(Dms, axis=1))


def _e2_np():
    e2 = np.zeros((128, 128), dtype=np.float32)
    for e in range(NEXP):
        e2[e, e * R:(e + 1) * R] = 2.0
    return e2


def shard_inputs(S, L0_attn_hidden_states, L0_attn_w_qkv, L0_attn_w_c,
                 L0_ffn_gate_w, L0_ffn_base_w, L0_ffn_base_b, L0_ffn_lora_A, L0_ffn_lora_B):
    TC = S // NCORE
    x = np.ascontiguousarray(np.asarray(L0_attn_hidden_states, dtype=np.float32).reshape(S, D))
    wqkv = np.asarray(L0_attn_w_qkv, dtype=np.float32)
    wc = np.asarray(L0_attn_w_c, dtype=np.float32)
    gate_w = np.asarray(L0_ffn_gate_w, dtype=np.float32)
    base_w = np.asarray(L0_ffn_base_w, dtype=np.float32)
    base_b = np.asarray(L0_ffn_base_b, dtype=np.float32)
    lora_A = np.asarray(L0_ffn_lora_A, dtype=np.float32)
    lora_B = np.asarray(L0_ffn_lora_B, dtype=np.float32)

    basewT_np = np.ascontiguousarray(base_w.T)
    wkv_np = np.ascontiguousarray(wqkv[:, D:D + 256])
    gate_np = np.ascontiguousarray(gate_w.T)
    lat_np = np.ascontiguousarray(lora_A.reshape(NEXP * R, D).T)
    lbt_np = np.ascontiguousarray(lora_B.transpose(0, 2, 1).reshape(NEXP * R, D))
    bias_np = np.ascontiguousarray(base_b.reshape(C, 128).T)
    consts_np = _consts_np()
    e2_np = _e2_np()
    dm_np = _dm_np()
    dm2_np = _dm2_np()

    perm = np.zeros(D, dtype=np.int64)
    for r in range(NCORE):
        for hh in range(4):
            h = 8 * hh + r
            perm[r * 512 + hh * 128: r * 512 + (hh + 1) * 128] = np.arange(h * DH, (h + 1) * DH)
    wc_p_np = np.ascontiguousarray(wc[perm, :])

    in_maps = []
    for i in range(NCORE):
        heads = [8 * j + i for j in range(4)]
        wq_i = np.ascontiguousarray(
            np.concatenate([wqkv[:, h * DH:(h + 1) * DH] for h in heads], axis=1))
        slope4_np = np.zeros((128, 4), dtype=np.float32)
        for j, h in enumerate(heads):
            slope4_np[:, j] = SLOPES[h]
        x_i = np.ascontiguousarray(x[i * TC:(i + 1) * TC, :].T)
        in_maps.append(dict(
            x_f=x_i, basewT=basewT_np, wkv=wkv_np, wq=wq_i, wc_p=wc_p_np,
            gate_wT=gate_np, lora_AT=lat_np, lora_BT=lbt_np,
            bias_fm=bias_np, slope4=slope4_np, consts=consts_np, e2_in=e2_np, dm_in=dm_np, dm2_in=dm2_np,
        ))
    return in_maps


def assemble_output(S, results):
    TC = S // NCORE
    out = np.empty((S, D), dtype=np.float32)
    for i in range(NCORE):
        out[i * TC:(i + 1) * TC, :] = np.asarray(results[i]["out_f"]).T
    return out.reshape(1, S, D)


def kernel(**inputs):
    S = int(np.asarray(inputs["L0_attn_hidden_states"]).shape[1])
    in_maps = shard_inputs(S, **inputs)
    nc = build_program(S=S)
    res = run_bass_kernel_spmd(nc, in_maps, core_ids=list(range(NCORE)))
    return assemble_output(S, res.results)

